# version 05
# baseline (speedup 1.0000x reference)
import contextlib
import numpy as np
import ml_dtypes

import concourse.bass as bass
import concourse.mybir as mybir
from concourse.ap import AP
from concourse.bass_utils import run_bass_kernel_spmd

F32 = mybir.dt.float32
BF16 = mybir.dt.bfloat16
I32 = mybir.dt.int32
AF = mybir.ActivationFunctionType
OP = mybir.AluOpType
AX = mybir.AxisListType
NBF = ml_dtypes.bfloat16

C = 1024
H = 16
D = 64
HID = 3072
CS = C // 128
HS = HID // 128
HSH = HS // 2
HIDH = HID // 2
L = 128
S127 = float(np.float32(1.0) / np.float32(127.0))


class Sem:
    def __init__(self, es, nc, name):
        self.h = es.enter_context(nc.semaphore(name))
        self.n = 0


class Prog:
    def __init__(self):
        self.recs = []
        self.marks = {}
        self.sems = []
        self.snaps = []

    def _snap(self):
        self.snaps.append(tuple(s.n for s in self.sems))

    def mark(self, name, sems=None):
        self.marks[name] = (len(self.recs), {s: s.n for s in (sems or [])})

    def do(self, eng, fn, inc=None, k=1):
        def run(e, fn=fn, inc=inc, k=k):
            ins = fn(e)
            if inc is not None:
                ins.then_inc(inc.h, k)
        if inc is not None and k == 1 and inc.n > 0:
            self.recs.append((eng, lambda e, s=inc, v=inc.n: e.wait_ge(s.h, v)))
            self._snap()
        self.recs.append((eng, run))
        if inc is not None:
            inc.n += k
        self._snap()

    def wait(self, eng, sem, val=None):
        v = sem.n if val is None else val
        if v > 0:
            self.recs.append((eng, lambda e, s=sem, v=v: e.wait_ge(s.h, v)))
            self._snap()

    def drain(self, eng):
        self.recs.append((eng, lambda e: e.drain()))
        self._snap()

    def replay(self, eng_obj, name):
        for eng, run in self.recs:
            if eng == name:
                run(eng_obj)


def build_nc(T, debug=False, upto=None, solvemode="", phb=""):
    nch = T // L
    assert T % L == 0 and nch % 2 == 0
    nc = bass.Bass("TRN2", target_bir_lowering=False)

    dr = {}
    def din(name, shape, dt=F32):
        dr[name] = nc.dram_tensor(name, shape, dt, kind="ExternalInput").ap(); return dr[name]
    def dout(name, shape, dt=F32):
        dr[name] = nc.dram_tensor(name, shape, dt, kind="ExternalOutput").ap(); return dr[name]
    def dint(name, shape, dt=F32):
        dr[name] = nc.dram_tensor(name, shape, dt, kind="Internal").ap(); return dr[name]

    din("x", [T, C])
    for w in ("WkT", "WvT", "WrT", "WgT", "WoT"):
        din(w, [C, C])
    din("wgT", [C, HID], I32); din("wuT", [C, HID], I32); din("wdT", [HID, C], I32)
    din("biases", [1, 5 * C])
    din("tmcols", [128, 4 * CS])
    din("gncols", [128, 2 * CS])
    din("decay_eta", [1, 32])
    din("identbf", [128, 128], BF16)
    din("identf", [128, 128], F32)
    din("negibf", [64, 64], BF16)
    din("masksneg", [128, 128])
    din("maskincl", [128, 128])
    din("mask32r4", [32, 128])
    din("iota_r", [1, 128])
    din("iota1_r", [1, 128])
    din("onesb_r", [1, 128], BF16)
    din("onescol_f", [128, 1])
    dout("out", [T, C])
    if debug:
        dout("dbg_xn", [T, C], BF16); dout("dbg_k", [T, C], BF16)
        dout("dbg_v", [T, C], BF16); dout("dbg_r", [T, C], BF16)
        dout("dbg_wkv", [T, C], BF16); dout("dbg_beta", [T, H])
        dout("dbg_g", [T, C], BF16); dout("dbg_sg", [T, C], BF16)
        dout("dbg_gsig", [T, C], BF16)
        dout("dbg_stats", [1, 64])
    dint("xnfm_sc", [128, CS, T], BF16)
    dint("gsig_sc", [T, C], BF16)
    dint("wkvfm_sc", [128, CS, T], BF16)
    dint("dpart_sc", [T, C])
    dint("bn1", [2, 64 * 1024], BF16)
    dint("bn2", [2, 64 * 1024], BF16)
    dint("rowb", [2, 16])

    es = contextlib.ExitStack()
    def sb(name, shape, dt=F32):
        return es.enter_context(nc.sbuf_tensor(name, shape, dt))
    def psum(name, shape, dt=F32):
        return es.enter_context(nc.psum_tensor(name, shape, dt))

    WAR = sb("WAR", [128, 45056], BF16)
    def warv(i):
        return WAR[:, i * CS * C:(i + 1) * CS * C].rearrange("p (k c) -> p k c", c=C)
    wkT, wvT, wrT, wgT = (warv(i) for i in range(4))
    woT = WAR[:, 36864:45056].rearrange("p (k c) -> p k c", c=C)
    fgh = WAR[:, 0:12288].rearrange("p (k c) -> p k c", c=HIDH)
    fuh = WAR[:, 12288:24576].rearrange("p (k c) -> p k c", c=HIDH)
    fdh = WAR[:, 24576:24576 + HSH * C].rearrange("p (k c) -> p k c", c=C)

    wstage = sb("wstage", [128, 1536], F32)
    brow4 = WAR[0:1, 32768:36864]
    brow_o = WAR[0:1, 0:C]
    borow = sb("borow", [1, C], BF16)
    tmc = sb("tmc", [128, 4, CS], F32)
    gnc = sb("gnc", [128, 2, CS], F32)
    identbf = sb("sb_identbf", [128, 128], BF16)
    identf = sb("sb_identf", [128, 128], F32)
    negibf = sb("sb_negibf", [64, 64], BF16)
    masksneg = sb("sb_masksneg", [128, 128], F32)
    maskincl = sb("sb_maskincl", [128, 128], F32)
    mask32r4 = sb("sb_mask32r4", [32, 128], F32)
    ones_r = maskincl[0:1, 0:128]
    negones_r = masksneg[0:1, 1:65]
    iota_r = sb("sb_iota_r", [1, 128], F32)
    iota1_r = sb("sb_iota1_r", [1, 128], F32)
    onesb_r = sb("sb_onesb_r", [1, 128], BF16)
    onescol_f = sb("sb_onescol_f", [128, 1], F32)
    de_row = sb("de_row", [1, 32], F32)
    hrow = sb("hrow", [1, 128], F32)
    t_negdelta = sb("t_negdelta", [128, H], F32)
    t_etadelta = sb("t_etadelta", [128, H], F32)
    t_eta_b = sb("t_eta_b", [128, H], F32)
    t_q = sb("t_q", [128, H], F32)
    t_a = sb("t_a", [128, H], F32)
    t_et2 = sb("t_et2", [128, H], F32)
    t_dec2t = sb("t_dec2t", [128, H], F32)
    t_dect1 = sb("t_dect1", [16, 128], F32)
    t_dec2c = sb("t_dec2c", [16, 1], F32)
    etamask = sb("etamask", [32, 2048], BF16)
    etab32 = sb("etab32", [32, 16], F32)

    xbuf = sb("xbuf", [128, 2, C], F32)
    mcol = sb("mcol", [128, 6], F32)
    xntok = sb("xntok", [128, C], BF16)
    xnfm = sb("xnfm", [128, CS, 130], BF16)
    xtr = [sb(f"xtr{k}", [128, 128], BF16) for k in range(CS)]
    scanbig = sb("scanbig", [128, 3072], BF16)
    mixtmp = scanbig[:, 0:1024].rearrange("p (k c) -> p k c", c=128)
    xproj = sb("xproj", [128, 4, CS, 128], BF16)
    uni2 = sb("uni2", [128, 3 * C], F32)
    uni2f = uni2[:].bitcast(BF16)
    ktokp = [uni2f[:, 0:C], uni2f[:, C:2 * C]]
    vtokp = [uni2f[:, 2 * C:3 * C], uni2f[:, 3 * C:4 * C]]
    rtok = uni2f[:, 4 * C:5 * C]
    gsigb = uni2f[:, 5 * C:6 * C]
    sqs = None
    nuss = sb("nuss", [128, 2 * H], F32)
    krfm = sb("krfm", [64, 2, H, 2, 128], BF16)
    mup = sb("mup", [128, 2 * H * 128], BF16)
    gkrm = sb("gkrm", [128, 2 * H * 128], BF16)
    nev = scanbig[0:32, 1024:3072]
    solw = sb("solw", [128, 1024], BF16)
    soltmp = sb("soltmp", [128, 1024], BF16)
    sqs = soltmp[:, 0:C]
    soln = sb("soln", [128, 1024], BF16)
    thtiles = sb("thtiles", [128, 2, 512], BF16)
    dbv = sb("dbv", [128, C], BF16)
    dev = sb("dev", [128, C], BF16)
    ptl = sb("ptl", [128, C], BF16)
    ysb = sb("ysb", [128, D], BF16)
    omb = sb("omb", [128, C], BF16)
    qcol = sb("qcol", [128, H], F32)
    acol = sb("acol", [128, H], F32)
    ttrjunk = sqs
    scol = sb("scol", [128, H], F32)
    sc1 = sb("sc1", [128, H], F32)
    sigT = sb("sigT", [16, 128], F32)
    lamT = sb("lamT", [16, 128], F32)
    recT = sigT
    gamT = lamT
    betT = sigT
    cT = sb("cT", [16, 128], F32)
    lamc = sb("lamc", [16, 2], F32)
    betcol = sb("betcol", [128, H], F32)
    glrow = sb("glrow", [1, 16], F32)
    nbLb = sb("nbLb", [64, H], F32)
    sigma = sb("sigma", [64, H * D], BF16)
    wkvt = xntok
    wkvfmb = scanbig[:, 1024:2048].rearrange("p (k c) -> p k c", c=128)
    uni1 = sb("uni1", [128, H, 16], F32)
    statS = uni1[:, :, 0:8]
    statQ = uni1[:, :, 8:16]
    statacc = sb("statacc", [128, 2 * H], F32)
    stmp = sb("stmp", [128, H], F32)
    statr = sb("statr", [128, 2 * H], F32)
    murow = sb("murow", [1, 64], F32)
    mubc = sb("mubc", [128, H], F32)
    rstdb = sb("rstdb", [128, H], F32)
    shcol = sb("shcol", [128, CS], BF16)
    svcol = sb("svcol", [128, CS], F32)
    shf32 = sb("shf32", [128, CS], F32)
    xprojflat = xproj[:].rearrange("p a k t -> p (a k t)")
    p2xn = xprojflat[:, 0:2048].rearrange("p (k c) -> p k c", c=256)
    p2wk = xprojflat[:, 2048:4096].rearrange("p (k c) -> p k c", c=256)
    p2gs = mup[:].rearrange("p (a c) -> p a c", c=C)
    p2sgb = omb
    p2x = xbuf
    p2dp = gkrm[:].bitcast(F32).rearrange("p (a c) -> p a c", c=C)
    gatsl = omb[:, 0:256]
    hfm = scanbig[:, 0:3072].rearrange("p (k c) -> p k c", c=256)
    gffn = dbv
    t5b = dev
    outb = uni2[:, 0:2 * C].rearrange("p (a c) -> p a c", c=C)

    pjA = psum("pjA", [128, 1024], F32)
    pjB = psum("pjB", [128, 1024], F32)
    pgB = psum("pgB", [128, 512], F32)
    pg = pgB[:, 0:256]
    ppB = psum("ppB", [128, 512], F32)
    pp = ppB[:, 0:64]
    pk2 = psum("pk2", [128, 512], F32)
    pp2 = pk2[:, 0:64]
    pw = pk2[:, 64:128]
    psg = pk2[0:64, 128:192]
    psmall = pk2[:, 192:208]
    psm2 = pk2[0:16, 208:336]
    pbl = pk2[0:64, 336:352]
    ptw = psum("ptw", [128, 640], BF16)
    pt = ptw[0:64, 0:512]
    pwt = ptw[:, 512:640]
    pd = pjB[0:32, :]
    prow = pjA[0:1, :]

    s_ld = Sem(es, nc, "s_ld")
    s_x0 = Sem(es, nc, "s_x0")
    s_x1 = Sem(es, nc, "s_x1")
    s_tr = Sem(es, nc, "s_tr")
    s_g = Sem(es, nc, "s_g")
    s_t = Sem(es, nc, "s_t")
    s_v = Sem(es, nc, "s_v")
    s_s = Sem(es, nc, "s_s")
    s_o = Sem(es, nc, "s_o")

    P = Prog()
    P.sems = [s_ld, s_x0, s_x1, s_tr, s_g, s_t, s_v, s_s, s_o]
    for dst, name in [(identbf, "identbf"), (identf, "identf"), (negibf, "negibf"),
                      (masksneg, "masksneg"), (maskincl, "maskincl"),
                      (mask32r4, "mask32r4"), (iota_r, "iota_r"), (iota1_r, "iota1_r"),
                      (onesb_r, "onesb_r"), (onescol_f, "onescol_f"), (de_row, "decay_eta"),
                      (tmc, "tmcols"), (gnc, "gncols")]:
        P.do("sync", lambda e, d=dst, n=name: e.dma_start(out=d[:], in_=dr[n][:]), inc=s_ld, k=16)
    for bi in range(4):
        P.wait("sync", s_v)
        P.do("sync", lambda e, bi=bi: e.dma_start(out=wstage[0:1, 0:C], in_=dr["biases"][0:1, C * bi:C * (bi + 1)]), inc=s_ld, k=16)
        P.wait("vector", s_ld)
        P.do("vector", lambda e, bi=bi: e.tensor_copy(brow4[0:1, C * bi:C * (bi + 1)], wstage[0:1, 0:C]), inc=s_v)
        P.drain("vector")
    n_const = s_ld.n

    P.wait("vector", s_ld)
    for nm, wdst in [("WkT", wkT), ("WvT", wvT), ("WrT", wrT), ("WgT", wgT)]:
        for k in range(CS):
            P.wait("sync", s_v)
            P.do("sync", lambda e, nm=nm, k=k: e.dma_start(
                out=wstage[:, 0:C],
                in_=dr[nm][:].rearrange("(k p) c -> k p c", p=128)[k]), inc=s_ld, k=16)
            P.wait("vector", s_ld)
            P.do("vector", lambda e, wdst=wdst, k=k: e.tensor_copy(wdst[:, k, :], wstage[:, 0:C]), inc=s_v)

    P.wait("scalar", s_ld, n_const)
    P.do("scalar", lambda e: e.activation(hrow[0:1, 0:32], de_row[0:1, 0:32], AF.Sigmoid))
    P.drain("scalar")
    P.do("scalar", lambda e: e.activation(hrow[0:1, 32:48], hrow[0:1, 0:16], AF.Ln), inc=s_s)
    P.wait("vector", s_s)
    for fn in [
        lambda e: e.reciprocal(hrow[0:1, 48:64], hrow[0:1, 0:16]),
        lambda e: e.tensor_mul(hrow[0:1, 48:64], hrow[0:1, 48:64], hrow[0:1, 16:32]),
        lambda e: e.scalar_tensor_tensor(hrow[0:1, 64:80], hrow[0:1, 0:16], -2.0, hrow[0:1, 16:32], OP.mult, OP.add),
        lambda e: e.tensor_mul(hrow[0:1, 64:80], hrow[0:1, 64:80], hrow[0:1, 16:32]),
        lambda e: e.tensor_sub(hrow[0:1, 80:96], hrow[0:1, 0:16], hrow[0:1, 16:32]),
        lambda e: e.tensor_mul(hrow[0:1, 80:96], hrow[0:1, 80:96], hrow[0:1, 16:32]),
        lambda e: e.tensor_scalar_mul(hrow[0:1, 80:96], hrow[0:1, 80:96], 2.0),
        lambda e: e.reciprocal(hrow[0:1, 96:112], hrow[0:1, 48:64]),
        lambda e: e.tensor_mul(hrow[0:1, 80:96], hrow[0:1, 80:96], hrow[0:1, 96:112]),
        lambda e: e.tensor_mul(hrow[0:1, 96:112], hrow[0:1, 16:32], hrow[0:1, 16:32]),
        lambda e: e.tensor_mul(hrow[0:1, 112:128], hrow[0:1, 0:16], hrow[0:1, 0:16]),
    ]:
        P.do("vector", fn)
        P.drain("vector")
    P.do("vector", lambda e: e.engine_nop(), inc=s_v)
    v_hrow = s_v.n

    def mm_tab(lhs_fn, rhs_fn, out_ap_fn, post_eng, post_fn):
        P.wait("tensor", s_v, v_hrow)
        P.wait("tensor", s_s)
        P.do("tensor", lambda e, l=lhs_fn, r=rhs_fn, o=out_ap_fn: e.matmul(
            o(), l(), r(), start=True, stop=True), inc=s_t)
        P.wait(post_eng, s_t)
        P.do(post_eng, post_fn, inc=s_s)

    mm_tab(lambda: iota_r[:], lambda: hrow[0:1, 32:48], lambda: psmall[:], "scalar",
           lambda e: e.activation(t_negdelta[:], psmall[:], AF.Exp, scale=-1.0))
    P.wait("vector", s_s)
    P.do("vector", lambda e: e.tensor_scalar_mul(t_negdelta[:], t_negdelta[:], -1.0), inc=s_v)
    mm_tab(lambda: ones_r[:], lambda: hrow[0:1, 48:64], lambda: psmall[:], "vector",
           lambda e: e.tensor_copy(t_eta_b[:], psmall[:]))
    P.wait("vector", s_s)
    P.drain("vector")
    P.do("vector", lambda e: e.tensor_mul(t_etadelta[:], t_eta_b[:], t_negdelta[:]))
    P.drain("vector")
    P.do("vector", lambda e: e.tensor_scalar_mul(t_etadelta[:], t_etadelta[:], -1.0), inc=s_v)
    mm_tab(lambda: iota_r[:], lambda: hrow[0:1, 32:48], lambda: psmall[:], "scalar",
           lambda e: e.activation(t_dec2t[:], psmall[:], AF.Exp, scale=2.0))
    mm_tab(lambda: ones_r[:], lambda: hrow[0:1, 64:80], lambda: psmall[:], "vector",
           lambda e: e.tensor_mul(t_q[:], psmall[:], t_dec2t[:]))
    mm_tab(lambda: ones_r[:], lambda: hrow[0:1, 80:96], lambda: psmall[:], "vector",
           lambda e: e.tensor_mul(t_a[:], psmall[:], t_dec2t[:]))
    mm_tab(lambda: ones_r[:], lambda: hrow[0:1, 96:112], lambda: psmall[:], "vector",
           lambda e: e.tensor_copy(t_et2[:], psmall[:]))
    mm_tab(lambda: hrow[0:1, 32:48], lambda: iota1_r[:], lambda: psm2[:], "scalar",
           lambda e: e.activation(t_dect1[:], psm2[:], AF.Exp))
    mm_tab(lambda: hrow[0:1, 112:128], lambda: ones_r[:, 0:1], lambda: psm2[0:16, 0:1], "vector",
           lambda e: e.tensor_copy(t_dec2c[:], psm2[0:16, 0:1]))
    mm_tab(lambda: ones_r[:, 0:32], lambda: hrow[0:1, 48:64], lambda: psmall[0:32, :], "vector",
           lambda e: e.tensor_copy(etab32[:], psmall[0:32, :]))
    P.wait("vector", s_s)
    P.drain("vector")
    for h in range(H):
        P.do("vector", lambda e, h=h: e.scalar_tensor_tensor(
            etamask[:].rearrange("p (j h c) -> p j h c", h=H, c=32)[:, :, h, :],
            mask32r4[:].rearrange("p (j c) -> p j c", c=32),
            etab32[:, h:h + 1],
            mask32r4[:].rearrange("p (j c) -> p j c", c=32),
            OP.mult, OP.mult))
    P.do("vector", lambda e: e.memset(sigma[:], 0.0))
    P.do("vector", lambda e: e.memset(uni1[:], 0.0))
    P.do("vector", lambda e: e.memset(statacc[:], 0.0))
    P.do("vector", lambda e: e.memset(lamc[:], 0.0))
    P.do("vector", lambda e: e.memset(xnfm[:, :, 0:1], 0.0))
    P.do("vector", lambda e: e.engine_nop(), inc=s_v)
    v_init = s_v.n

    P.mark("p0", [s_ld, s_x0, s_x1, s_tr, s_g, s_t, s_v, s_s, s_o])
    BANKS = {
        id(pjA): (0, 1), id(prow): (0, 1),
        id(pjB): (2, 3), id(pd): (2, 3),
        id(pgB): (4,), id(pg): (4,),
        id(ppB): (5,), id(pp): (5,),
        id(pk2): (6,), id(pp2): (6,), id(pw): (6,), id(psg): (6,),
        id(psmall): (6,), id(psm2): (6,), id(pbl): (6,),
        id(ptw): (7,), id(pt): (7,), id(pwt): (7,),
    }
    free = {}
    def mark_free(ps_id, sem):
        for b in BANKS.get(ps_id, (ps_id,)):
            free[b] = (sem, sem.n)
    def wait_free(eng, ps_id):
        for b in BANKS.get(ps_id, (ps_id,)):
            if b in free:
                sem, n = free[b]
                P.wait(eng, sem, n)

    solw_diag = AP(solw, 0, [[1024, 128], [33, 32]])

    xn_done = []
    def phaseA(ch):
        par = ch % 2
        xb = xbuf[:, par, :]
        tile = slice(128 * ch, 128 * (ch + 1))
        P.wait("vector", s_g)
        P.wait("scalar", s_g)

        sx = s_x0 if par == 0 else s_x1
        if ch >= 2:
            P.wait("sync", s_v, xn_done[ch - 2])
        P.do("sync", lambda e, xb=xb, tile=tile: e.dma_start(out=xb[:], in_=dr["x"][tile, :]), inc=sx, k=16)
        P.wait("vector", sx)
        if ch == 0:
            P.wait("vector", s_v, v_init)
        P.do("vector", lambda e, xb=xb: e.tensor_reduce(mcol[:, 0:1], xb[:], AX.X, OP.add))
        P.wait("scalar", sx)
        wait_free("scalar", id(sqs))
        P.do("scalar", lambda e, xb=xb: e.activation(sqs[:].bitcast(F32)[:, 0:512], xb[:, 0:512], AF.Square, accum_out=mcol[:, 2:3]))
        P.drain("scalar")
        P.do("scalar", lambda e, xb=xb: e.activation(sqs[:].bitcast(F32)[:, 0:512], xb[:, 512:1024], AF.Square, accum_out=mcol[:, 3:4]), inc=s_s)
        P.drain("vector")
        P.do("vector", lambda e: e.tensor_scalar_mul(mcol[:, 0:1], mcol[:, 0:1], 1.0 / C))
        P.wait("vector", s_s)
        P.drain("vector")
        P.do("vector", lambda e: e.tensor_mul(mcol[:, 5:6], mcol[:, 0:1], mcol[:, 0:1]))
        P.do("vector", lambda e: e.tensor_add(mcol[:, 4:5], mcol[:, 2:3], mcol[:, 3:4]))
        P.drain("vector")
        P.do("vector", lambda e: e.tensor_scalar_mul(mcol[:, 4:5], mcol[:, 4:5], 1.0 / C))
        P.drain("vector")
        P.do("vector", lambda e: e.tensor_sub(mcol[:, 4:5], mcol[:, 4:5], mcol[:, 5:6]))
        P.drain("vector")
        P.do("vector", lambda e: e.tensor_scalar(mcol[:, 4:5], mcol[:, 4:5], 1.0, 1e-5, OP.mult, OP.add))
        P.drain("vector")
        P.do("vector", lambda e: e.reciprocal(mcol[:, 5:6], mcol[:, 4:5]), inc=s_v)
        P.wait("scalar", s_v)
        P.do("scalar", lambda e: e.activation(mcol[:, 5:6], mcol[:, 5:6], AF.Sqrt), inc=s_s)
        P.wait("vector", s_s)
        P.do("vector", lambda e, xb=xb: e.tensor_scalar(xntok[:], xb[:], mcol[:, 0:1], mcol[:, 5:6], OP.subtract, OP.mult), inc=s_v)
        v_xn = s_v.n
        xn_done.append(v_xn)
        if debug:
            P.wait("gpsimd", s_v, v_xn)
            P.do("gpsimd", lambda e, tile=tile: e.dma_start(out=dr["dbg_xn"][tile, :], in_=xntok[:]), inc=s_g, k=16)

        P.wait("scalar", s_v, v_xn)
        for k in range(CS):
            P.do("scalar", lambda e, k=k: e.dma_start(
                out=xtr[k][:], in_=xntok[:, 128 * k:128 * (k + 1)], transpose=True), inc=s_tr, k=16)
        tr_xn = s_tr.n
        P.wait("scalar", s_tr, tr_xn)
        P.wait("scalar", s_g)
        for k in range(CS):
            P.do("scalar", lambda e, k=k: e.copy(xnfm[:, k, 1:129], xtr[k][:]),
                 inc=(s_s if k == CS - 1 else None))
        s_fm = s_s.n
        P.wait("gpsimd", s_s, s_fm)
        for hp in range(2):
            P.do("gpsimd", lambda e, ch=ch, hp=hp: e.dma_start(
                out=dr["xnfm_sc"][64 * hp:64 * (hp + 1), :, 128 * ch:128 * (ch + 1)],
                in_=xnfm[64 * hp:64 * (hp + 1), :, 1:129]), inc=s_g, k=16)

        P.wait("vector", s_s, s_fm)
        xnv = xnfm[:, :, 1:129]
        xsv = xnfm[:, :, 0:128]
        P.do("vector", lambda e, xnv=xnv, xsv=xsv: e.tensor_sub(mixtmp[:], xnv, xsv))
        P.drain("vector")
        for pi in range(4):
            for k in range(CS):
                P.do("vector", lambda e, pi=pi, k=k, xsv=xsv: e.scalar_tensor_tensor(
                    xproj[:, pi, k, :], mixtmp[:, k, :], tmc[:, pi, k:k + 1], xsv[:, k, :],
                    OP.mult, OP.add))
        P.do("vector", lambda e: e.engine_nop(), inc=s_v)
        v_mix = s_v.n
        P.drain("vector")
        P.do("vector", lambda e, xnv=xnv: e.tensor_copy(xnfm[:, :, 0:1], xnfm[:, :, 128:129]))
        P.do("vector", lambda e: e.engine_nop(), inc=s_v)

        wlist = [wkT, wvT, wrT, wgT]
        def proj(pi, pst, bias_i, evac_fn):
            P.wait("tensor", s_v, v_mix)
            wait_free("tensor", id(pst))
            for nn in range(2):
                for k in range(CS):
                    P.do("tensor", lambda e, pi=pi, k=k, pst=pst, nn=nn: e.matmul(
                        pst[:, 512 * nn:512 * (nn + 1)], xproj[:, pi, k, :],
                        wlist[pi][:, k, 512 * nn:512 * (nn + 1)], start=(k == 0), stop=False))
                P.do("tensor", lambda e, pst=pst, bias_i=bias_i, nn=nn: e.matmul(
                    pst[:, 512 * nn:512 * (nn + 1)], onesb_r[:],
                    brow4[0:1, C * bias_i + 512 * nn:C * bias_i + 512 * (nn + 1)],
                    start=False, stop=True), inc=(s_t if nn == 1 else None))
            evac_fn()

        def ev_k():
            P.wait("vector", s_t)
            P.do("vector", lambda e: e.tensor_copy(ktokp[par][:], pjA[:]), inc=s_v)
            mark_free(id(pjA), s_v)
        def ev_v():
            P.wait("vector", s_t)
            P.do("vector", lambda e: e.tensor_copy(vtokp[par][:], pjB[:]), inc=s_v)
            mark_free(id(pjB), s_v)
        def ev_r():
            P.wait("scalar", s_t)
            P.do("scalar", lambda e: e.activation(rtok[:], pjA[:], AF.Sigmoid), inc=s_s)
            mark_free(id(pjA), s_s)
        def ev_g():
            P.wait("scalar", s_t)
            P.do("scalar", lambda e: e.activation(gsigb[:], pjB[:], AF.Silu), inc=s_s)
            mark_free(id(pjB), s_s)

        proj(0, pjA, 0, ev_k)
        proj(1, pjB, 1, ev_v)
        proj(2, pjA, 2, ev_r)
        proj(3, pjB, 3, ev_g)
        s_rg = s_s.n
        P.wait("gpsimd", s_s, s_rg)
        P.do("gpsimd", lambda e, tile=tile: e.dma_start(out=dr["gsig_sc"][tile, :], in_=gsigb[:]), inc=s_g, k=16)

        P.drain("vector")
        wait_free("vector", id(sqs))
        P.do("vector", lambda e: e.tensor_mul(sqs[:], ktokp[par][:], ktokp[par][:]))
        P.drain("vector")
        P.do("vector", lambda e: e.tensor_reduce(nuss[:, 0:H], sqs[:].rearrange("p (h d) -> p h d", d=D), AX.X, OP.add))
        P.drain("vector")
        P.do("vector", lambda e: e.tensor_mul(sqs[:], vtokp[par][:], vtokp[par][:]))
        P.drain("vector")
        P.do("vector", lambda e: e.tensor_reduce(nuss[:, H:2 * H], sqs[:].rearrange("p (h d) -> p h d", d=D), AX.X, OP.add))
        P.drain("vector")
        mark_free(id(sqs), s_v)
        P.do("vector", lambda e: e.tensor_scalar_max(nuss[:], nuss[:], 1e-24))
        P.drain("vector")
        P.do("vector", lambda e: e.reciprocal(nuss[:], nuss[:]), inc=s_v)
        P.wait("scalar", s_v)
        P.do("scalar", lambda e: e.activation(nuss[:], nuss[:], AF.Sqrt), inc=s_s)
        P.wait("vector", s_s)
        P.do("vector", lambda e: e.tensor_tensor(
            ktokp[par][:].rearrange("p (h d) -> p h d", d=D), ktokp[par][:].rearrange("p (h d) -> p h d", d=D),
            nuss[:, 0:H].unsqueeze(2).to_broadcast([128, H, D]), OP.mult))
        P.do("vector", lambda e: e.tensor_tensor(
            vtokp[par][:].rearrange("p (h d) -> p h d", d=D), vtokp[par][:].rearrange("p (h d) -> p h d", d=D),
            nuss[:, H:2 * H].unsqueeze(2).to_broadcast([128, H, D]), OP.mult), inc=s_v)
        P.drain("vector")
        P.do("vector", lambda e: e.engine_nop(), inc=s_v)
        v_kv = s_v.n
        if debug:
            P.wait("gpsimd", s_v, v_kv)
            for nm, src in [("dbg_k", ktokp[par]), ("dbg_v", vtokp[par]), ("dbg_r", rtok)]:
                P.do("gpsimd", lambda e, tile=tile, nm=nm, src=src: e.dma_start(
                    out=dr[nm][tile, :], in_=src[:]), inc=s_g, k=16)

        P.wait("tensor", s_v, v_kv)
        for hh in range(0, H, 2):
            wait_free("tensor", id(pt))
            for j in range(4):
                h = hh + (j // 2)
                src = ktokp[par] if (j % 2 == 0) else rtok
                P.do("tensor", lambda e, src=src, h=h, j=j: e.transpose(
                    pt[:, 128 * j:128 * (j + 1)], src[:, D * h:D * (h + 1)], identbf[:]),
                    inc=(s_t if j == 3 else None))
            P.wait("scalar", s_t)
            P.do("scalar", lambda e, hh=hh: e.copy(
                krfm[:, par, hh:hh + 2, :, :].rearrange("p a b t -> p (a b t)"), pt[:]), inc=s_s)
            mark_free(id(pt), s_s)
        s_krfm = s_s.n

        P.wait("tensor", s_s, s_krfm)
        for h in range(H):
            wait_free("tensor", id(pg))
            P.do("tensor", lambda e, h=h: e.matmul(
                pg[:], krfm[:, par, h, 0, :], krfm[:, par, h, :, :].rearrange("p a t -> p (a t)"),
                start=True, stop=True), inc=s_t)
            P.wait("vector", s_t)
            P.do("vector", lambda e, h=h: e.scalar_tensor_tensor(
                mup[:, par * 2048 + 128 * h:par * 2048 + 128 * (h + 1)], pg[:, 0:128], t_eta_b[:, h:h + 1],
                masksneg[:], OP.mult, OP.mult))
            P.do("vector", lambda e, h=h: e.scalar_tensor_tensor(
                gkrm[:, par * 2048 + 128 * h:par * 2048 + 128 * (h + 1)], pg[:, 128:256], -1.0,
                maskincl[:], OP.mult, OP.mult), inc=s_v)
            mark_free(id(pg), s_v)
        v_masks = s_v.n

        for half in range(2):
            wait_free("tensor", id(pd))
            for h8 in range(8):
                h = 8 * half + h8
                for j in range(4):
                    last = (h8 == 7 and j == 3)
                    P.do("tensor", lambda e, h=h, h8=h8, j=j: e.matmul(
                        pd[:, 256 * j + 32 * h8:256 * j + 32 * (h8 + 1)],
                        krfm[:, par, h, 0, 32 * j:32 * (j + 1)], krfm[:, par, h, 0, 32 * j:32 * (j + 1)],
                        start=True, stop=True), inc=(s_t if last else None))
            P.wait("vector", s_t)
            P.do("vector", lambda e, half=half: e.tensor_tensor(
                nev[:].rearrange("p (j h c) -> p j h c", h=H, c=32)[:, :, 8 * half:8 * (half + 1), :],
                pd[:].rearrange("p (j h c) -> p j h c", h=8, c=32),
                etamask[:].rearrange("p (j h c) -> p j h c", h=H, c=32)[:, :, 8 * half:8 * (half + 1), :],
                OP.mult), inc=s_v)
            mark_free(id(pd), s_v)
            mark_free(id(pjB), s_v)
        v_nev = s_v.n

        P.wait("gpsimd", s_v, v_nev)
        for rr in range(4):
            P.do("gpsimd", lambda e, par=par, rr=rr: e.dma_start(
                out=dr["bn1"][par].rearrange("(i r c) -> r i c", r=32, c=32)[8 * rr:8 * (rr + 1), :, :],
                in_=nev[8 * rr:8 * (rr + 1), :].rearrange("r (i c) -> r i c", c=32)), inc=s_g, k=16)
        P.wait("gpsimd", s_g)
        P.do("gpsimd", lambda e, par=par: e.dma_start(
            out=soln[64 * par:64 * (par + 1), :],
            in_=dr["bn1"][par].rearrange("(i f) -> i f", i=64)), inc=s_g, k=16)
        g_soln = s_g.n


    def dosolve(ch):
        par = ch % 2
        if par == 1:
            P.wait("vector", s_g)
            P.do("vector", lambda e: e.memset(solw[:], 0.0))
            P.drain("vector")
            if "nodiag" not in solvemode:
                P.do("vector", lambda e: e.memset(solw_diag, 1.0))
            P.drain("vector")
            for r in range(30, -1, -1):
                if "noiter" in solvemode:
                    break
                ns = 31 - r
                P.do("vector", lambda e, r=r, ns=ns: e.tensor_tensor(
                    soltmp[:, 0:ns * ns].rearrange("p (c s) -> p c s", s=ns),
                    solw[:].rearrange("p (s c) -> p c s", c=32)[:, r + 1:32, r + 1:32],
                    soln[:, 32 * r + r + 1:32 * r + 32].unsqueeze(1).to_broadcast([128, ns, ns]),
                    OP.mult))
                P.drain("vector")
                def _red(e, r=r, ns=ns):
                    with nc.allow_low_precision(reason="bf16 triangular-solve accumulate (matmul-grade)"):
                        return e.tensor_reduce(
                            solw[:, 32 * r + r + 1:32 * r + 32],
                            soltmp[:, 0:ns * ns].rearrange("p (c s) -> p c s", s=ns),
                            AX.X, OP.add, negate=True)
                P.do("vector", _red)
                P.drain("vector")
            P.do("vector", lambda e: e.engine_nop(), inc=s_v)
            v_solve = s_v.n
            P.wait("gpsimd", s_t)
            for pr in range(2):
                if "nodma" in solvemode:
                    break
                P.wait("gpsimd", s_v, v_solve)
                for j in range(4):
                    P.do("gpsimd", lambda e, pr=pr, j=j: e.dma_start(
                        out=dr["bn2"][pr].rearrange("(r h c) -> h r c", h=16, c=32)
                            [:, 32 * j:32 * (j + 1), :] if False else
                            dr["bn2"][pr].rearrange("(rr c) -> rr c", c=32)
                            .rearrange("(r h) c -> h r c", h=16)[:, :, :][:, 32 * j:32 * (j + 1), :]
                            if False else
                            AP(dr["bn2"].tensor, pr * 65536 + 512 * 32 * j, [[32, 16], [512, 32], [1, 32]]),
                        in_=solw[64 * pr + 16 * j:64 * pr + 16 * (j + 1), :]
                            .rearrange("h (r c) -> h r c", c=32)), inc=s_g, k=16)
                P.wait("gpsimd", s_g)
                P.do("gpsimd", lambda e, pr=pr: e.dma_start(
                    out=thtiles[:, pr, :],
                    in_=dr["bn2"][pr].rearrange("(r f) -> r f", r=128)), inc=s_g, k=16)
            


    def phaseB(ch):
        par = ch % 2
        tile = slice(128 * ch, 128 * (ch + 1))
        P.wait("vector", s_g)
        P.wait("scalar", s_g)
        skip_qa = "noqa" in phb
        if skip_qa:
            P.do("vector", lambda e: e.memset(omb[:], 0.1))
            P.do("vector", lambda e: e.memset(qcol[:], 0.1))
            P.do("vector", lambda e: e.memset(acol[:], 0.1), inc=s_v)
        v_dv = s_v.n
        if not skip_qa:
            P.do("vector", lambda e: e.tensor_tensor(
                dbv[:].rearrange("p (h d) -> p h d", d=D), vtokp[par][:].rearrange("p (h d) -> p h d", d=D),
                t_negdelta[:].unsqueeze(2).to_broadcast([128, H, D]), OP.mult))
            P.do("vector", lambda e: e.tensor_tensor(
                dev[:].rearrange("p (h d) -> p h d", d=D), vtokp[par][:].rearrange("p (h d) -> p h d", d=D),
                t_etadelta[:].unsqueeze(2).to_broadcast([128, H, D]), OP.mult), inc=s_v)
            v_dv = s_v.n

        P.wait("tensor", s_g)
        P.wait("tensor", s_v, v_dv)
        thp = thtiles[:, par, :]
        for h in range(H if "noptl" not in phb else 0):
            wait_free("tensor", id(pp))
            wait_free("tensor", id(pp2))
            P.do("tensor", lambda e, h=h: e.matmul(
                pp[:], krfm[:, par, h, 0, :], sigma[:, D * h:D * (h + 1)], start=True, stop=False,
                skip_group_check=True))
            for jj in range(4):
                P.do("tensor", lambda e, h=h, jj=jj: e.matmul(
                    pp[32 * jj:32 * (jj + 1), :],
                    mup[:, par * 2048 + 128 * h + 32 * jj:par * 2048 + 128 * h + 32 * (jj + 1)],
                    dbv[:, D * h:D * (h + 1)],
                    start=False, stop=(jj == 0), skip_group_check=True,
                    tile_position=((0, 96) if jj == 3 else None)), inc=(s_t if jj == 3 else None))
            for j in range(4):
                if j > 0:
                    P.wait("tensor", s_s)
                    P.do("tensor", lambda e, h=h, j=j: e.matmul(
                        pp[32 * j:32 * (j + 1), :],
                        mup[0:32 * j, par * 2048 + 128 * h + 32 * j:par * 2048 + 128 * h + 32 * (j + 1)],
                        ptl[0:32 * j, D * h:D * (h + 1)], start=False, stop=True,
                        skip_group_check=True,
                        tile_position=((0, 96) if j == 3 else None)), inc=s_t)
                P.wait("scalar", s_t)
                P.do("scalar", lambda e, j=j: e.copy(ysb[32 * j:32 * (j + 1), :], pp[32 * j:32 * (j + 1), :]), inc=s_s)
                P.wait("tensor", s_s)
                P.do("tensor", lambda e, h=h, j=j, thp=thp: e.matmul(
                    pp2[32 * j:32 * (j + 1), :],
                    thp[32 * j:32 * (j + 1), 32 * h:32 * (h + 1)],
                    ysb[32 * j:32 * (j + 1), :], start=True, stop=True,
                    skip_group_check=True,
                    tile_position=((96, 96) if j == 3 else None)), inc=s_t)
                P.wait("scalar", s_t)
                P.do("scalar", lambda e, h=h, j=j: e.copy(
                    ptl[32 * j:32 * (j + 1), D * h:D * (h + 1)], pp2[32 * j:32 * (j + 1), :]), inc=s_s)
            mark_free(id(pp), s_s)
            mark_free(id(pp2), s_s)
        s_ptl = s_s.n

        P.wait("vector", s_s, s_ptl)
        for h in range(H if not skip_qa else 0):
            P.do("scalar", lambda e, h=h: e.activation(
                ttrjunk[:, D * h:D * (h + 1)], ptl[:, D * h:D * (h + 1)], AF.Square,
                accum_out=qcol[:, h:h + 1]), inc=s_s)
            P.do("vector", lambda e, h=h: e.scalar_tensor_tensor(
                scanbig[:, 1024 + D * h:1024 + D * (h + 1)], ptl[:, D * h:D * (h + 1)], 1.0,
                dev[:, D * h:D * (h + 1)],
                OP.mult, OP.mult, accum_out=acol[:, h:h + 1]))
            P.drain("vector")
        P.drain("vector")
        if not skip_qa:
            P.do("vector", lambda e: e.tensor_tensor(
                omb[:].rearrange("p (h d) -> p h d", d=D), ptl[:].rearrange("p (h d) -> p h d", d=D),
                t_eta_b[:].unsqueeze(2).to_broadcast([128, H, D]), OP.mult))
            P.drain("vector")
            P.do("vector", lambda e: e.tensor_sub(omb[:], omb[:], dev[:]), inc=s_v)
        v_omb = s_v.n
        P.wait("vector", s_s)
        P.do("vector", lambda e: e.tensor_mul(sc1[:], qcol[:], t_q[:]))
        P.drain("vector")
        P.do("vector", lambda e: e.tensor_mul(scol[:], acol[:], t_a[:]))
        P.drain("vector")
        P.do("vector", lambda e: e.tensor_add(scol[:], scol[:], sc1[:]))
        P.drain("vector")
        P.do("vector", lambda e: e.tensor_add(scol[:], scol[:], t_et2[:]), inc=s_v)
        v_scol = s_v.n
        if "noscal" in phb:
            P.do("vector", lambda e: e.memset(betcol[:], 0.5))
            P.do("vector", lambda e: e.memset(nbLb[:], 0.5), inc=s_v)
            skip_scal = True
        else:
            skip_scal = False
        if not skip_scal:
            P.wait("tensor", s_v, v_scol)
            wait_free("tensor", id(psm2))
            P.do("tensor", lambda e: e.transpose(psm2[:], scol[:], identf[:]), inc=s_t)
            P.wait("vector", s_t)
            P.do("vector", lambda e: e.tensor_copy(sigT[:], psm2[:]), inc=s_v)
            mark_free(id(psm2), s_v)
            P.drain("vector")
            P.do("vector", lambda e, par=par: e.tensor_tensor_scan(
                lamT[:], t_dec2c[:].to_broadcast([16, 128]), sigT[:], lamc[:, par:par + 1],
                OP.mult, OP.add))
            P.drain("vector")
            P.do("vector", lambda e: e.tensor_scalar_max(lamT[:], lamT[:], 1e-30))
            P.drain("vector")
            P.do("vector", lambda e, par=par: e.tensor_scalar_min(
                lamc[:, 1 - par:2 - par], lamT[:, 127:128], 256.0))
            P.drain("vector")
            P.do("vector", lambda e: e.reciprocal(recT[:], lamT[:]))
            P.drain("vector")
            P.do("vector", lambda e: e.tensor_scalar(cT[:], recT[:], 256.0, 1.0, OP.mult, OP.min), inc=s_v)
            P.wait("scalar", s_v)
            P.do("scalar", lambda e: e.activation(cT[:], cT[:], AF.Sqrt), inc=s_s)
            P.wait("vector", s_s)
            P.do("vector", lambda e: e.tensor_tensor_scan(
                gamT[:], cT[:], cT[:], 1.0, OP.mult, OP.bypass))
            P.drain("vector")
            P.do("vector", lambda e: e.tensor_mul(betT[:], gamT[:], t_dect1[:]), inc=s_v)
            v_bet = s_v.n
            P.wait("tensor", s_v, v_bet)
            wait_free("tensor", id(psm2))
            P.do("tensor", lambda e: e.transpose(psmall[:], betT[:], identf[0:16, 0:16]), inc=s_t)
            P.wait("vector", s_t)
            P.do("vector", lambda e: e.tensor_copy(betcol[:], psmall[:]), inc=s_v)
            mark_free(id(psmall), s_v)
            v_betc = s_v.n
            if debug:
                P.wait("gpsimd", s_v, v_betc)
                P.do("gpsimd", lambda e, tile=tile: e.dma_start(out=dr["dbg_beta"][tile, :], in_=betcol[:]), inc=s_g, k=16)
            P.wait("gpsimd", s_v, v_bet)
            P.do("gpsimd", lambda e: e.dma_start(out=dr["rowb"][0, :], in_=betT[:, 127:128]), inc=s_g, k=16)
            P.wait("gpsimd", s_g)
            P.do("gpsimd", lambda e: e.dma_start(out=glrow[:], in_=dr["rowb"][0:1, :]), inc=s_g, k=16)
            P.wait("tensor", s_g)
            wait_free("tensor", id(pbl))
            P.do("tensor", lambda e: e.matmul(pbl[:], negones_r[:], glrow[:], start=True, stop=True), inc=s_t)
            P.wait("vector", s_t)
            P.do("vector", lambda e: e.tensor_copy(nbLb[:], pbl[:]), inc=s_v)
            mark_free(id(pbl), s_v)
            v_nbl = s_v.n

        if skip_scal:
            v_betc = s_v.n
            v_nbl = s_v.n
        P.wait("tensor", s_v, v_omb)
        P.wait("tensor", s_v, v_betc)
        P.wait("tensor", s_v, v_nbl)
        for h in range(H if "nowkv" not in phb else 0):
            wait_free("tensor", id(pw))
            P.do("tensor", lambda e, h=h: e.matmul(
                pw[:], krfm[:, par, h, 1, :], sigma[:, D * h:D * (h + 1)], start=True, stop=False))
            P.do("tensor", lambda e, h=h: e.matmul(
                pw[:], gkrm[:, par * 2048 + 128 * h:par * 2048 + 128 * (h + 1)], omb[:, D * h:D * (h + 1)],
                start=False, stop=True), inc=s_t)
            P.wait("vector", s_t)
            acc1 = None if "noacc" in phb else 1
            P.do("vector", lambda e, h=h, ch=ch, acc1=acc1: e.tensor_scalar(
                wkvt[:, D * h:D * (h + 1)], pw[:], betcol[:, h:h + 1], 0.0, OP.mult, OP.add,
                accum_out=(statS[:, h, (ch % 8):(ch % 8) + 1] if acc1 else None)), inc=s_v)
            mark_free(id(pw), s_v)
            P.drain("vector")
            P.do("vector", lambda e, h=h, ch=ch, acc1=acc1: e.scalar_tensor_tensor(
                scanbig[:, 1024 + D * h:1024 + D * (h + 1)], wkvt[:, D * h:D * (h + 1)], 1.0,
                wkvt[:, D * h:D * (h + 1)],
                OP.mult, OP.mult, accum_out=(statQ[:, h, (ch % 8):(ch % 8) + 1] if acc1 else None)), inc=s_v)
            if "nosig" in phb:
                continue
            P.wait("tensor", s_v)
            wait_free("tensor", id(psg))
            P.do("tensor", lambda e, h=h: e.matmul(
                psg[:], negibf[:], sigma[:, D * h:D * (h + 1)], start=True, stop=False))
            P.do("tensor", lambda e, h=h: e.matmul(
                psg[:], ktokp[par][:, D * h:D * (h + 1)], omb[:, D * h:D * (h + 1)],
                start=False, stop=True), inc=s_t)
            P.wait("vector", s_t)
            P.do("vector", lambda e, h=h: e.tensor_scalar(
                sigma[:, D * h:D * (h + 1)], psg[:], nbLb[:, h:h + 1], 0.0, OP.mult, OP.add), inc=s_v)
            mark_free(id(psg), s_v)
        v_wkv = s_v.n
        if (ch % 8 == 7) or (ch == nch - 1):
            nsl = (ch % 8) + 1
            P.drain("vector")
            P.do("vector", lambda e, nsl=nsl: e.tensor_reduce(stmp[:], statS[:, :, 0:nsl], AX.X, OP.add))
            P.drain("vector")
            P.do("vector", lambda e: e.tensor_add(statacc[:, 0:H], statacc[:, 0:H], stmp[:]))
            P.drain("vector")
            P.do("vector", lambda e, nsl=nsl: e.tensor_reduce(stmp[:], statQ[:, :, 0:nsl], AX.X, OP.add))
            P.drain("vector")
            P.do("vector", lambda e: e.tensor_add(statacc[:, H:2 * H], statacc[:, H:2 * H], stmp[:]), inc=s_v)
        if debug:
            P.wait("gpsimd", s_v, v_wkv)
            P.do("gpsimd", lambda e, tile=tile: e.dma_start(out=dr["dbg_wkv"][tile, :], in_=wkvt[:]), inc=s_g, k=16)

        P.wait("tensor", s_v, v_wkv)
        for hh in range(0, H if ("nowkv" not in phb and "nofmt" not in phb) else 0, 2):
            wait_free("tensor", id(pwt))
            P.do("tensor", lambda e, hh=hh: e.transpose(
                pwt[0:64, :], wkvt[:, D * hh:D * (hh + 1)], identbf[:]))
            P.do("tensor", lambda e, hh=hh: e.transpose(
                pwt[64:128, :], wkvt[:, D * (hh + 1):D * (hh + 2)], identbf[:]), inc=s_t)
            P.wait("vector", s_t)
            P.do("vector", lambda e, hh=hh: e.tensor_copy(wkvfmb[:, hh // 2, :], pwt[:]), inc=s_v)
            mark_free(id(pwt), s_v)
        v_wfm = s_v.n
        P.wait("gpsimd", s_v, v_wfm)
        for hp in range(2 if ("nowkv" not in phb and "nofmt" not in phb) else 0):
            P.do("gpsimd", lambda e, ch=ch, hp=hp: e.dma_start(
                out=dr["wkvfm_sc"][64 * hp:64 * (hp + 1), :, 128 * ch:128 * (ch + 1)],
                in_=wkvfmb[64 * hp:64 * (hp + 1), :, :]), inc=s_g, k=16)


    for pp_ in range(nch // 2):
        phaseA(2 * pp_)
        P.mark(f"A{2 * pp_}", [s_ld, s_x0, s_x1, s_tr, s_g, s_t, s_v, s_s, s_o])
        phaseA(2 * pp_ + 1)
        P.mark(f"A{2 * pp_ + 1}", [s_ld, s_x0, s_x1, s_tr, s_g, s_t, s_v, s_s, s_o])
        dosolve(2 * pp_ + 1)
        P.mark(f"S{pp_}", [s_ld, s_x0, s_x1, s_tr, s_g, s_t, s_v, s_s, s_o])
        phaseB(2 * pp_)
        P.mark(f"B{2 * pp_}", [s_ld, s_x0, s_x1, s_tr, s_g, s_t, s_v, s_s, s_o])
        phaseB(2 * pp_ + 1)
        P.mark(f"pair{pp_}", [s_ld, s_x0, s_x1, s_tr, s_g, s_t, s_v, s_s, s_o])
    P.mark("pass1", [s_ld, s_x0, s_x1, s_tr, s_g, s_t, s_v, s_s, s_o])
    P.drain("vector")
    P.do("vector", lambda e: e.tensor_copy(statr[:], statacc[:]), inc=s_v)
    v_statr = s_v.n
    P.wait("tensor", s_v, v_statr)
    wait_free("tensor", id(psm2))
    P.do("tensor", lambda e: e.matmul(psm2[0:1, 0:32], onescol_f[:], statr[:], start=True, stop=True), inc=s_t)
    P.wait("vector", s_t)
    P.do("vector", lambda e: e.tensor_scalar_mul(murow[0:1, 0:32], psm2[0:1, 0:32], 1.0 / (T * D)), inc=s_v)
    mark_free(id(psm2), s_v)
    P.drain("vector")
    P.do("vector", lambda e: e.tensor_mul(murow[0:1, 32:48], murow[0:1, 0:16], murow[0:1, 0:16]))
    P.drain("vector")
    P.do("vector", lambda e: e.tensor_sub(murow[0:1, 16:32], murow[0:1, 16:32], murow[0:1, 32:48]))
    P.drain("vector")
    P.do("vector", lambda e: e.tensor_scalar(murow[0:1, 16:32], murow[0:1, 16:32], 1.0, 1e-5, OP.mult, OP.add))
    P.drain("vector")
    P.do("vector", lambda e: e.reciprocal(murow[0:1, 16:32], murow[0:1, 16:32]), inc=s_v)
    P.wait("scalar", s_v)
    P.do("scalar", lambda e: e.activation(murow[0:1, 16:32], murow[0:1, 16:32], AF.Sqrt), inc=s_s)
    P.wait("tensor", s_s)
    wait_free("tensor", id(psmall))
    P.do("tensor", lambda e: e.matmul(psmall[:], ones_r[:], murow[0:1, 0:16], start=True, stop=True), inc=s_t)
    P.wait("vector", s_t)
    P.do("vector", lambda e: e.tensor_copy(mubc[:], psmall[:]), inc=s_v)
    P.wait("tensor", s_v)
    P.do("tensor", lambda e: e.matmul(psmall[:], ones_r[:], murow[0:1, 16:32], start=True, stop=True), inc=s_t)
    P.wait("vector", s_t)
    P.do("vector", lambda e: e.tensor_copy(rstdb[:], psmall[:]), inc=s_v)
    mark_free(id(psmall), s_v)
    P.drain("vector")
    for k in range(CS):
        for hf in range(2):
            rows = slice(64 * hf, 64 * (hf + 1))
            hidx = 2 * k + hf
            P.do("vector", lambda e, k=k, rows=rows, hidx=hidx: e.tensor_tensor(
                svcol[rows, k:k + 1], gnc[rows, 0, k:k + 1], rstdb[rows, hidx:hidx + 1], OP.mult))
    P.drain("vector")
    for k in range(CS):
        for hf in range(2):
            rows = slice(64 * hf, 64 * (hf + 1))
            hidx = 2 * k + hf
            P.do("vector", lambda e, k=k, rows=rows, hidx=hidx: e.tensor_tensor(
                shf32[rows, k:k + 1], mubc[rows, hidx:hidx + 1], svcol[rows, k:k + 1], OP.mult))
    P.drain("vector")
    P.do("vector", lambda e: e.tensor_sub(shf32[:], gnc[:, 1, :], shf32[:]))
    P.drain("vector")
    P.do("vector", lambda e: e.tensor_copy(shcol[:], shf32[:]), inc=s_v)
    v_fold = s_v.n

    P.wait("sync", s_v)
    P.do("sync", lambda e: e.dma_start(out=wstage[0:1, 0:C], in_=dr["biases"][0:1, 4 * C:5 * C]), inc=s_ld, k=16)
    P.wait("vector", s_ld)
    P.do("vector", lambda e: e.tensor_copy(brow_o[:], wstage[0:1, 0:C]), inc=s_v)
    for k in range(CS):
        P.wait("sync", s_v)
        P.do("sync", lambda e, k=k: e.dma_start(
            out=wstage[:, 0:C],
            in_=dr["WoT"][:].rearrange("(k p) c -> k p c", p=128)[k]), inc=s_ld, k=16)
        P.wait("vector", s_ld)
        P.do("vector", lambda e, k=k: e.tensor_copy(woT[:, k, :], wstage[:, 0:C]), inc=s_v)
    v_wo = s_v.n
    P.wait("tensor", s_v, v_wo)
    wait_free("tensor", id(prow))
    for nn in range(2):
        for k in range(CS):
            P.do("tensor", lambda e, k=k, nn=nn: e.matmul(
                prow[:, 512 * nn:512 * (nn + 1)], shcol[:, k:k + 1],
                woT[:, k, 512 * nn:512 * (nn + 1)], start=(k == 0), stop=False))
        P.do("tensor", lambda e, nn=nn: e.matmul(
            prow[:, 512 * nn:512 * (nn + 1)], onesb_r[:, 0:1],
            brow_o[0:1, 512 * nn:512 * (nn + 1)],
            start=False, stop=True), inc=(s_t if nn == 1 else None))
    P.wait("scalar", s_t)
    P.do("scalar", lambda e: e.copy(borow[:], prow[:]), inc=s_s)
    mark_free(id(prow), s_s)
    P.wait("vector", s_s)
    for k in range(CS):
        for hf in range(2):
            rows = slice(64 * hf, 64 * (hf + 1))
            P.do("vector", lambda e, k=k, rows=rows: e.tensor_scalar(
                woT[:, k, :][rows, :], woT[:, k, :][rows, :], svcol[rows, k:k + 1], None, OP.mult))
    P.do("vector", lambda e: e.engine_nop(), inc=s_v)
    v_woscaled = s_v.n
    if debug:
        P.wait("gpsimd", s_v, v_woscaled)
        P.do("gpsimd", lambda e: e.dma_start(out=dr["dbg_stats"][:, 0:48], in_=murow[0:1, 0:48]), inc=s_g, k=16)

    P.mark("fold", [s_ld, s_x0, s_x1, s_tr, s_g, s_t, s_v, s_s, s_o])
    nmac = T // 256
    for phase in range(2):
        hof = phase * HIDH
        P.wait("scalar", s_t)
        P.wait("scalar", s_v)
        for wname, wdst, wid, ncols in (("wgT", fgh, 0, HIDH), ("wuT", fuh, 1, HIDH)):
            for k in range(CS):
                P.wait("sync", s_s)
                if phase == 0 and wname == "wgT" and k == 0:
                    P.wait("sync", s_v, v_woscaled if False else v_fold)
                P.do("sync", lambda e, wname=wname, k=k, hof=hof: e.dma_start(
                    out=wstage[:, 0:HIDH].bitcast(I32),
                    in_=dr[wname][:].rearrange("(k p) c -> k p c", p=128)[k][:, hof:hof + HIDH]), inc=s_ld, k=16)
                P.wait("scalar", s_ld)
                P.do("scalar", lambda e, wdst=wdst, k=k: e.copy(wdst[:, k, :], wstage[:, 0:HIDH].bitcast(I32)), inc=s_s)
        for ks in range(HSH):
            P.wait("sync", s_s)
            P.do("sync", lambda e, ks=ks, hof=hof: e.dma_start(
                out=wstage[:, 0:C].bitcast(I32),
                in_=dr["wdT"][:].rearrange("(k p) c -> k p c", p=128)[hof // 128 + ks]), inc=s_ld, k=16)
            P.wait("scalar", s_ld)
            P.do("scalar", lambda e, ks=ks: e.copy(fdh[:, ks, :], wstage[:, 0:C].bitcast(I32)), inc=s_s)
        s_wffn = s_s.n
        P.wait("tensor", s_s, s_wffn)
        if phase == 1:
            P.wait("tensor", s_v, v_woscaled)

        mc_done = 0
        for mc in range(nmac):
            trange = slice(256 * mc, 256 * (mc + 1))
            P.wait("sync", s_s)
            P.wait("sync", s_v, mc_done)
            P.wait("vector", s_g)
            P.wait("vector", s_o)
            P.wait("scalar", s_g)
            for hp in range(2):
                P.do("sync", lambda e, trange=trange, hp=hp: e.dma_start(
                    out=p2xn[64 * hp:64 * (hp + 1), :, :], in_=dr["xnfm_sc"][64 * hp:64 * (hp + 1), :, trange]), inc=s_ld, k=16)
            if phase == 1:
                for hp in range(2):
                    P.do("sync", lambda e, trange=trange, hp=hp: e.dma_start(
                        out=p2wk[64 * hp:64 * (hp + 1), :, :], in_=dr["wkvfm_sc"][64 * hp:64 * (hp + 1), :, trange]), inc=s_ld, k=16)
                for tb in range(2):
                    tsl = slice(256 * mc + 128 * tb, 256 * mc + 128 * (tb + 1))
                    P.do("sync", lambda e, tb=tb, tsl=tsl: e.dma_start(
                        out=p2gs[:, tb, :], in_=dr["gsig_sc"][tsl, :]), inc=s_ld, k=16)
                    P.do("sync", lambda e, tb=tb, tsl=tsl: e.dma_start(
                        out=p2x[:, tb, :], in_=dr["x"][tsl, :]), inc=s_ld, k=16)
                    P.do("sync", lambda e, tb=tb, tsl=tsl: e.dma_start(
                        out=p2dp[:, tb, :], in_=dr["dpart_sc"][tsl, :]), inc=s_ld, k=16)
            ld_mc = s_ld.n
            P.wait("tensor", s_ld, ld_mc)
            for hs in range(HSH):
                wait_free("tensor", id(pg))
                for k in range(CS):
                    P.do("tensor", lambda e, hs=hs, k=k: e.matmul(
                        pg[:], fgh[:, k, 128 * hs:128 * (hs + 1)], p2xn[:, k, :],
                        start=(k == 0), stop=(k == CS - 1)), inc=(s_t if k == CS - 1 else None))
                P.wait("scalar", s_t)
                P.wait("scalar", s_v)
                P.do("scalar", lambda e, hs=hs: e.activation(gatsl[:], pg[:], AF.Silu, scale=S127), inc=s_s)
                mark_free(id(pg), s_s)
                wait_free("tensor", id(pjA))
                for k in range(CS):
                    P.do("tensor", lambda e, hs=hs, k=k: e.matmul(
                        pjA[:, 0:256], fuh[:, k, 128 * hs:128 * (hs + 1)], p2xn[:, k, :],
                        start=(k == 0), stop=(k == CS - 1)), inc=(s_t if k == CS - 1 else None))
                P.wait("vector", s_t)
                P.wait("vector", s_s)
                P.do("vector", lambda e, hs=hs: e.scalar_tensor_tensor(
                    hfm[:, hs, :], pjA[:, 0:256], S127, gatsl[:], OP.mult, OP.mult), inc=s_v)
                mark_free(id(pjA), s_v)
            v_hfm = s_v.n
            P.wait("tensor", s_v, v_hfm)
            for tb in range(2):
                pdn = pjB
                wait_free("tensor", id(pjB))
                for nn in range(2):
                    for ks in range(HSH):
                        P.do("tensor", lambda e, ks=ks, tb=tb, pdn=pdn, nn=nn: e.matmul(
                            pdn[:, 512 * nn:512 * (nn + 1)], hfm[:, ks, 128 * tb:128 * (tb + 1)],
                            fdh[:, ks, 512 * nn:512 * (nn + 1)],
                            start=(ks == 0), stop=(ks == HSH - 1)),
                            inc=(s_t if (ks == HSH - 1 and nn == 1) else None))
                if phase == 0:
                    P.wait("vector", s_t)
                    P.do("vector", lambda e, pdn=pdn, tb=tb: e.tensor_copy(outb[:, tb, :], pdn[:]), inc=s_v)
                    mark_free(id(pjB), s_v)
                    P.wait("gpsimd", s_v)
                    P.do("gpsimd", lambda e, mc=mc, tb=tb: e.dma_start(
                        out=dr["dpart_sc"][256 * mc + 128 * tb:256 * mc + 128 * (tb + 1), :],
                        in_=outb[:, tb, :]), inc=s_g, k=16)
                    mc_done = s_v.n
                else:
                    P.wait("vector", s_t)
                    P.do("vector", lambda e, pdn=pdn, tb=tb: e.tensor_add(outb[:, tb, :], pdn[:], p2dp[:, tb, :]), inc=s_v)
                    mark_free(id(pjB), s_v)
                    P.wait("scalar", s_v)
                    P.do("scalar", lambda e, tb=tb: e.activation(gffn[:], outb[:, tb, :], AF.Silu, scale=S127), inc=s_s)
                    wait_free("tensor", id(pjA))
                    P.wait("tensor", s_v)
                    for nn in range(2):
                        for k in range(CS):
                            P.do("tensor", lambda e, k=k, tb=tb, nn=nn: e.matmul(
                                pjA[:, 512 * nn:512 * (nn + 1)], p2wk[:, k, 128 * tb:128 * (tb + 1)],
                                woT[:, k, 512 * nn:512 * (nn + 1)], start=(k == 0), stop=False))
                        P.do("tensor", lambda e, nn=nn: e.matmul(
                            pjA[:, 512 * nn:512 * (nn + 1)], onesb_r[:],
                            borow[0:1, 512 * nn:512 * (nn + 1)], start=False, stop=True),
                            inc=(s_t if nn == 1 else None))
                    P.wait("vector", s_t)
                    P.do("vector", lambda e, tb=tb: e.tensor_tensor(p2sgb[:], pjA[:], p2gs[:, tb, :], OP.mult), inc=s_v)
                    mark_free(id(pjA), s_v)
                    P.wait("vector", s_s)
                    if debug:
                        P.wait("gpsimd", s_v)
                        P.wait("gpsimd", s_s)
                        tslx = slice(256 * mc + 128 * tb, 256 * mc + 128 * (tb + 1))
                        P.do("gpsimd", lambda e, tslx=tslx: e.dma_start(out=dr["dbg_g"][tslx, :], in_=gffn[:]), inc=s_g, k=16)
                        P.do("gpsimd", lambda e, tslx=tslx, tb=tb: e.dma_start(out=dr["dbg_sg"][tslx, :], in_=p2sgb[:]), inc=s_g, k=16)
                        P.do("gpsimd", lambda e, tslx=tslx, tb=tb: e.dma_start(out=dr["dbg_gsig"][tslx, :], in_=p2gs[:, tb, :]), inc=s_g, k=16)
                        P.wait("vector", s_g)
                    P.drain("vector")
                    P.do("vector", lambda e: e.tensor_mul(t5b[:], p2sgb[:], gffn[:]))
                    P.drain("vector")
                    P.do("vector", lambda e, tb=tb: e.scalar_tensor_tensor(
                        outb[:, tb, :], t5b[:], 1.0, p2x[:, tb, :], OP.mult, OP.add), inc=s_v)
                    P.wait("sync", s_v)
                    P.do("sync", lambda e, mc=mc, tb=tb: e.dma_start(
                        out=dr["out"][256 * mc + 128 * tb:256 * mc + 128 * (tb + 1), :],
                        in_=outb[:, tb, :]), inc=s_o, k=16)
                    mc_done = s_v.n

    P.mark("p2", [s_ld, s_x0, s_x1, s_tr, s_g, s_t, s_v, s_s, s_o])
    if upto is not None:
        if isinstance(upto, int):
            cut = upto
            snap = dict(zip(P.sems, P.snaps[cut - 1])) if cut > 0 else {}
        else:
            cut, snap = P.marks[upto]
        P.recs = P.recs[:cut]
        for s, v in snap.items():
            P.recs.append(("sync", lambda e, s=s, v=v: e.wait_ge(s.h, v) if v > 0 else e.nop()))
    else:
        P.wait("sync", s_o)
        P.wait("sync", s_g)
        P.wait("sync", s_tr)
        P.wait("sync", s_v)
        P.wait("sync", s_s)
        P.wait("sync", s_t)

    es.close()
    with nc.Block() as block:
        @block.sync
        def _(e):
            P.replay(e, "sync")
        @block.vector
        def _(e):
            P.replay(e, "vector")
        @block.scalar
        def _(e):
            P.replay(e, "scalar")
        @block.tensor
        def _(e):
            P.replay(e, "tensor")
        @block.gpsimd
        def _(e):
            P.replay(e, "gpsimd")
    global _last_marks, _last_nrec
    _last_marks = {k: v[0] for k, v in P.marks.items()}
    _last_nrec = len(P.snaps)
    return nc, dr


def host_consts():
    eye128 = np.eye(128, dtype=np.float32)
    s, t = np.meshgrid(np.arange(128), np.arange(128), indexing="ij")
    r32, c32 = np.meshgrid(np.arange(32), np.arange(32), indexing="ij")
    m32 = (r32 < c32).astype(np.float32)
    return {
        "identbf": eye128.astype(NBF),
        "identf": eye128,
        "negibf": (-np.eye(64, dtype=np.float32)).astype(NBF),
        "masksneg": (-(s < t).astype(np.float32)),
        "maskincl": (s <= t).astype(np.float32),
        "mask32r4": np.tile(m32, (1, 4)),
        "iota_r": np.arange(128, dtype=np.float32)[None, :],
        "iota1_r": np.arange(1, 129, dtype=np.float32)[None, :],
        "onesb_r": np.ones((1, 128), np.float32).astype(NBF),
        "onescol_f": np.ones((128, 1), np.float32),
    }


def make_in_map(b, x, W, consts):
    m = dict(consts)
    m["x"] = np.ascontiguousarray(x[b])
    for k, v in W.items():
        m[k] = v
    return m


def prep_weights(inp):
    cT = lambda a: np.ascontiguousarray(np.asarray(a).T)
    W = {
        "WkT": cT(inp["L0_attn_W_k"]), "WvT": cT(inp["L0_attn_W_v"]),
        "WrT": cT(inp["L0_attn_W_r"]), "WgT": cT(inp["L0_attn_W_g"]),
        "WoT": cT(inp["L0_attn_W_o"]),
        "wgT": cT(inp["L0_ffn_w_gate"]), "wuT": cT(inp["L0_ffn_w_up"]),
        "wdT": cT(inp["L0_ffn_w_down"]),
        "biases": np.concatenate([np.asarray(inp[f"L0_attn_b_{p}"]).reshape(-1)
                                  for p in ("k", "v", "r", "g", "o")])[None, :].copy(),
        "decay_eta": np.concatenate([np.asarray(inp["L0_attn_decay"]).reshape(-1),
                                     np.asarray(inp["L0_attn_eta"]).reshape(-1)])[None, :].copy(),
    }
    def cols(vecs):
        a = np.stack([np.asarray(v).reshape(C) for v in vecs])
        n = a.shape[0]
        return np.ascontiguousarray(a.reshape(n, CS, 128).transpose(2, 0, 1).reshape(128, n * CS))
    W["tmcols"] = cols([inp["L0_attn_time_mix_k"], inp["L0_attn_time_mix_v"],
                        inp["L0_attn_time_mix_r"], inp["L0_attn_time_mix_g"]])
    W["gncols"] = cols([inp["L0_attn_gn_weight"], inp["L0_attn_gn_bias"]])
    return W


def kernel(**inputs):
    x = np.asarray(inputs["L0_attn_x"])
    B, T, _ = x.shape
    nc, dr = build_nc(T, debug=False)
    W = prep_weights(inputs)
    consts = host_consts()
    in_maps = [make_in_map(b, x, W, consts) for b in range(B)]
    res = run_bass_kernel_spmd(nc, in_maps, core_ids=list(range(B)))
    return np.stack([np.asarray(r["out"]) for r in res.results], axis=0)


# version 06
# speedup vs baseline: 1.0597x; 1.0597x over previous
import contextlib
import numpy as np
import ml_dtypes

import concourse.bass as bass
import concourse.mybir as mybir
from concourse.ap import AP
from concourse.bass_utils import run_bass_kernel_spmd

F32 = mybir.dt.float32
BF16 = mybir.dt.bfloat16
I32 = mybir.dt.int32
AF = mybir.ActivationFunctionType
OP = mybir.AluOpType
AX = mybir.AxisListType
NBF = ml_dtypes.bfloat16

C = 1024
H = 16
D = 64
HID = 3072
CS = C // 128
HS = HID // 128
HSH = HS // 2
HIDH = HID // 2
L = 128
S127 = float(np.float32(1.0) / np.float32(127.0))


class Sem:
    def __init__(self, es, nc, name):
        self.h = es.enter_context(nc.semaphore(name))
        self.n = 0


class Prog:
    def __init__(self):
        self.recs = []
        self.marks = {}
        self.sems = []
        self.snaps = []

    def _snap(self):
        self.snaps.append(tuple(s.n for s in self.sems))

    def mark(self, name, sems=None):
        self.marks[name] = (len(self.recs), {s: s.n for s in (sems or [])})

    def do(self, eng, fn, inc=None, k=1):
        def run(e, fn=fn, inc=inc, k=k):
            ins = fn(e)
            if inc is not None:
                ins.then_inc(inc.h, k)
        if inc is not None and k == 1 and inc.n > 0:
            self.recs.append((eng, lambda e, s=inc, v=inc.n: e.wait_ge(s.h, v)))
            self._snap()
        self.recs.append((eng, run))
        if inc is not None:
            inc.n += k
        self._snap()

    def wait(self, eng, sem, val=None):
        v = sem.n if val is None else val
        if v > 0:
            self.recs.append((eng, lambda e, s=sem, v=v: e.wait_ge(s.h, v)))
            self._snap()

    def drain(self, eng):
        self.recs.append((eng, lambda e: e.drain()))
        self._snap()

    def replay(self, eng_obj, name):
        for eng, run in self.recs:
            if eng == name:
                run(eng_obj)


def build_nc(T, debug=False, upto=None, solvemode="", phb=""):
    nch = T // L
    assert T % L == 0 and nch % 2 == 0
    nc = bass.Bass("TRN2", target_bir_lowering=False)

    dr = {}
    def din(name, shape, dt=F32):
        dr[name] = nc.dram_tensor(name, shape, dt, kind="ExternalInput").ap(); return dr[name]
    def dout(name, shape, dt=F32):
        dr[name] = nc.dram_tensor(name, shape, dt, kind="ExternalOutput").ap(); return dr[name]
    def dint(name, shape, dt=F32):
        dr[name] = nc.dram_tensor(name, shape, dt, kind="Internal").ap(); return dr[name]

    din("x", [T, C])
    for w in ("WkT", "WvT", "WrT", "WgT", "WoT"):
        din(w, [C, C])
    din("wgT", [C, HID], I32); din("wuT", [C, HID], I32); din("wdT", [HID, C], I32)
    din("biases", [1, 5 * C])
    din("tmcols", [128, 4 * CS])
    din("gncols", [128, 2 * CS])
    din("decay_eta", [1, 32])
    din("identbf", [128, 128], BF16)
    din("identf", [128, 128], F32)
    din("negibf", [64, 64], BF16)
    din("masksneg", [128, 128])
    din("maskincl", [128, 128])
    din("mask32r4", [32, 128])
    din("iota_r", [1, 128])
    din("iota1_r", [1, 128])
    din("onesb_r", [1, 128], BF16)
    din("onescol_f", [128, 1])
    dout("out", [T, C])
    if debug:
        dout("dbg_xn", [T, C], BF16); dout("dbg_k", [T, C], BF16)
        dout("dbg_v", [T, C], BF16); dout("dbg_r", [T, C], BF16)
        dout("dbg_wkv", [T, C], BF16); dout("dbg_beta", [T, H])
        dout("dbg_g", [T, C], BF16); dout("dbg_sg", [T, C], BF16)
        dout("dbg_gsig", [T, C], BF16)
        dout("dbg_stats", [1, 64])
    dint("xnfm_sc", [128, CS, T], BF16)
    dint("gsig_sc", [T, C], BF16)
    dint("wkvfm_sc", [128, CS, T], BF16)
    dint("dpart_sc", [T, C])
    dint("bn1", [2, 64 * 1024], BF16)
    dint("bn2", [2, 64 * 1024], BF16)
    dint("rowb", [2, 16])

    es = contextlib.ExitStack()
    def sb(name, shape, dt=F32):
        return es.enter_context(nc.sbuf_tensor(name, shape, dt))
    def psum(name, shape, dt=F32):
        return es.enter_context(nc.psum_tensor(name, shape, dt))

    WAR = sb("WAR", [128, 45056], BF16)
    def warv(i):
        return WAR[:, i * CS * C:(i + 1) * CS * C].rearrange("p (k c) -> p k c", c=C)
    wkT, wvT, wrT, wgT = (warv(i) for i in range(4))
    woT = WAR[:, 36864:45056].rearrange("p (k c) -> p k c", c=C)
    fgh = WAR[:, 0:12288].rearrange("p (k c) -> p k c", c=HIDH)
    fuh = WAR[:, 12288:24576].rearrange("p (k c) -> p k c", c=HIDH)
    fdh = WAR[:, 24576:24576 + HSH * C].rearrange("p (k c) -> p k c", c=C)

    wstage = sb("wstage", [128, 1536], F32)
    brow4 = WAR[0:1, 32768:36864]
    brow_o = WAR[0:1, 0:C]
    borow = sb("borow", [1, C], BF16)
    tmc = sb("tmc", [128, 4, CS], F32)
    gnc = sb("gnc", [128, 2, CS], F32)
    identbf = sb("sb_identbf", [128, 128], BF16)
    identf = sb("sb_identf", [128, 128], F32)
    negibf = sb("sb_negibf", [64, 64], BF16)
    masksneg = sb("sb_masksneg", [128, 128], F32)
    maskincl = sb("sb_maskincl", [128, 128], F32)
    mask32r4 = sb("sb_mask32r4", [32, 128], F32)
    ones_r = maskincl[0:1, 0:128]
    negones_r = masksneg[0:1, 1:65]
    iota_r = sb("sb_iota_r", [1, 128], F32)
    iota1_r = sb("sb_iota1_r", [1, 128], F32)
    onesb_r = sb("sb_onesb_r", [1, 128], BF16)
    onescol_f = sb("sb_onescol_f", [128, 1], F32)
    de_row = sb("de_row", [1, 32], F32)
    hrow = sb("hrow", [1, 128], F32)
    t_negdelta = sb("t_negdelta", [128, H], F32)
    t_etadelta = sb("t_etadelta", [128, H], F32)
    t_eta_b = sb("t_eta_b", [128, H], F32)
    t_q = sb("t_q", [128, H], F32)
    t_a = sb("t_a", [128, H], F32)
    t_et2 = sb("t_et2", [128, H], F32)
    t_dec2t = sb("t_dec2t", [128, H], F32)
    t_dect1 = sb("t_dect1", [16, 128], F32)
    t_dec2c = sb("t_dec2c", [16, 1], F32)
    etamask = sb("etamask", [32, 2048], BF16)
    etab32 = sb("etab32", [32, 16], F32)

    xbuf = sb("xbuf", [128, 2, C], F32)
    mcol = sb("mcol", [128, 6], F32)
    xntok = sb("xntok", [128, C], BF16)
    xnfm = sb("xnfm", [128, CS, 130], BF16)
    xtr = [sb(f"xtr{k}", [128, 128], BF16) for k in range(CS)]
    scanbig = sb("scanbig", [128, 3072], BF16)
    mixtmp = scanbig[:, 0:1024].rearrange("p (k c) -> p k c", c=128)
    xproj = sb("xproj", [128, 4, CS, 128], BF16)
    uni2 = sb("uni2", [128, 3 * C], F32)
    uni2f = uni2[:].bitcast(BF16)
    ktokp = [uni2f[:, 0:C], uni2f[:, C:2 * C]]
    vtokp = [uni2f[:, 2 * C:3 * C], uni2f[:, 3 * C:4 * C]]
    rtok = uni2f[:, 4 * C:5 * C]
    gsigb = uni2f[:, 5 * C:6 * C]
    sqs = None
    nuss = sb("nuss", [128, 2 * H], F32)
    krfm = sb("krfm", [64, 2, H, 2, 128], BF16)
    mup = sb("mup", [128, 2 * H * 128], BF16)
    gkrm = sb("gkrm", [128, 2 * H * 128], BF16)
    nev = scanbig[0:32, 1024:3072]
    solw = sb("solw", [128, 1024], BF16)
    soltmp = sb("soltmp", [128, 1024], BF16)
    sqs = soltmp[:, 0:C]
    soln = sb("soln", [128, 1024], BF16)
    thtiles = sb("thtiles", [128, 2, 512], BF16)
    dbv = sb("dbv", [128, C], BF16)
    dev = sb("dev", [128, C], BF16)
    ptl = sb("ptl", [128, C], BF16)
    ysb = sb("ysb", [128, D], BF16)
    omb = sb("omb", [128, C], BF16)
    qcol = sb("qcol", [128, H], F32)
    acol = sb("acol", [128, H], F32)
    ttrjunk = sqs
    scol = sb("scol", [128, H], F32)
    sc1 = sb("sc1", [128, H], F32)
    sigT = sb("sigT", [16, 128], F32)
    lamT = sb("lamT", [16, 128], F32)
    recT = sigT
    gamT = lamT
    betT = sigT
    cT = sb("cT", [16, 128], F32)
    lamc = sb("lamc", [16, 2], F32)
    betcol = sb("betcol", [128, H], F32)
    glrow = sb("glrow", [1, 16], F32)
    nbLb = sb("nbLb", [64, H], F32)
    sigma = sb("sigma", [64, H * D], BF16)
    wkvt = xntok
    wkvfmb = scanbig[:, 1024:2048].rearrange("p (k c) -> p k c", c=128)
    uni1 = sb("uni1", [128, H, 16], F32)
    statS = uni1[:, :, 0:8]
    statQ = uni1[:, :, 8:16]
    statacc = sb("statacc", [128, 2 * H], F32)
    stmp = sb("stmp", [128, H], F32)
    statr = sb("statr", [128, 2 * H], F32)
    murow = sb("murow", [1, 64], F32)
    mubc = sb("mubc", [128, H], F32)
    rstdb = sb("rstdb", [128, H], F32)
    shcol = sb("shcol", [128, CS], BF16)
    svcol = sb("svcol", [128, CS], F32)
    shf32 = sb("shf32", [128, CS], F32)
    xprojflat = xproj[:].rearrange("p a k t -> p (a k t)")
    p2xn = xprojflat[:, 0:2048].rearrange("p (k c) -> p k c", c=256)
    p2wk = xprojflat[:, 2048:4096].rearrange("p (k c) -> p k c", c=256)
    p2gs = mup[:].rearrange("p (a c) -> p a c", c=C)
    p2sgb = omb
    p2x = xbuf
    p2dp = gkrm[:].bitcast(F32).rearrange("p (a c) -> p a c", c=C)
    gatsl = omb[:, 0:256]
    hfm = scanbig[:, 0:3072].rearrange("p (k c) -> p k c", c=256)
    gffn = dbv
    t5b = dev
    outb = uni2[:, 0:2 * C].rearrange("p (a c) -> p a c", c=C)

    pjA = psum("pjA", [128, 1024], F32)
    pjB = psum("pjB", [128, 1024], F32)
    pgB = psum("pgB", [128, 512], F32)
    pg = pgB[:, 0:256]
    ppB = psum("ppB", [128, 512], F32)
    pp = ppB[:, 0:64]
    pk2 = psum("pk2", [128, 512], F32)
    pp2 = pk2[:, 0:64]
    pw = pk2[:, 64:128]
    psg = pk2[0:64, 128:192]
    psmall = pk2[:, 192:208]
    psm2 = pk2[0:16, 208:336]
    pbl = pk2[0:64, 336:352]
    ptw = psum("ptw", [128, 640], BF16)
    pt = ptw[0:64, 0:512]
    pwt = ptw[:, 512:640]
    pd = pjB[0:32, :]
    prow = pjA[0:1, :]

    s_ld = Sem(es, nc, "s_ld")
    s_x0 = Sem(es, nc, "s_x0")
    s_x1 = Sem(es, nc, "s_x1")
    s_tr = Sem(es, nc, "s_tr")
    s_g = Sem(es, nc, "s_g")
    s_t = Sem(es, nc, "s_t")
    s_v = Sem(es, nc, "s_v")
    s_s = Sem(es, nc, "s_s")
    s_o = Sem(es, nc, "s_o")

    P = Prog()
    P.sems = [s_ld, s_x0, s_x1, s_tr, s_g, s_t, s_v, s_s, s_o]
    for dst, name in [(identbf, "identbf"), (identf, "identf"), (negibf, "negibf"),
                      (masksneg, "masksneg"), (maskincl, "maskincl"),
                      (mask32r4, "mask32r4"), (iota_r, "iota_r"), (iota1_r, "iota1_r"),
                      (onesb_r, "onesb_r"), (onescol_f, "onescol_f"), (de_row, "decay_eta"),
                      (tmc, "tmcols"), (gnc, "gncols")]:
        P.do("sync", lambda e, d=dst, n=name: e.dma_start(out=d[:], in_=dr[n][:]), inc=s_ld, k=16)
    for bi in range(4):
        P.wait("sync", s_v)
        P.do("sync", lambda e, bi=bi: e.dma_start(out=wstage[0:1, 0:C], in_=dr["biases"][0:1, C * bi:C * (bi + 1)]), inc=s_ld, k=16)
        P.wait("vector", s_ld)
        P.do("vector", lambda e, bi=bi: e.tensor_copy(brow4[0:1, C * bi:C * (bi + 1)], wstage[0:1, 0:C]), inc=s_v)
        P.drain("vector")
    n_const = s_ld.n

    P.wait("vector", s_ld)
    for nm, wdst in [("WkT", wkT), ("WvT", wvT), ("WrT", wrT), ("WgT", wgT)]:
        for k in range(CS):
            P.wait("sync", s_v)
            P.do("sync", lambda e, nm=nm, k=k: e.dma_start(
                out=wstage[:, 0:C],
                in_=dr[nm][:].rearrange("(k p) c -> k p c", p=128)[k]), inc=s_ld, k=16)
            P.wait("vector", s_ld)
            P.do("vector", lambda e, wdst=wdst, k=k: e.tensor_copy(wdst[:, k, :], wstage[:, 0:C]), inc=s_v)

    P.wait("scalar", s_ld, n_const)
    P.do("scalar", lambda e: e.activation(hrow[0:1, 0:32], de_row[0:1, 0:32], AF.Sigmoid))
    P.drain("scalar")
    P.do("scalar", lambda e: e.activation(hrow[0:1, 32:48], hrow[0:1, 0:16], AF.Ln), inc=s_s)
    P.wait("vector", s_s)
    for fn in [
        lambda e: e.reciprocal(hrow[0:1, 48:64], hrow[0:1, 0:16]),
        lambda e: e.tensor_mul(hrow[0:1, 48:64], hrow[0:1, 48:64], hrow[0:1, 16:32]),
        lambda e: e.scalar_tensor_tensor(hrow[0:1, 64:80], hrow[0:1, 0:16], -2.0, hrow[0:1, 16:32], OP.mult, OP.add),
        lambda e: e.tensor_mul(hrow[0:1, 64:80], hrow[0:1, 64:80], hrow[0:1, 16:32]),
        lambda e: e.tensor_sub(hrow[0:1, 80:96], hrow[0:1, 0:16], hrow[0:1, 16:32]),
        lambda e: e.tensor_mul(hrow[0:1, 80:96], hrow[0:1, 80:96], hrow[0:1, 16:32]),
        lambda e: e.tensor_scalar_mul(hrow[0:1, 80:96], hrow[0:1, 80:96], 2.0),
        lambda e: e.reciprocal(hrow[0:1, 96:112], hrow[0:1, 48:64]),
        lambda e: e.tensor_mul(hrow[0:1, 80:96], hrow[0:1, 80:96], hrow[0:1, 96:112]),
        lambda e: e.tensor_mul(hrow[0:1, 96:112], hrow[0:1, 16:32], hrow[0:1, 16:32]),
        lambda e: e.tensor_mul(hrow[0:1, 112:128], hrow[0:1, 0:16], hrow[0:1, 0:16]),
    ]:
        P.do("vector", fn)
        P.drain("vector")
    P.do("vector", lambda e: e.engine_nop(), inc=s_v)
    v_hrow = s_v.n

    def mm_tab(lhs_fn, rhs_fn, out_ap_fn, post_eng, post_fn):
        P.wait("tensor", s_v, v_hrow)
        P.wait("tensor", s_s)
        P.do("tensor", lambda e, l=lhs_fn, r=rhs_fn, o=out_ap_fn: e.matmul(
            o(), l(), r(), start=True, stop=True), inc=s_t)
        P.wait(post_eng, s_t)
        P.do(post_eng, post_fn, inc=s_s)

    mm_tab(lambda: iota_r[:], lambda: hrow[0:1, 32:48], lambda: psmall[:], "scalar",
           lambda e: e.activation(t_negdelta[:], psmall[:], AF.Exp, scale=-1.0))
    P.wait("vector", s_s)
    P.do("vector", lambda e: e.tensor_scalar_mul(t_negdelta[:], t_negdelta[:], -1.0), inc=s_v)
    mm_tab(lambda: ones_r[:], lambda: hrow[0:1, 48:64], lambda: psmall[:], "vector",
           lambda e: e.tensor_copy(t_eta_b[:], psmall[:]))
    P.wait("vector", s_s)
    P.drain("vector")
    P.do("vector", lambda e: e.tensor_mul(t_etadelta[:], t_eta_b[:], t_negdelta[:]))
    P.drain("vector")
    P.do("vector", lambda e: e.tensor_scalar_mul(t_etadelta[:], t_etadelta[:], -1.0), inc=s_v)
    mm_tab(lambda: iota_r[:], lambda: hrow[0:1, 32:48], lambda: psmall[:], "scalar",
           lambda e: e.activation(t_dec2t[:], psmall[:], AF.Exp, scale=2.0))
    mm_tab(lambda: ones_r[:], lambda: hrow[0:1, 64:80], lambda: psmall[:], "vector",
           lambda e: e.tensor_mul(t_q[:], psmall[:], t_dec2t[:]))
    mm_tab(lambda: ones_r[:], lambda: hrow[0:1, 80:96], lambda: psmall[:], "vector",
           lambda e: e.tensor_mul(t_a[:], psmall[:], t_dec2t[:]))
    mm_tab(lambda: ones_r[:], lambda: hrow[0:1, 96:112], lambda: psmall[:], "vector",
           lambda e: e.tensor_copy(t_et2[:], psmall[:]))
    mm_tab(lambda: hrow[0:1, 32:48], lambda: iota1_r[:], lambda: psm2[:], "scalar",
           lambda e: e.activation(t_dect1[:], psm2[:], AF.Exp))
    mm_tab(lambda: hrow[0:1, 112:128], lambda: ones_r[:, 0:1], lambda: psm2[0:16, 0:1], "vector",
           lambda e: e.tensor_copy(t_dec2c[:], psm2[0:16, 0:1]))
    mm_tab(lambda: ones_r[:, 0:32], lambda: hrow[0:1, 48:64], lambda: psmall[0:32, :], "vector",
           lambda e: e.tensor_copy(etab32[:], psmall[0:32, :]))
    P.wait("vector", s_s)
    P.drain("vector")
    for h in range(H):
        P.do("vector", lambda e, h=h: e.scalar_tensor_tensor(
            etamask[:].rearrange("p (j h c) -> p j h c", h=H, c=32)[:, :, h, :],
            mask32r4[:].rearrange("p (j c) -> p j c", c=32),
            etab32[:, h:h + 1],
            mask32r4[:].rearrange("p (j c) -> p j c", c=32),
            OP.mult, OP.mult))
    P.do("vector", lambda e: e.memset(sigma[:], 0.0))
    P.do("vector", lambda e: e.memset(uni1[:], 0.0))
    P.do("vector", lambda e: e.memset(statacc[:], 0.0))
    P.do("vector", lambda e: e.memset(lamc[:], 0.0))
    P.do("vector", lambda e: e.memset(xnfm[:, :, 0:1], 0.0))
    P.do("vector", lambda e: e.engine_nop(), inc=s_v)
    v_init = s_v.n

    P.mark("p0", [s_ld, s_x0, s_x1, s_tr, s_g, s_t, s_v, s_s, s_o])
    BANKS = {
        id(pjA): (0, 1), id(prow): (0, 1),
        id(pjB): (2, 3), id(pd): (2, 3),
        id(pgB): (4,), id(pg): (4,),
        id(ppB): (5,), id(pp): (5,),
        id(pk2): (6,), id(pp2): (6,), id(pw): (6,), id(psg): (6,),
        id(psmall): (6,), id(psm2): (6,), id(pbl): (6,),
        id(ptw): (7,), id(pt): (7,), id(pwt): (7,),
    }
    free = {}
    def mark_free(ps_id, sem):
        for b in BANKS.get(ps_id, (ps_id,)):
            free[b] = (sem, sem.n)
    def wait_free(eng, ps_id):
        for b in BANKS.get(ps_id, (ps_id,)):
            if b in free:
                sem, n = free[b]
                P.wait(eng, sem, n)

    solw_diag = AP(solw, 0, [[1024, 128], [33, 32]])

    xn_done = []
    def phaseA(ch):
        par = ch % 2
        xb = xbuf[:, par, :]
        tile = slice(128 * ch, 128 * (ch + 1))
        P.wait("vector", s_g)
        P.wait("scalar", s_g)

        sx = s_x0 if par == 0 else s_x1
        if ch >= 2:
            P.wait("sync", s_v, xn_done[ch - 2])
        P.do("sync", lambda e, xb=xb, tile=tile: e.dma_start(out=xb[:], in_=dr["x"][tile, :]), inc=sx, k=16)
        P.wait("vector", sx)
        if ch == 0:
            P.wait("vector", s_v, v_init)
        P.do("vector", lambda e, xb=xb: e.tensor_reduce(mcol[:, 0:1], xb[:], AX.X, OP.add))
        P.wait("scalar", sx)
        wait_free("scalar", id(sqs))
        P.do("scalar", lambda e, xb=xb: e.activation(sqs[:].bitcast(F32)[:, 0:512], xb[:, 0:512], AF.Square, accum_out=mcol[:, 2:3]))
        P.drain("scalar")
        P.do("scalar", lambda e, xb=xb: e.activation(sqs[:].bitcast(F32)[:, 0:512], xb[:, 512:1024], AF.Square, accum_out=mcol[:, 3:4]), inc=s_s)
        P.drain("vector")
        P.do("vector", lambda e: e.tensor_scalar_mul(mcol[:, 0:1], mcol[:, 0:1], 1.0 / C))
        P.wait("vector", s_s)
        P.drain("vector")
        P.do("vector", lambda e: e.tensor_mul(mcol[:, 5:6], mcol[:, 0:1], mcol[:, 0:1]))
        P.do("vector", lambda e: e.tensor_add(mcol[:, 4:5], mcol[:, 2:3], mcol[:, 3:4]))
        P.drain("vector")
        P.do("vector", lambda e: e.tensor_scalar_mul(mcol[:, 4:5], mcol[:, 4:5], 1.0 / C))
        P.drain("vector")
        P.do("vector", lambda e: e.tensor_sub(mcol[:, 4:5], mcol[:, 4:5], mcol[:, 5:6]))
        P.drain("vector")
        P.do("vector", lambda e: e.tensor_scalar(mcol[:, 4:5], mcol[:, 4:5], 1.0, 1e-5, OP.mult, OP.add))
        P.drain("vector")
        P.do("vector", lambda e: e.reciprocal(mcol[:, 5:6], mcol[:, 4:5]), inc=s_v)
        P.wait("scalar", s_v)
        P.do("scalar", lambda e: e.activation(mcol[:, 5:6], mcol[:, 5:6], AF.Sqrt), inc=s_s)
        P.wait("vector", s_s)
        P.do("vector", lambda e, xb=xb: e.tensor_scalar(xntok[:], xb[:], mcol[:, 0:1], mcol[:, 5:6], OP.subtract, OP.mult), inc=s_v)
        v_xn = s_v.n
        xn_done.append(v_xn)
        if debug:
            P.wait("gpsimd", s_v, v_xn)
            P.do("gpsimd", lambda e, tile=tile: e.dma_start(out=dr["dbg_xn"][tile, :], in_=xntok[:]), inc=s_g, k=16)

        P.wait("scalar", s_v, v_xn)
        for k in range(CS):
            P.do("scalar", lambda e, k=k: e.dma_start(
                out=xtr[k][:], in_=xntok[:, 128 * k:128 * (k + 1)], transpose=True), inc=s_tr, k=16)
        tr_xn = s_tr.n
        P.wait("scalar", s_tr, tr_xn)
        P.wait("scalar", s_g)
        for k in range(CS):
            P.do("scalar", lambda e, k=k: e.copy(xnfm[:, k, 1:129], xtr[k][:]),
                 inc=(s_s if k == CS - 1 else None))
        s_fm = s_s.n
        P.wait("gpsimd", s_s, s_fm)
        for hp in range(2):
            P.do("gpsimd", lambda e, ch=ch, hp=hp: e.dma_start(
                out=dr["xnfm_sc"][64 * hp:64 * (hp + 1), :, 128 * ch:128 * (ch + 1)],
                in_=xnfm[64 * hp:64 * (hp + 1), :, 1:129]), inc=s_g, k=16)

        P.wait("vector", s_s, s_fm)
        xnv = xnfm[:, :, 1:129]
        xsv = xnfm[:, :, 0:128]
        P.do("vector", lambda e, xnv=xnv, xsv=xsv: e.tensor_sub(mixtmp[:], xnv, xsv))
        P.drain("vector")
        for pi in range(4):
            for k in range(CS):
                P.do("vector", lambda e, pi=pi, k=k, xsv=xsv: e.scalar_tensor_tensor(
                    xproj[:, pi, k, :], mixtmp[:, k, :], tmc[:, pi, k:k + 1], xsv[:, k, :],
                    OP.mult, OP.add))
        P.do("vector", lambda e: e.engine_nop(), inc=s_v)
        v_mix = s_v.n
        P.drain("vector")
        P.do("vector", lambda e, xnv=xnv: e.tensor_copy(xnfm[:, :, 0:1], xnfm[:, :, 128:129]))
        P.do("vector", lambda e: e.engine_nop(), inc=s_v)

        wlist = [wkT, wvT, wrT, wgT]
        def proj(pi, pst, bias_i, evac_fn):
            P.wait("tensor", s_v, v_mix)
            wait_free("tensor", id(pst))
            for nn in range(2):
                for k in range(CS):
                    P.do("tensor", lambda e, pi=pi, k=k, pst=pst, nn=nn: e.matmul(
                        pst[:, 512 * nn:512 * (nn + 1)], xproj[:, pi, k, :],
                        wlist[pi][:, k, 512 * nn:512 * (nn + 1)], start=(k == 0), stop=False))
                P.do("tensor", lambda e, pst=pst, bias_i=bias_i, nn=nn: e.matmul(
                    pst[:, 512 * nn:512 * (nn + 1)], onesb_r[:],
                    brow4[0:1, C * bias_i + 512 * nn:C * bias_i + 512 * (nn + 1)],
                    start=False, stop=True), inc=(s_t if nn == 1 else None))
            evac_fn()

        def ev_k():
            P.wait("vector", s_t)
            P.do("vector", lambda e: e.tensor_copy(ktokp[par][:], pjA[:]), inc=s_v)
            mark_free(id(pjA), s_v)
        def ev_v():
            P.wait("vector", s_t)
            P.do("vector", lambda e: e.tensor_copy(vtokp[par][:], pjB[:]), inc=s_v)
            mark_free(id(pjB), s_v)
        def ev_r():
            P.wait("scalar", s_t)
            P.do("scalar", lambda e: e.activation(rtok[:], pjA[:], AF.Sigmoid), inc=s_s)
            mark_free(id(pjA), s_s)
        def ev_g():
            P.wait("scalar", s_t)
            P.do("scalar", lambda e: e.activation(gsigb[:], pjB[:], AF.Silu), inc=s_s)
            mark_free(id(pjB), s_s)

        proj(0, pjA, 0, ev_k)
        proj(1, pjB, 1, ev_v)
        proj(2, pjA, 2, ev_r)
        proj(3, pjB, 3, ev_g)
        s_rg = s_s.n
        P.wait("gpsimd", s_s, s_rg)
        P.do("gpsimd", lambda e, tile=tile: e.dma_start(out=dr["gsig_sc"][tile, :], in_=gsigb[:]), inc=s_g, k=16)

        P.drain("vector")
        wait_free("vector", id(sqs))
        P.do("vector", lambda e: e.tensor_mul(sqs[:], ktokp[par][:], ktokp[par][:]))
        P.drain("vector")
        P.do("vector", lambda e: e.tensor_reduce(nuss[:, 0:H], sqs[:].rearrange("p (h d) -> p h d", d=D), AX.X, OP.add))
        P.drain("vector")
        P.do("vector", lambda e: e.tensor_mul(sqs[:], vtokp[par][:], vtokp[par][:]))
        P.drain("vector")
        P.do("vector", lambda e: e.tensor_reduce(nuss[:, H:2 * H], sqs[:].rearrange("p (h d) -> p h d", d=D), AX.X, OP.add))
        P.drain("vector")
        mark_free(id(sqs), s_v)
        P.do("vector", lambda e: e.tensor_scalar_max(nuss[:], nuss[:], 1e-24))
        P.drain("vector")
        P.do("vector", lambda e: e.reciprocal(nuss[:], nuss[:]), inc=s_v)
        P.wait("scalar", s_v)
        P.do("scalar", lambda e: e.activation(nuss[:], nuss[:], AF.Sqrt), inc=s_s)
        P.wait("vector", s_s)
        P.do("vector", lambda e: e.tensor_tensor(
            ktokp[par][:].rearrange("p (h d) -> p h d", d=D), ktokp[par][:].rearrange("p (h d) -> p h d", d=D),
            nuss[:, 0:H].unsqueeze(2).to_broadcast([128, H, D]), OP.mult))
        P.do("vector", lambda e: e.tensor_tensor(
            vtokp[par][:].rearrange("p (h d) -> p h d", d=D), vtokp[par][:].rearrange("p (h d) -> p h d", d=D),
            nuss[:, H:2 * H].unsqueeze(2).to_broadcast([128, H, D]), OP.mult), inc=s_v)
        P.drain("vector")
        P.do("vector", lambda e: e.engine_nop(), inc=s_v)
        v_kv = s_v.n
        if debug:
            P.wait("gpsimd", s_v, v_kv)
            for nm, src in [("dbg_k", ktokp[par]), ("dbg_v", vtokp[par]), ("dbg_r", rtok)]:
                P.do("gpsimd", lambda e, tile=tile, nm=nm, src=src: e.dma_start(
                    out=dr[nm][tile, :], in_=src[:]), inc=s_g, k=16)

        P.wait("tensor", s_v, v_kv)
        for hh in range(0, H, 2):
            wait_free("tensor", id(pt))
            for j in range(4):
                h = hh + (j // 2)
                src = ktokp[par] if (j % 2 == 0) else rtok
                P.do("tensor", lambda e, src=src, h=h, j=j: e.transpose(
                    pt[:, 128 * j:128 * (j + 1)], src[:, D * h:D * (h + 1)], identbf[:]),
                    inc=(s_t if j == 3 else None))
            P.wait("scalar", s_t)
            P.do("scalar", lambda e, hh=hh: e.copy(
                krfm[:, par, hh:hh + 2, :, :].rearrange("p a b t -> p (a b t)"), pt[:]), inc=s_s)
            mark_free(id(pt), s_s)
        s_krfm = s_s.n

        P.wait("tensor", s_s, s_krfm)
        for h in range(H):
            wait_free("tensor", id(pg))
            P.do("tensor", lambda e, h=h: e.matmul(
                pg[:], krfm[:, par, h, 0, :], krfm[:, par, h, :, :].rearrange("p a t -> p (a t)"),
                start=True, stop=True), inc=s_t)
            P.wait("vector", s_t)
            P.do("vector", lambda e, h=h: e.scalar_tensor_tensor(
                mup[:, par * 2048 + 128 * h:par * 2048 + 128 * (h + 1)], pg[:, 0:128], t_eta_b[:, h:h + 1],
                masksneg[:], OP.mult, OP.mult))
            P.do("vector", lambda e, h=h: e.scalar_tensor_tensor(
                gkrm[:, par * 2048 + 128 * h:par * 2048 + 128 * (h + 1)], pg[:, 128:256], -1.0,
                maskincl[:], OP.mult, OP.mult), inc=s_v)
            mark_free(id(pg), s_v)
        v_masks = s_v.n

        for half in range(2):
            wait_free("tensor", id(pd))
            for h8 in range(8):
                h = 8 * half + h8
                for j in range(4):
                    last = (h8 == 7 and j == 3)
                    P.do("tensor", lambda e, h=h, h8=h8, j=j: e.matmul(
                        pd[:, 256 * j + 32 * h8:256 * j + 32 * (h8 + 1)],
                        krfm[:, par, h, 0, 32 * j:32 * (j + 1)], krfm[:, par, h, 0, 32 * j:32 * (j + 1)],
                        start=True, stop=True), inc=(s_t if last else None))
            P.wait("vector", s_t)
            P.do("vector", lambda e, half=half: e.tensor_tensor(
                nev[:].rearrange("p (j h c) -> p j h c", h=H, c=32)[:, :, 8 * half:8 * (half + 1), :],
                pd[:].rearrange("p (j h c) -> p j h c", h=8, c=32),
                etamask[:].rearrange("p (j h c) -> p j h c", h=H, c=32)[:, :, 8 * half:8 * (half + 1), :],
                OP.mult), inc=s_v)
            mark_free(id(pd), s_v)
            mark_free(id(pjB), s_v)
        v_nev = s_v.n

        P.wait("gpsimd", s_v, v_nev)
        for rr in range(4):
            P.do("gpsimd", lambda e, par=par, rr=rr: e.dma_start(
                out=dr["bn1"][par].rearrange("(i r c) -> r i c", r=32, c=32)[8 * rr:8 * (rr + 1), :, :],
                in_=nev[8 * rr:8 * (rr + 1), :].rearrange("r (i c) -> r i c", c=32)), inc=s_g, k=16)
        P.wait("gpsimd", s_g)
        P.do("gpsimd", lambda e, par=par: e.dma_start(
            out=soln[64 * par:64 * (par + 1), :],
            in_=dr["bn1"][par].rearrange("(i f) -> i f", i=64)), inc=s_g, k=16)
        g_soln = s_g.n


    def dosolve(ch):
        par = ch % 2
        if par == 1:
            P.wait("vector", s_g)
            P.do("vector", lambda e: e.memset(solw[:], 0.0))
            P.drain("vector")
            if "nodiag" not in solvemode:
                P.do("vector", lambda e: e.memset(solw_diag, 1.0))
            P.drain("vector")
            for r in range(30, -1, -1):
                if "noiter" in solvemode:
                    break
                ns = 31 - r
                P.do("vector", lambda e, r=r, ns=ns: e.tensor_tensor(
                    soltmp[:, 0:ns * ns].rearrange("p (c s) -> p c s", s=ns),
                    solw[:].rearrange("p (s c) -> p c s", c=32)[:, r + 1:32, r + 1:32],
                    soln[:, 32 * r + r + 1:32 * r + 32].unsqueeze(1).to_broadcast([128, ns, ns]),
                    OP.mult))
                P.drain("vector")
                def _red(e, r=r, ns=ns):
                    with nc.allow_low_precision(reason="bf16 triangular-solve accumulate (matmul-grade)"):
                        return e.tensor_reduce(
                            solw[:, 32 * r + r + 1:32 * r + 32],
                            soltmp[:, 0:ns * ns].rearrange("p (c s) -> p c s", s=ns),
                            AX.X, OP.add, negate=True)
                P.do("vector", _red)
                P.drain("vector")
            P.do("vector", lambda e: e.engine_nop(), inc=s_v)
            v_solve = s_v.n
            P.wait("gpsimd", s_t)
            for pr in range(2):
                if "nodma" in solvemode:
                    break
                P.wait("gpsimd", s_v, v_solve)
                for j in range(4):
                    P.do("gpsimd", lambda e, pr=pr, j=j: e.dma_start(
                        out=dr["bn2"][pr].rearrange("(r h c) -> h r c", h=16, c=32)
                            [:, 32 * j:32 * (j + 1), :] if False else
                            dr["bn2"][pr].rearrange("(rr c) -> rr c", c=32)
                            .rearrange("(r h) c -> h r c", h=16)[:, :, :][:, 32 * j:32 * (j + 1), :]
                            if False else
                            AP(dr["bn2"].tensor, pr * 65536 + 512 * 32 * j, [[32, 16], [512, 32], [1, 32]]),
                        in_=solw[64 * pr + 16 * j:64 * pr + 16 * (j + 1), :]
                            .rearrange("h (r c) -> h r c", c=32)), inc=s_g, k=16)
                P.wait("gpsimd", s_g)
                P.do("gpsimd", lambda e, pr=pr: e.dma_start(
                    out=thtiles[:, pr, :],
                    in_=dr["bn2"][pr].rearrange("(r f) -> r f", r=128)), inc=s_g, k=16)
            


    def phaseB(ch):
        par = ch % 2
        tile = slice(128 * ch, 128 * (ch + 1))
        P.wait("vector", s_g)
        P.wait("scalar", s_g)
        skip_qa = "noqa" in phb
        if skip_qa:
            P.do("vector", lambda e: e.memset(omb[:], 0.1))
            P.do("vector", lambda e: e.memset(qcol[:], 0.1))
            P.do("vector", lambda e: e.memset(acol[:], 0.1), inc=s_v)
        v_dv = s_v.n
        if not skip_qa:
            P.do("vector", lambda e: e.tensor_tensor(
                dbv[:].rearrange("p (h d) -> p h d", d=D), vtokp[par][:].rearrange("p (h d) -> p h d", d=D),
                t_negdelta[:].unsqueeze(2).to_broadcast([128, H, D]), OP.mult))
            P.do("vector", lambda e: e.tensor_tensor(
                dev[:].rearrange("p (h d) -> p h d", d=D), vtokp[par][:].rearrange("p (h d) -> p h d", d=D),
                t_etadelta[:].unsqueeze(2).to_broadcast([128, H, D]), OP.mult), inc=s_v)
            v_dv = s_v.n

        P.wait("tensor", s_g)
        P.wait("tensor", s_v, v_dv)
        thp = thtiles[:, par, :]
        for h in range(H if "noptl" not in phb else 0):
            wait_free("tensor", id(pp))
            wait_free("tensor", id(pp2))
            P.do("tensor", lambda e, h=h: e.matmul(
                pp[:], krfm[:, par, h, 0, :], sigma[:, D * h:D * (h + 1)], start=True, stop=False,
                skip_group_check=True))
            for jj in range(4):
                P.do("tensor", lambda e, h=h, jj=jj: e.matmul(
                    pp[32 * jj:32 * (jj + 1), :],
                    mup[:, par * 2048 + 128 * h + 32 * jj:par * 2048 + 128 * h + 32 * (jj + 1)],
                    dbv[:, D * h:D * (h + 1)],
                    start=False, stop=(jj == 0), skip_group_check=True,
                    tile_position=((0, 96) if jj == 3 else None)), inc=(s_t if jj == 3 else None))
            for j in range(4):
                if j > 0:
                    P.wait("tensor", s_s)
                    P.do("tensor", lambda e, h=h, j=j: e.matmul(
                        pp[32 * j:32 * (j + 1), :],
                        mup[0:32 * j, par * 2048 + 128 * h + 32 * j:par * 2048 + 128 * h + 32 * (j + 1)],
                        ptl[0:32 * j, D * h:D * (h + 1)], start=False, stop=True,
                        skip_group_check=True,
                        tile_position=((0, 96) if j == 3 else None)), inc=s_t)
                P.wait("scalar", s_t)
                P.do("scalar", lambda e, j=j: e.copy(ysb[32 * j:32 * (j + 1), :], pp[32 * j:32 * (j + 1), :]), inc=s_s)
                P.wait("tensor", s_s)
                P.do("tensor", lambda e, h=h, j=j, thp=thp: e.matmul(
                    pp2[32 * j:32 * (j + 1), :],
                    thp[32 * j:32 * (j + 1), 32 * h:32 * (h + 1)],
                    ysb[32 * j:32 * (j + 1), :], start=True, stop=True,
                    skip_group_check=True,
                    tile_position=((96, 96) if j == 3 else None)), inc=s_t)
                P.wait("scalar", s_t)
                P.do("scalar", lambda e, h=h, j=j: e.copy(
                    ptl[32 * j:32 * (j + 1), D * h:D * (h + 1)], pp2[32 * j:32 * (j + 1), :]), inc=s_s)
            mark_free(id(pp), s_s)
            mark_free(id(pp2), s_s)
        s_ptl = s_s.n

        P.wait("vector", s_s, s_ptl)
        for h in range(H if not skip_qa else 0):
            P.do("scalar", lambda e, h=h: e.activation(
                ttrjunk[:, D * h:D * (h + 1)], ptl[:, D * h:D * (h + 1)], AF.Square,
                accum_out=qcol[:, h:h + 1]), inc=s_s)
            P.do("vector", lambda e, h=h: e.scalar_tensor_tensor(
                scanbig[:, 1024 + D * h:1024 + D * (h + 1)], ptl[:, D * h:D * (h + 1)], 1.0,
                dev[:, D * h:D * (h + 1)],
                OP.mult, OP.mult, accum_out=acol[:, h:h + 1]))
            P.drain("vector")
        P.drain("vector")
        if not skip_qa:
            P.do("vector", lambda e: e.tensor_tensor(
                omb[:].rearrange("p (h d) -> p h d", d=D), ptl[:].rearrange("p (h d) -> p h d", d=D),
                t_eta_b[:].unsqueeze(2).to_broadcast([128, H, D]), OP.mult))
            P.drain("vector")
            P.do("vector", lambda e: e.tensor_sub(omb[:], omb[:], dev[:]), inc=s_v)
        v_omb = s_v.n
        P.wait("vector", s_s)
        P.do("vector", lambda e: e.tensor_mul(sc1[:], qcol[:], t_q[:]))
        P.drain("vector")
        P.do("vector", lambda e: e.tensor_mul(scol[:], acol[:], t_a[:]))
        P.drain("vector")
        P.do("vector", lambda e: e.tensor_add(scol[:], scol[:], sc1[:]))
        P.drain("vector")
        P.do("vector", lambda e: e.tensor_add(scol[:], scol[:], t_et2[:]), inc=s_v)
        v_scol = s_v.n
        if "noscal" in phb:
            P.do("vector", lambda e: e.memset(betcol[:], 0.5))
            P.do("vector", lambda e: e.memset(nbLb[:], 0.5), inc=s_v)
            skip_scal = True
        else:
            skip_scal = False
        if not skip_scal:
            P.wait("tensor", s_v, v_scol)
            wait_free("tensor", id(psm2))
            P.do("tensor", lambda e: e.transpose(psm2[:], scol[:], identf[:]), inc=s_t)
            P.wait("vector", s_t)
            P.do("vector", lambda e: e.tensor_copy(sigT[:], psm2[:]), inc=s_v)
            mark_free(id(psm2), s_v)
            P.drain("vector")
            P.do("vector", lambda e, par=par: e.tensor_tensor_scan(
                lamT[:], t_dec2c[:].to_broadcast([16, 128]), sigT[:], lamc[:, par:par + 1],
                OP.mult, OP.add))
            P.drain("vector")
            P.do("vector", lambda e: e.tensor_scalar_max(lamT[:], lamT[:], 1e-30))
            P.drain("vector")
            P.do("vector", lambda e, par=par: e.tensor_scalar_min(
                lamc[:, 1 - par:2 - par], lamT[:, 127:128], 256.0))
            P.drain("vector")
            P.do("vector", lambda e: e.reciprocal(recT[:], lamT[:]))
            P.drain("vector")
            P.do("vector", lambda e: e.tensor_scalar(cT[:], recT[:], 256.0, 1.0, OP.mult, OP.min), inc=s_v)
            P.wait("scalar", s_v)
            P.do("scalar", lambda e: e.activation(cT[:], cT[:], AF.Sqrt), inc=s_s)
            P.wait("vector", s_s)
            P.do("vector", lambda e: e.tensor_tensor_scan(
                gamT[:], cT[:], cT[:], 1.0, OP.mult, OP.bypass))
            P.drain("vector")
            P.do("vector", lambda e: e.tensor_mul(betT[:], gamT[:], t_dect1[:]), inc=s_v)
            v_bet = s_v.n
            P.wait("tensor", s_v, v_bet)
            wait_free("tensor", id(psm2))
            P.do("tensor", lambda e: e.transpose(psmall[:], betT[:], identf[0:16, 0:16]), inc=s_t)
            P.wait("vector", s_t)
            P.do("vector", lambda e: e.tensor_copy(betcol[:], psmall[:]), inc=s_v)
            mark_free(id(psmall), s_v)
            v_betc = s_v.n
            if debug:
                P.wait("gpsimd", s_v, v_betc)
                P.do("gpsimd", lambda e, tile=tile: e.dma_start(out=dr["dbg_beta"][tile, :], in_=betcol[:]), inc=s_g, k=16)
            P.wait("gpsimd", s_v, v_bet)
            P.do("gpsimd", lambda e: e.dma_start(out=dr["rowb"][0, :], in_=betT[:, 127:128]), inc=s_g, k=16)
            P.wait("gpsimd", s_g)
            P.do("gpsimd", lambda e: e.dma_start(out=glrow[:], in_=dr["rowb"][0:1, :]), inc=s_g, k=16)
            P.wait("tensor", s_g)
            wait_free("tensor", id(pbl))
            P.do("tensor", lambda e: e.matmul(pbl[:], negones_r[:], glrow[:], start=True, stop=True), inc=s_t)
            P.wait("vector", s_t)
            P.do("vector", lambda e: e.tensor_copy(nbLb[:], pbl[:]), inc=s_v)
            mark_free(id(pbl), s_v)
            v_nbl = s_v.n

        if skip_scal:
            v_betc = s_v.n
            v_nbl = s_v.n
        P.wait("tensor", s_v, v_omb)
        P.wait("tensor", s_v, v_betc)
        P.wait("tensor", s_v, v_nbl)
        acc1 = None if "noacc" in phb else 1
        for hg in range(2 if "nowkv" not in phb else 0):
            wait_free("tensor", id(pk2))
            for hh in range(8):
                h = 8 * hg + hh
                P.do("tensor", lambda e, h=h, hh=hh: e.matmul(
                    pk2[:, 64 * hh:64 * (hh + 1)], krfm[:, par, h, 1, :],
                    sigma[:, D * h:D * (h + 1)], start=True, stop=False))
                P.do("tensor", lambda e, h=h, hh=hh: e.matmul(
                    pk2[:, 64 * hh:64 * (hh + 1)],
                    gkrm[:, par * 2048 + 128 * h:par * 2048 + 128 * (h + 1)],
                    omb[:, D * h:D * (h + 1)],
                    start=False, stop=True), inc=(s_t if hh == 7 else None))
            P.wait("vector", s_t)
            for hh in range(8):
                h = 8 * hg + hh
                P.do("vector", lambda e, h=h, hh=hh, ch=ch, acc1=acc1: e.tensor_scalar(
                    wkvt[:, D * h:D * (h + 1)], pk2[:, 64 * hh:64 * (hh + 1)], betcol[:, h:h + 1],
                    0.0, OP.mult, OP.add,
                    accum_out=(statS[:, h, (ch % 8):(ch % 8) + 1] if acc1 else None)),
                    inc=(s_v if hh == 7 else None))
            mark_free(id(pk2), s_v)
            P.drain("vector")
            for hh in range(8):
                h = 8 * hg + hh
                P.do("vector", lambda e, h=h, ch=ch, acc1=acc1: e.scalar_tensor_tensor(
                    scanbig[:, 1024 + D * h:1024 + D * (h + 1)], wkvt[:, D * h:D * (h + 1)], 1.0,
                    wkvt[:, D * h:D * (h + 1)],
                    OP.mult, OP.mult, accum_out=(statQ[:, h, (ch % 8):(ch % 8) + 1] if acc1 else None)),
                    inc=(s_v if hh == 7 else None))
            if "nosig" in phb:
                continue
            wait_free("tensor", id(pk2))
            for hh in range(8):
                h = 8 * hg + hh
                P.do("tensor", lambda e, h=h, hh=hh: e.matmul(
                    pk2[0:64, 64 * hh:64 * (hh + 1)], negibf[:],
                    sigma[:, D * h:D * (h + 1)], start=True, stop=False))
                P.do("tensor", lambda e, h=h, hh=hh: e.matmul(
                    pk2[0:64, 64 * hh:64 * (hh + 1)], ktokp[par][:, D * h:D * (h + 1)],
                    omb[:, D * h:D * (h + 1)],
                    start=False, stop=True), inc=(s_t if hh == 7 else None))
            P.wait("vector", s_t)
            for hh in range(8):
                h = 8 * hg + hh
                P.do("vector", lambda e, h=h, hh=hh: e.tensor_scalar(
                    sigma[:, D * h:D * (h + 1)], pk2[0:64, 64 * hh:64 * (hh + 1)], nbLb[:, h:h + 1],
                    0.0, OP.mult, OP.add), inc=(s_v if hh == 7 else None))
            mark_free(id(pk2), s_v)
        v_wkv = s_v.n
        if (ch % 8 == 7) or (ch == nch - 1):
            nsl = (ch % 8) + 1
            P.drain("vector")
            P.do("vector", lambda e, nsl=nsl: e.tensor_reduce(stmp[:], statS[:, :, 0:nsl], AX.X, OP.add))
            P.drain("vector")
            P.do("vector", lambda e: e.tensor_add(statacc[:, 0:H], statacc[:, 0:H], stmp[:]))
            P.drain("vector")
            P.do("vector", lambda e, nsl=nsl: e.tensor_reduce(stmp[:], statQ[:, :, 0:nsl], AX.X, OP.add))
            P.drain("vector")
            P.do("vector", lambda e: e.tensor_add(statacc[:, H:2 * H], statacc[:, H:2 * H], stmp[:]), inc=s_v)
        if debug:
            P.wait("gpsimd", s_v, v_wkv)
            P.do("gpsimd", lambda e, tile=tile: e.dma_start(out=dr["dbg_wkv"][tile, :], in_=wkvt[:]), inc=s_g, k=16)

        P.wait("tensor", s_v, v_wkv)
        for hh in range(0, H if ("nowkv" not in phb and "nofmt" not in phb) else 0, 2):
            wait_free("tensor", id(pwt))
            P.do("tensor", lambda e, hh=hh: e.transpose(
                pwt[0:64, :], wkvt[:, D * hh:D * (hh + 1)], identbf[:]))
            P.do("tensor", lambda e, hh=hh: e.transpose(
                pwt[64:128, :], wkvt[:, D * (hh + 1):D * (hh + 2)], identbf[:]), inc=s_t)
            P.wait("vector", s_t)
            P.do("vector", lambda e, hh=hh: e.tensor_copy(wkvfmb[:, hh // 2, :], pwt[:]), inc=s_v)
            mark_free(id(pwt), s_v)
        v_wfm = s_v.n
        P.wait("gpsimd", s_v, v_wfm)
        for hp in range(2 if ("nowkv" not in phb and "nofmt" not in phb) else 0):
            P.do("gpsimd", lambda e, ch=ch, hp=hp: e.dma_start(
                out=dr["wkvfm_sc"][64 * hp:64 * (hp + 1), :, 128 * ch:128 * (ch + 1)],
                in_=wkvfmb[64 * hp:64 * (hp + 1), :, :]), inc=s_g, k=16)


    for pp_ in range(nch // 2):
        phaseA(2 * pp_)
        P.mark(f"A{2 * pp_}", [s_ld, s_x0, s_x1, s_tr, s_g, s_t, s_v, s_s, s_o])
        phaseA(2 * pp_ + 1)
        P.mark(f"A{2 * pp_ + 1}", [s_ld, s_x0, s_x1, s_tr, s_g, s_t, s_v, s_s, s_o])
        dosolve(2 * pp_ + 1)
        P.mark(f"S{pp_}", [s_ld, s_x0, s_x1, s_tr, s_g, s_t, s_v, s_s, s_o])
        phaseB(2 * pp_)
        P.mark(f"B{2 * pp_}", [s_ld, s_x0, s_x1, s_tr, s_g, s_t, s_v, s_s, s_o])
        phaseB(2 * pp_ + 1)
        P.mark(f"pair{pp_}", [s_ld, s_x0, s_x1, s_tr, s_g, s_t, s_v, s_s, s_o])
    P.mark("pass1", [s_ld, s_x0, s_x1, s_tr, s_g, s_t, s_v, s_s, s_o])
    P.drain("vector")
    P.do("vector", lambda e: e.tensor_copy(statr[:], statacc[:]), inc=s_v)
    v_statr = s_v.n
    P.wait("tensor", s_v, v_statr)
    wait_free("tensor", id(psm2))
    P.do("tensor", lambda e: e.matmul(psm2[0:1, 0:32], onescol_f[:], statr[:], start=True, stop=True), inc=s_t)
    P.wait("vector", s_t)
    P.do("vector", lambda e: e.tensor_scalar_mul(murow[0:1, 0:32], psm2[0:1, 0:32], 1.0 / (T * D)), inc=s_v)
    mark_free(id(psm2), s_v)
    P.drain("vector")
    P.do("vector", lambda e: e.tensor_mul(murow[0:1, 32:48], murow[0:1, 0:16], murow[0:1, 0:16]))
    P.drain("vector")
    P.do("vector", lambda e: e.tensor_sub(murow[0:1, 16:32], murow[0:1, 16:32], murow[0:1, 32:48]))
    P.drain("vector")
    P.do("vector", lambda e: e.tensor_scalar(murow[0:1, 16:32], murow[0:1, 16:32], 1.0, 1e-5, OP.mult, OP.add))
    P.drain("vector")
    P.do("vector", lambda e: e.reciprocal(murow[0:1, 16:32], murow[0:1, 16:32]), inc=s_v)
    P.wait("scalar", s_v)
    P.do("scalar", lambda e: e.activation(murow[0:1, 16:32], murow[0:1, 16:32], AF.Sqrt), inc=s_s)
    P.wait("tensor", s_s)
    wait_free("tensor", id(psmall))
    P.do("tensor", lambda e: e.matmul(psmall[:], ones_r[:], murow[0:1, 0:16], start=True, stop=True), inc=s_t)
    P.wait("vector", s_t)
    P.do("vector", lambda e: e.tensor_copy(mubc[:], psmall[:]), inc=s_v)
    P.wait("tensor", s_v)
    P.do("tensor", lambda e: e.matmul(psmall[:], ones_r[:], murow[0:1, 16:32], start=True, stop=True), inc=s_t)
    P.wait("vector", s_t)
    P.do("vector", lambda e: e.tensor_copy(rstdb[:], psmall[:]), inc=s_v)
    mark_free(id(psmall), s_v)
    P.drain("vector")
    for k in range(CS):
        for hf in range(2):
            rows = slice(64 * hf, 64 * (hf + 1))
            hidx = 2 * k + hf
            P.do("vector", lambda e, k=k, rows=rows, hidx=hidx: e.tensor_tensor(
                svcol[rows, k:k + 1], gnc[rows, 0, k:k + 1], rstdb[rows, hidx:hidx + 1], OP.mult))
    P.drain("vector")
    for k in range(CS):
        for hf in range(2):
            rows = slice(64 * hf, 64 * (hf + 1))
            hidx = 2 * k + hf
            P.do("vector", lambda e, k=k, rows=rows, hidx=hidx: e.tensor_tensor(
                shf32[rows, k:k + 1], mubc[rows, hidx:hidx + 1], svcol[rows, k:k + 1], OP.mult))
    P.drain("vector")
    P.do("vector", lambda e: e.tensor_sub(shf32[:], gnc[:, 1, :], shf32[:]))
    P.drain("vector")
    P.do("vector", lambda e: e.tensor_copy(shcol[:], shf32[:]), inc=s_v)
    v_fold = s_v.n

    P.wait("sync", s_v)
    P.do("sync", lambda e: e.dma_start(out=wstage[0:1, 0:C], in_=dr["biases"][0:1, 4 * C:5 * C]), inc=s_ld, k=16)
    P.wait("vector", s_ld)
    P.do("vector", lambda e: e.tensor_copy(brow_o[:], wstage[0:1, 0:C]), inc=s_v)
    for k in range(CS):
        P.wait("sync", s_v)
        P.do("sync", lambda e, k=k: e.dma_start(
            out=wstage[:, 0:C],
            in_=dr["WoT"][:].rearrange("(k p) c -> k p c", p=128)[k]), inc=s_ld, k=16)
        P.wait("vector", s_ld)
        P.do("vector", lambda e, k=k: e.tensor_copy(woT[:, k, :], wstage[:, 0:C]), inc=s_v)
    v_wo = s_v.n
    P.wait("tensor", s_v, v_wo)
    wait_free("tensor", id(prow))
    for nn in range(2):
        for k in range(CS):
            P.do("tensor", lambda e, k=k, nn=nn: e.matmul(
                prow[:, 512 * nn:512 * (nn + 1)], shcol[:, k:k + 1],
                woT[:, k, 512 * nn:512 * (nn + 1)], start=(k == 0), stop=False))
        P.do("tensor", lambda e, nn=nn: e.matmul(
            prow[:, 512 * nn:512 * (nn + 1)], onesb_r[:, 0:1],
            brow_o[0:1, 512 * nn:512 * (nn + 1)],
            start=False, stop=True), inc=(s_t if nn == 1 else None))
    P.wait("scalar", s_t)
    P.do("scalar", lambda e: e.copy(borow[:], prow[:]), inc=s_s)
    mark_free(id(prow), s_s)
    P.wait("vector", s_s)
    for k in range(CS):
        for hf in range(2):
            rows = slice(64 * hf, 64 * (hf + 1))
            P.do("vector", lambda e, k=k, rows=rows: e.tensor_scalar(
                woT[:, k, :][rows, :], woT[:, k, :][rows, :], svcol[rows, k:k + 1], None, OP.mult))
    P.do("vector", lambda e: e.engine_nop(), inc=s_v)
    v_woscaled = s_v.n
    if debug:
        P.wait("gpsimd", s_v, v_woscaled)
        P.do("gpsimd", lambda e: e.dma_start(out=dr["dbg_stats"][:, 0:48], in_=murow[0:1, 0:48]), inc=s_g, k=16)

    P.mark("fold", [s_ld, s_x0, s_x1, s_tr, s_g, s_t, s_v, s_s, s_o])
    nmac = T // 256
    for phase in range(2):
        hof = phase * HIDH
        P.wait("scalar", s_t)
        P.wait("scalar", s_v)
        for wname, wdst, wid, ncols in (("wgT", fgh, 0, HIDH), ("wuT", fuh, 1, HIDH)):
            for k in range(CS):
                P.wait("sync", s_s)
                if phase == 0 and wname == "wgT" and k == 0:
                    P.wait("sync", s_v, v_woscaled if False else v_fold)
                P.do("sync", lambda e, wname=wname, k=k, hof=hof: e.dma_start(
                    out=wstage[:, 0:HIDH].bitcast(I32),
                    in_=dr[wname][:].rearrange("(k p) c -> k p c", p=128)[k][:, hof:hof + HIDH]), inc=s_ld, k=16)
                P.wait("scalar", s_ld)
                P.do("scalar", lambda e, wdst=wdst, k=k: e.copy(wdst[:, k, :], wstage[:, 0:HIDH].bitcast(I32)), inc=s_s)
        for ks in range(HSH):
            P.wait("sync", s_s)
            P.do("sync", lambda e, ks=ks, hof=hof: e.dma_start(
                out=wstage[:, 0:C].bitcast(I32),
                in_=dr["wdT"][:].rearrange("(k p) c -> k p c", p=128)[hof // 128 + ks]), inc=s_ld, k=16)
            P.wait("scalar", s_ld)
            P.do("scalar", lambda e, ks=ks: e.copy(fdh[:, ks, :], wstage[:, 0:C].bitcast(I32)), inc=s_s)
        s_wffn = s_s.n
        P.wait("tensor", s_s, s_wffn)
        if phase == 1:
            P.wait("tensor", s_v, v_woscaled)

        mc_done = 0
        for mc in range(nmac):
            trange = slice(256 * mc, 256 * (mc + 1))
            P.wait("sync", s_s)
            P.wait("sync", s_v, mc_done)
            P.wait("vector", s_g)
            P.wait("vector", s_o)
            P.wait("scalar", s_g)
            for hp in range(2):
                P.do("sync", lambda e, trange=trange, hp=hp: e.dma_start(
                    out=p2xn[64 * hp:64 * (hp + 1), :, :], in_=dr["xnfm_sc"][64 * hp:64 * (hp + 1), :, trange]), inc=s_ld, k=16)
            if phase == 1:
                for hp in range(2):
                    P.do("sync", lambda e, trange=trange, hp=hp: e.dma_start(
                        out=p2wk[64 * hp:64 * (hp + 1), :, :], in_=dr["wkvfm_sc"][64 * hp:64 * (hp + 1), :, trange]), inc=s_ld, k=16)
                for tb in range(2):
                    tsl = slice(256 * mc + 128 * tb, 256 * mc + 128 * (tb + 1))
                    P.do("sync", lambda e, tb=tb, tsl=tsl: e.dma_start(
                        out=p2gs[:, tb, :], in_=dr["gsig_sc"][tsl, :]), inc=s_ld, k=16)
                    P.do("sync", lambda e, tb=tb, tsl=tsl: e.dma_start(
                        out=p2x[:, tb, :], in_=dr["x"][tsl, :]), inc=s_ld, k=16)
                    P.do("sync", lambda e, tb=tb, tsl=tsl: e.dma_start(
                        out=p2dp[:, tb, :], in_=dr["dpart_sc"][tsl, :]), inc=s_ld, k=16)
            ld_mc = s_ld.n
            P.wait("tensor", s_ld, ld_mc)
            for hs in range(HSH):
                wait_free("tensor", id(pg))
                for k in range(CS):
                    P.do("tensor", lambda e, hs=hs, k=k: e.matmul(
                        pg[:], fgh[:, k, 128 * hs:128 * (hs + 1)], p2xn[:, k, :],
                        start=(k == 0), stop=(k == CS - 1)), inc=(s_t if k == CS - 1 else None))
                P.wait("scalar", s_t)
                P.wait("scalar", s_v)
                P.do("scalar", lambda e, hs=hs: e.activation(gatsl[:], pg[:], AF.Silu, scale=S127), inc=s_s)
                mark_free(id(pg), s_s)
                wait_free("tensor", id(pjA))
                for k in range(CS):
                    P.do("tensor", lambda e, hs=hs, k=k: e.matmul(
                        pjA[:, 0:256], fuh[:, k, 128 * hs:128 * (hs + 1)], p2xn[:, k, :],
                        start=(k == 0), stop=(k == CS - 1)), inc=(s_t if k == CS - 1 else None))
                P.wait("vector", s_t)
                P.wait("vector", s_s)
                P.do("vector", lambda e, hs=hs: e.scalar_tensor_tensor(
                    hfm[:, hs, :], pjA[:, 0:256], S127, gatsl[:], OP.mult, OP.mult), inc=s_v)
                mark_free(id(pjA), s_v)
            v_hfm = s_v.n
            P.wait("tensor", s_v, v_hfm)
            for tb in range(2):
                pdn = pjB
                wait_free("tensor", id(pjB))
                for nn in range(2):
                    for ks in range(HSH):
                        P.do("tensor", lambda e, ks=ks, tb=tb, pdn=pdn, nn=nn: e.matmul(
                            pdn[:, 512 * nn:512 * (nn + 1)], hfm[:, ks, 128 * tb:128 * (tb + 1)],
                            fdh[:, ks, 512 * nn:512 * (nn + 1)],
                            start=(ks == 0), stop=(ks == HSH - 1)),
                            inc=(s_t if (ks == HSH - 1 and nn == 1) else None))
                if phase == 0:
                    P.wait("vector", s_t)
                    P.do("vector", lambda e, pdn=pdn, tb=tb: e.tensor_copy(outb[:, tb, :], pdn[:]), inc=s_v)
                    mark_free(id(pjB), s_v)
                    P.wait("gpsimd", s_v)
                    P.do("gpsimd", lambda e, mc=mc, tb=tb: e.dma_start(
                        out=dr["dpart_sc"][256 * mc + 128 * tb:256 * mc + 128 * (tb + 1), :],
                        in_=outb[:, tb, :]), inc=s_g, k=16)
                    mc_done = s_v.n
                else:
                    P.wait("vector", s_t)
                    P.do("vector", lambda e, pdn=pdn, tb=tb: e.tensor_add(outb[:, tb, :], pdn[:], p2dp[:, tb, :]), inc=s_v)
                    mark_free(id(pjB), s_v)
                    P.wait("scalar", s_v)
                    P.do("scalar", lambda e, tb=tb: e.activation(gffn[:], outb[:, tb, :], AF.Silu, scale=S127), inc=s_s)
                    wait_free("tensor", id(pjA))
                    P.wait("tensor", s_v)
                    for nn in range(2):
                        for k in range(CS):
                            P.do("tensor", lambda e, k=k, tb=tb, nn=nn: e.matmul(
                                pjA[:, 512 * nn:512 * (nn + 1)], p2wk[:, k, 128 * tb:128 * (tb + 1)],
                                woT[:, k, 512 * nn:512 * (nn + 1)], start=(k == 0), stop=False))
                        P.do("tensor", lambda e, nn=nn: e.matmul(
                            pjA[:, 512 * nn:512 * (nn + 1)], onesb_r[:],
                            borow[0:1, 512 * nn:512 * (nn + 1)], start=False, stop=True),
                            inc=(s_t if nn == 1 else None))
                    P.wait("vector", s_t)
                    P.do("vector", lambda e, tb=tb: e.tensor_tensor(p2sgb[:], pjA[:], p2gs[:, tb, :], OP.mult), inc=s_v)
                    mark_free(id(pjA), s_v)
                    P.wait("vector", s_s)
                    if debug:
                        P.wait("gpsimd", s_v)
                        P.wait("gpsimd", s_s)
                        tslx = slice(256 * mc + 128 * tb, 256 * mc + 128 * (tb + 1))
                        P.do("gpsimd", lambda e, tslx=tslx: e.dma_start(out=dr["dbg_g"][tslx, :], in_=gffn[:]), inc=s_g, k=16)
                        P.do("gpsimd", lambda e, tslx=tslx, tb=tb: e.dma_start(out=dr["dbg_sg"][tslx, :], in_=p2sgb[:]), inc=s_g, k=16)
                        P.do("gpsimd", lambda e, tslx=tslx, tb=tb: e.dma_start(out=dr["dbg_gsig"][tslx, :], in_=p2gs[:, tb, :]), inc=s_g, k=16)
                        P.wait("vector", s_g)
                    P.drain("vector")
                    P.do("vector", lambda e: e.tensor_mul(t5b[:], p2sgb[:], gffn[:]))
                    P.drain("vector")
                    P.do("vector", lambda e, tb=tb: e.scalar_tensor_tensor(
                        outb[:, tb, :], t5b[:], 1.0, p2x[:, tb, :], OP.mult, OP.add), inc=s_v)
                    P.wait("sync", s_v)
                    P.do("sync", lambda e, mc=mc, tb=tb: e.dma_start(
                        out=dr["out"][256 * mc + 128 * tb:256 * mc + 128 * (tb + 1), :],
                        in_=outb[:, tb, :]), inc=s_o, k=16)
                    mc_done = s_v.n

    P.mark("p2", [s_ld, s_x0, s_x1, s_tr, s_g, s_t, s_v, s_s, s_o])
    if upto is not None:
        if isinstance(upto, int):
            cut = upto
            snap = dict(zip(P.sems, P.snaps[cut - 1])) if cut > 0 else {}
        else:
            cut, snap = P.marks[upto]
        P.recs = P.recs[:cut]
        for s, v in snap.items():
            P.recs.append(("sync", lambda e, s=s, v=v: e.wait_ge(s.h, v) if v > 0 else e.nop()))
    else:
        P.wait("sync", s_o)
        P.wait("sync", s_g)
        P.wait("sync", s_tr)
        P.wait("sync", s_v)
        P.wait("sync", s_s)
        P.wait("sync", s_t)

    es.close()
    with nc.Block() as block:
        @block.sync
        def _(e):
            P.replay(e, "sync")
        @block.vector
        def _(e):
            P.replay(e, "vector")
        @block.scalar
        def _(e):
            P.replay(e, "scalar")
        @block.tensor
        def _(e):
            P.replay(e, "tensor")
        @block.gpsimd
        def _(e):
            P.replay(e, "gpsimd")
    global _last_marks, _last_nrec
    _last_marks = {k: v[0] for k, v in P.marks.items()}
    _last_nrec = len(P.snaps)
    return nc, dr


def host_consts():
    eye128 = np.eye(128, dtype=np.float32)
    s, t = np.meshgrid(np.arange(128), np.arange(128), indexing="ij")
    r32, c32 = np.meshgrid(np.arange(32), np.arange(32), indexing="ij")
    m32 = (r32 < c32).astype(np.float32)
    return {
        "identbf": eye128.astype(NBF),
        "identf": eye128,
        "negibf": (-np.eye(64, dtype=np.float32)).astype(NBF),
        "masksneg": (-(s < t).astype(np.float32)),
        "maskincl": (s <= t).astype(np.float32),
        "mask32r4": np.tile(m32, (1, 4)),
        "iota_r": np.arange(128, dtype=np.float32)[None, :],
        "iota1_r": np.arange(1, 129, dtype=np.float32)[None, :],
        "onesb_r": np.ones((1, 128), np.float32).astype(NBF),
        "onescol_f": np.ones((128, 1), np.float32),
    }


def make_in_map(b, x, W, consts):
    m = dict(consts)
    m["x"] = np.ascontiguousarray(x[b])
    for k, v in W.items():
        m[k] = v
    return m


def prep_weights(inp):
    cT = lambda a: np.ascontiguousarray(np.asarray(a).T)
    W = {
        "WkT": cT(inp["L0_attn_W_k"]), "WvT": cT(inp["L0_attn_W_v"]),
        "WrT": cT(inp["L0_attn_W_r"]), "WgT": cT(inp["L0_attn_W_g"]),
        "WoT": cT(inp["L0_attn_W_o"]),
        "wgT": cT(inp["L0_ffn_w_gate"]), "wuT": cT(inp["L0_ffn_w_up"]),
        "wdT": cT(inp["L0_ffn_w_down"]),
        "biases": np.concatenate([np.asarray(inp[f"L0_attn_b_{p}"]).reshape(-1)
                                  for p in ("k", "v", "r", "g", "o")])[None, :].copy(),
        "decay_eta": np.concatenate([np.asarray(inp["L0_attn_decay"]).reshape(-1),
                                     np.asarray(inp["L0_attn_eta"]).reshape(-1)])[None, :].copy(),
    }
    def cols(vecs):
        a = np.stack([np.asarray(v).reshape(C) for v in vecs])
        n = a.shape[0]
        return np.ascontiguousarray(a.reshape(n, CS, 128).transpose(2, 0, 1).reshape(128, n * CS))
    W["tmcols"] = cols([inp["L0_attn_time_mix_k"], inp["L0_attn_time_mix_v"],
                        inp["L0_attn_time_mix_r"], inp["L0_attn_time_mix_g"]])
    W["gncols"] = cols([inp["L0_attn_gn_weight"], inp["L0_attn_gn_bias"]])
    return W


def kernel(**inputs):
    x = np.asarray(inputs["L0_attn_x"])
    B, T, _ = x.shape
    nc, dr = build_nc(T, debug=False)
    W = prep_weights(inputs)
    consts = host_consts()
    in_maps = [make_in_map(b, x, W, consts) for b in range(B)]
    res = run_bass_kernel_spmd(nc, in_maps, core_ids=list(range(B)))
    return np.stack([np.asarray(r["out"]) for r in res.results], axis=0)


# version 07 (final)
# speedup vs baseline: 1.0750x; 1.0145x over previous
import contextlib
import numpy as np
import ml_dtypes

import concourse.bass as bass
import concourse.mybir as mybir
from concourse.ap import AP
from concourse.bass_utils import run_bass_kernel_spmd

F32 = mybir.dt.float32
BF16 = mybir.dt.bfloat16
I32 = mybir.dt.int32
AF = mybir.ActivationFunctionType
OP = mybir.AluOpType
AX = mybir.AxisListType
NBF = ml_dtypes.bfloat16

C = 1024
H = 16
D = 64
HID = 3072
CS = C // 128
HS = HID // 128
HSH = HS // 2
HIDH = HID // 2
L = 128
S127 = float(np.float32(1.0) / np.float32(127.0))


class Sem:
    def __init__(self, es, nc, name):
        self.h = es.enter_context(nc.semaphore(name))
        self.n = 0


class Prog:
    def __init__(self):
        self.recs = []
        self.marks = {}
        self.sems = []
        self.snaps = []

    def _snap(self):
        self.snaps.append(tuple(s.n for s in self.sems))

    def mark(self, name, sems=None):
        self.marks[name] = (len(self.recs), {s: s.n for s in (sems or [])})

    def do(self, eng, fn, inc=None, k=1):
        def run(e, fn=fn, inc=inc, k=k):
            ins = fn(e)
            if inc is not None:
                ins.then_inc(inc.h, k)
        if inc is not None and k == 1 and inc.n > 0:
            self.recs.append((eng, lambda e, s=inc, v=inc.n: e.wait_ge(s.h, v)))
            self._snap()
        self.recs.append((eng, run))
        if inc is not None:
            inc.n += k
        self._snap()

    def wait(self, eng, sem, val=None):
        v = sem.n if val is None else val
        if v > 0:
            self.recs.append((eng, lambda e, s=sem, v=v: e.wait_ge(s.h, v)))
            self._snap()

    def drain(self, eng):
        self.recs.append((eng, lambda e: e.drain()))
        self._snap()

    def replay(self, eng_obj, name):
        for eng, run in self.recs:
            if eng == name:
                run(eng_obj)


def build_nc(T, debug=False, upto=None, solvemode="", phb=""):
    nch = T // L
    assert T % L == 0 and nch % 2 == 0
    nc = bass.Bass("TRN2", target_bir_lowering=False)

    dr = {}
    def din(name, shape, dt=F32):
        dr[name] = nc.dram_tensor(name, shape, dt, kind="ExternalInput").ap(); return dr[name]
    def dout(name, shape, dt=F32):
        dr[name] = nc.dram_tensor(name, shape, dt, kind="ExternalOutput").ap(); return dr[name]
    def dint(name, shape, dt=F32):
        dr[name] = nc.dram_tensor(name, shape, dt, kind="Internal").ap(); return dr[name]

    din("x", [T, C])
    for w in ("WkT", "WvT", "WrT", "WgT", "WoT"):
        din(w, [C, C])
    din("wgT", [C, HID], I32); din("wuT", [C, HID], I32); din("wdT", [HID, C], I32)
    din("biases", [1, 5 * C])
    din("tmcols", [128, 4 * CS])
    din("gncols", [128, 2 * CS])
    din("decay_eta", [1, 32])
    din("identbf", [128, 128], BF16)
    din("identf", [128, 128], F32)
    din("negibf", [64, 64], BF16)
    din("masksneg", [128, 128])
    din("maskincl", [128, 128])
    din("mask32r4", [32, 128])
    din("iota_r", [1, 128])
    din("iota1_r", [1, 128])
    din("onesb_r", [1, 128], BF16)
    din("onescol_f", [128, 1])
    dout("out", [T, C])
    if debug:
        dout("dbg_xn", [T, C], BF16); dout("dbg_k", [T, C], BF16)
        dout("dbg_v", [T, C], BF16); dout("dbg_r", [T, C], BF16)
        dout("dbg_wkv", [T, C], BF16); dout("dbg_beta", [T, H])
        dout("dbg_g", [T, C], BF16); dout("dbg_sg", [T, C], BF16)
        dout("dbg_gsig", [T, C], BF16)
        dout("dbg_stats", [1, 64])
    dint("xnfm_sc", [128, CS, T], BF16)
    dint("gsig_sc", [T, C], BF16)
    dint("wkvfm_sc", [128, CS, T], BF16)
    dint("dpart_sc", [T, C])
    dint("bn1", [2, 64 * 1024], BF16)
    dint("bn2", [2, 64 * 1024], BF16)
    dint("rowb", [2, 16])

    es = contextlib.ExitStack()
    def sb(name, shape, dt=F32):
        return es.enter_context(nc.sbuf_tensor(name, shape, dt))
    def psum(name, shape, dt=F32):
        return es.enter_context(nc.psum_tensor(name, shape, dt))

    WAR = sb("WAR", [128, 45056], BF16)
    def warv(i):
        return WAR[:, i * CS * C:(i + 1) * CS * C].rearrange("p (k c) -> p k c", c=C)
    wkT, wvT, wrT, wgT = (warv(i) for i in range(4))
    woT = WAR[:, 36864:45056].rearrange("p (k c) -> p k c", c=C)
    fgh = WAR[:, 0:12288].rearrange("p (k c) -> p k c", c=HIDH)
    fuh = WAR[:, 12288:24576].rearrange("p (k c) -> p k c", c=HIDH)
    fdh = WAR[:, 24576:24576 + HSH * C].rearrange("p (k c) -> p k c", c=C)

    wstage = sb("wstage", [128, 1536], F32)
    brow4 = WAR[0:1, 32768:36864]
    brow_o = WAR[0:1, 0:C]
    borow = sb("borow", [1, C], BF16)
    tmc = sb("tmc", [128, 4, CS], F32)
    gnc = sb("gnc", [128, 2, CS], F32)
    identbf = sb("sb_identbf", [128, 128], BF16)
    identf = sb("sb_identf", [128, 128], F32)
    negibf = sb("sb_negibf", [64, 64], BF16)
    masksneg = sb("sb_masksneg", [128, 128], F32)
    maskincl = sb("sb_maskincl", [128, 128], F32)
    mask32r4 = sb("sb_mask32r4", [32, 128], F32)
    ones_r = maskincl[0:1, 0:128]
    negones_r = masksneg[0:1, 1:65]
    iota_r = sb("sb_iota_r", [1, 128], F32)
    iota1_r = sb("sb_iota1_r", [1, 128], F32)
    onesb_r = sb("sb_onesb_r", [1, 128], BF16)
    onescol_f = sb("sb_onescol_f", [128, 1], F32)
    de_row = sb("de_row", [1, 32], F32)
    hrow = sb("hrow", [1, 128], F32)
    t_negdelta = sb("t_negdelta", [128, H], F32)
    t_etadelta = sb("t_etadelta", [128, H], F32)
    t_eta_b = sb("t_eta_b", [128, H], F32)
    t_q = sb("t_q", [128, H], F32)
    t_a = sb("t_a", [128, H], F32)
    t_et2 = sb("t_et2", [128, H], F32)
    t_dec2t = sb("t_dec2t", [128, H], F32)
    t_dect1 = sb("t_dect1", [16, 128], F32)
    t_dec2c = sb("t_dec2c", [16, 1], F32)
    etamask = sb("etamask", [32, 2048], BF16)
    etab32 = sb("etab32", [32, 16], F32)

    xbuf = sb("xbuf", [128, 2, C], F32)
    mcol = sb("mcol", [128, 6], F32)
    xntok = sb("xntok", [128, C], BF16)
    xnfm = sb("xnfm", [128, CS, 130], BF16)
    xtr = [sb(f"xtr{k}", [128, 128], BF16) for k in range(CS)]
    scanbig = sb("scanbig", [128, 3072], BF16)
    mixtmp = scanbig[:, 0:1024].rearrange("p (k c) -> p k c", c=128)
    xproj = sb("xproj", [128, 4, CS, 128], BF16)
    uni2 = sb("uni2", [128, 3 * C], F32)
    uni2f = uni2[:].bitcast(BF16)
    ktokp = [uni2f[:, 0:C], uni2f[:, C:2 * C]]
    vtokp = [uni2f[:, 2 * C:3 * C], uni2f[:, 3 * C:4 * C]]
    rtok = uni2f[:, 4 * C:5 * C]
    gsigb = uni2f[:, 5 * C:6 * C]
    sqs = None
    nuss = sb("nuss", [128, 2 * H], F32)
    krfm = sb("krfm", [64, 2, H, 2, 128], BF16)
    mup = sb("mup", [128, 2 * H * 128], BF16)
    gkrm = sb("gkrm", [128, 2 * H * 128], BF16)
    nev = scanbig[0:32, 1024:3072]
    solw = sb("solw", [128, 1024], BF16)
    soltmp = sb("soltmp", [128, 1024], BF16)
    sqs = soltmp[:, 0:C]
    soln = sb("soln", [128, 1024], BF16)
    thtiles = sb("thtiles", [128, 2, 512], BF16)
    dbv = sb("dbv", [128, C], BF16)
    dev = sb("dev", [128, C], BF16)
    ptl = sb("ptl", [128, C], BF16)
    ysb = sb("ysb", [128, D], BF16)
    omb = sb("omb", [128, C], BF16)
    qcol = sb("qcol", [128, H], F32)
    acol = sb("acol", [128, H], F32)
    ttrjunk = sqs
    scol = sb("scol", [128, H], F32)
    sc1 = sb("sc1", [128, H], F32)
    sigT = sb("sigT", [16, 128], F32)
    lamT = sb("lamT", [16, 128], F32)
    recT = sigT
    gamT = lamT
    betT = sigT
    cT = sb("cT", [16, 128], F32)
    lamc = sb("lamc", [16, 2], F32)
    betcol = sb("betcol", [128, H], F32)
    glrow = sb("glrow", [1, 16], F32)
    nbLb = sb("nbLb", [64, H], F32)
    sigma = sb("sigma", [64, H * D], BF16)
    wkvt = xntok
    wkvfmb = scanbig[:, 1024:2048].rearrange("p (k c) -> p k c", c=128)
    uni1 = sb("uni1", [128, H, 16], F32)
    statS = uni1[:, :, 0:8]
    statQ = uni1[:, :, 8:16]
    statacc = sb("statacc", [128, 2 * H], F32)
    stmp = sb("stmp", [128, H], F32)
    statr = sb("statr", [128, 2 * H], F32)
    murow = sb("murow", [1, 64], F32)
    mubc = sb("mubc", [128, H], F32)
    rstdb = sb("rstdb", [128, H], F32)
    shcol = sb("shcol", [128, CS], BF16)
    svcol = sb("svcol", [128, CS], F32)
    shf32 = sb("shf32", [128, CS], F32)
    xprojflat = xproj[:].rearrange("p a k t -> p (a k t)")
    p2xn = xprojflat[:, 0:2048].rearrange("p (k c) -> p k c", c=256)
    p2wk = xprojflat[:, 2048:4096].rearrange("p (k c) -> p k c", c=256)
    p2gs = mup[:].rearrange("p (a c) -> p a c", c=C)
    p2sgb = omb
    p2x = xbuf
    p2dp = gkrm[:].bitcast(F32).rearrange("p (a c) -> p a c", c=C)
    gatsl = omb[:, 0:256]
    hfm = scanbig[:, 0:3072].rearrange("p (k c) -> p k c", c=256)
    gffn = dbv
    t5b = dev
    outb = uni2[:, 0:2 * C].rearrange("p (a c) -> p a c", c=C)

    pjA = psum("pjA", [128, 1024], F32)
    pjB = psum("pjB", [128, 1024], F32)
    pgB = psum("pgB", [128, 512], F32)
    pg = pgB[:, 0:256]
    ppB = psum("ppB", [128, 512], F32)
    pp = ppB[:, 0:64]
    pk2 = psum("pk2", [128, 512], F32)
    pp2 = pk2[:, 0:64]
    pw = pk2[:, 64:128]
    psg = pk2[0:64, 128:192]
    psmall = pk2[:, 192:208]
    psm2 = pk2[0:16, 208:336]
    pbl = pk2[0:64, 336:352]
    ptw = psum("ptw", [128, 640], BF16)
    pt = ptw[0:64, 0:512]
    pwt = ptw[:, 512:640]
    pd = pjB[0:32, :]
    prow = pjA[0:1, :]

    s_ld = Sem(es, nc, "s_ld")
    s_x0 = Sem(es, nc, "s_x0")
    s_x1 = Sem(es, nc, "s_x1")
    s_tr = Sem(es, nc, "s_tr")
    s_g = Sem(es, nc, "s_g")
    s_t = Sem(es, nc, "s_t")
    s_v = Sem(es, nc, "s_v")
    s_s = Sem(es, nc, "s_s")
    s_o = Sem(es, nc, "s_o")

    P = Prog()
    P.sems = [s_ld, s_x0, s_x1, s_tr, s_g, s_t, s_v, s_s, s_o]
    for dst, name in [(identbf, "identbf"), (identf, "identf"), (negibf, "negibf"),
                      (masksneg, "masksneg"), (maskincl, "maskincl"),
                      (mask32r4, "mask32r4"), (iota_r, "iota_r"), (iota1_r, "iota1_r"),
                      (onesb_r, "onesb_r"), (onescol_f, "onescol_f"), (de_row, "decay_eta"),
                      (tmc, "tmcols"), (gnc, "gncols")]:
        P.do("sync", lambda e, d=dst, n=name: e.dma_start(out=d[:], in_=dr[n][:]), inc=s_ld, k=16)
    for bi in range(4):
        P.wait("sync", s_v)
        P.do("sync", lambda e, bi=bi: e.dma_start(out=wstage[0:1, 0:C], in_=dr["biases"][0:1, C * bi:C * (bi + 1)]), inc=s_ld, k=16)
        P.wait("vector", s_ld)
        P.do("vector", lambda e, bi=bi: e.tensor_copy(brow4[0:1, C * bi:C * (bi + 1)], wstage[0:1, 0:C]), inc=s_v)
        P.drain("vector")
    n_const = s_ld.n

    P.wait("vector", s_ld)
    for nm, wdst in [("WkT", wkT), ("WvT", wvT), ("WrT", wrT), ("WgT", wgT)]:
        for k in range(CS):
            P.wait("sync", s_v)
            P.do("sync", lambda e, nm=nm, k=k: e.dma_start(
                out=wstage[:, 0:C],
                in_=dr[nm][:].rearrange("(k p) c -> k p c", p=128)[k]), inc=s_ld, k=16)
            P.wait("vector", s_ld)
            P.do("vector", lambda e, wdst=wdst, k=k: e.tensor_copy(wdst[:, k, :], wstage[:, 0:C]), inc=s_v)

    P.wait("scalar", s_ld, n_const)
    P.do("scalar", lambda e: e.activation(hrow[0:1, 0:32], de_row[0:1, 0:32], AF.Sigmoid))
    P.drain("scalar")
    P.do("scalar", lambda e: e.activation(hrow[0:1, 32:48], hrow[0:1, 0:16], AF.Ln), inc=s_s)
    P.wait("vector", s_s)
    for fn in [
        lambda e: e.reciprocal(hrow[0:1, 48:64], hrow[0:1, 0:16]),
        lambda e: e.tensor_mul(hrow[0:1, 48:64], hrow[0:1, 48:64], hrow[0:1, 16:32]),
        lambda e: e.scalar_tensor_tensor(hrow[0:1, 64:80], hrow[0:1, 0:16], -2.0, hrow[0:1, 16:32], OP.mult, OP.add),
        lambda e: e.tensor_mul(hrow[0:1, 64:80], hrow[0:1, 64:80], hrow[0:1, 16:32]),
        lambda e: e.tensor_sub(hrow[0:1, 80:96], hrow[0:1, 0:16], hrow[0:1, 16:32]),
        lambda e: e.tensor_mul(hrow[0:1, 80:96], hrow[0:1, 80:96], hrow[0:1, 16:32]),
        lambda e: e.tensor_scalar_mul(hrow[0:1, 80:96], hrow[0:1, 80:96], 2.0),
        lambda e: e.reciprocal(hrow[0:1, 96:112], hrow[0:1, 48:64]),
        lambda e: e.tensor_mul(hrow[0:1, 80:96], hrow[0:1, 80:96], hrow[0:1, 96:112]),
        lambda e: e.tensor_mul(hrow[0:1, 96:112], hrow[0:1, 16:32], hrow[0:1, 16:32]),
        lambda e: e.tensor_mul(hrow[0:1, 112:128], hrow[0:1, 0:16], hrow[0:1, 0:16]),
    ]:
        P.do("vector", fn)
        P.drain("vector")
    P.do("vector", lambda e: e.engine_nop(), inc=s_v)
    v_hrow = s_v.n

    def mm_tab(lhs_fn, rhs_fn, out_ap_fn, post_eng, post_fn):
        P.wait("tensor", s_v, v_hrow)
        P.wait("tensor", s_s)
        P.do("tensor", lambda e, l=lhs_fn, r=rhs_fn, o=out_ap_fn: e.matmul(
            o(), l(), r(), start=True, stop=True), inc=s_t)
        P.wait(post_eng, s_t)
        P.do(post_eng, post_fn, inc=s_s)

    mm_tab(lambda: iota_r[:], lambda: hrow[0:1, 32:48], lambda: psmall[:], "scalar",
           lambda e: e.activation(t_negdelta[:], psmall[:], AF.Exp, scale=-1.0))
    P.wait("vector", s_s)
    P.do("vector", lambda e: e.tensor_scalar_mul(t_negdelta[:], t_negdelta[:], -1.0), inc=s_v)
    mm_tab(lambda: ones_r[:], lambda: hrow[0:1, 48:64], lambda: psmall[:], "vector",
           lambda e: e.tensor_copy(t_eta_b[:], psmall[:]))
    P.wait("vector", s_s)
    P.drain("vector")
    P.do("vector", lambda e: e.tensor_mul(t_etadelta[:], t_eta_b[:], t_negdelta[:]))
    P.drain("vector")
    P.do("vector", lambda e: e.tensor_scalar_mul(t_etadelta[:], t_etadelta[:], -1.0), inc=s_v)
    mm_tab(lambda: iota_r[:], lambda: hrow[0:1, 32:48], lambda: psmall[:], "scalar",
           lambda e: e.activation(t_dec2t[:], psmall[:], AF.Exp, scale=2.0))
    mm_tab(lambda: ones_r[:], lambda: hrow[0:1, 64:80], lambda: psmall[:], "vector",
           lambda e: e.tensor_mul(t_q[:], psmall[:], t_dec2t[:]))
    mm_tab(lambda: ones_r[:], lambda: hrow[0:1, 80:96], lambda: psmall[:], "vector",
           lambda e: e.tensor_mul(t_a[:], psmall[:], t_dec2t[:]))
    mm_tab(lambda: ones_r[:], lambda: hrow[0:1, 96:112], lambda: psmall[:], "vector",
           lambda e: e.tensor_copy(t_et2[:], psmall[:]))
    mm_tab(lambda: hrow[0:1, 32:48], lambda: iota1_r[:], lambda: psm2[:], "scalar",
           lambda e: e.activation(t_dect1[:], psm2[:], AF.Exp))
    mm_tab(lambda: hrow[0:1, 112:128], lambda: ones_r[:, 0:1], lambda: psm2[0:16, 0:1], "vector",
           lambda e: e.tensor_copy(t_dec2c[:], psm2[0:16, 0:1]))
    mm_tab(lambda: ones_r[:, 0:32], lambda: hrow[0:1, 48:64], lambda: psmall[0:32, :], "vector",
           lambda e: e.tensor_copy(etab32[:], psmall[0:32, :]))
    P.wait("vector", s_s)
    P.drain("vector")
    for h in range(H):
        P.do("vector", lambda e, h=h: e.scalar_tensor_tensor(
            etamask[:].rearrange("p (j h c) -> p j h c", h=H, c=32)[:, :, h, :],
            mask32r4[:].rearrange("p (j c) -> p j c", c=32),
            etab32[:, h:h + 1],
            mask32r4[:].rearrange("p (j c) -> p j c", c=32),
            OP.mult, OP.mult))
    P.do("vector", lambda e: e.memset(sigma[:], 0.0))
    P.do("vector", lambda e: e.memset(uni1[:], 0.0))
    P.do("vector", lambda e: e.memset(statacc[:], 0.0))
    P.do("vector", lambda e: e.memset(lamc[:], 0.0))
    P.do("vector", lambda e: e.memset(xnfm[:, :, 0:1], 0.0))
    P.do("vector", lambda e: e.engine_nop(), inc=s_v)
    v_init = s_v.n

    P.mark("p0", [s_ld, s_x0, s_x1, s_tr, s_g, s_t, s_v, s_s, s_o])
    BANKS = {
        id(pjA): (0, 1), id(prow): (0, 1),
        id(pjB): (2, 3), id(pd): (2, 3),
        id(pgB): (4,), id(pg): (4,),
        id(ppB): (5,), id(pp): (5,),
        id(pk2): (6,), id(pp2): (6,), id(pw): (6,), id(psg): (6,),
        id(psmall): (6,), id(psm2): (6,), id(pbl): (6,),
        id(ptw): (7,), id(pt): (7,), id(pwt): (7,),
    }
    free = {}
    def mark_free(ps_id, sem):
        for b in BANKS.get(ps_id, (ps_id,)):
            free[b] = (sem, sem.n)
    def wait_free(eng, ps_id):
        for b in BANKS.get(ps_id, (ps_id,)):
            if b in free:
                sem, n = free[b]
                P.wait(eng, sem, n)

    solw_diag = AP(solw, 0, [[1024, 128], [33, 32]])

    xn_done = []
    def phaseA(ch):
        par = ch % 2
        xb = xbuf[:, par, :]
        tile = slice(128 * ch, 128 * (ch + 1))
        P.wait("vector", s_g)
        P.wait("scalar", s_g)

        sx = s_x0 if par == 0 else s_x1
        if ch >= 2:
            P.wait("sync", s_v, xn_done[ch - 2])
        P.do("sync", lambda e, xb=xb, tile=tile: e.dma_start(out=xb[:], in_=dr["x"][tile, :]), inc=sx, k=16)
        P.wait("vector", sx)
        if ch == 0:
            P.wait("vector", s_v, v_init)
        P.do("vector", lambda e, xb=xb: e.tensor_reduce(mcol[:, 0:1], xb[:], AX.X, OP.add))
        P.wait("scalar", sx)
        wait_free("scalar", id(sqs))
        P.do("scalar", lambda e, xb=xb: e.activation(sqs[:].bitcast(F32)[:, 0:512], xb[:, 0:512], AF.Square, accum_out=mcol[:, 2:3]))
        P.drain("scalar")
        P.do("scalar", lambda e, xb=xb: e.activation(sqs[:].bitcast(F32)[:, 0:512], xb[:, 512:1024], AF.Square, accum_out=mcol[:, 3:4]), inc=s_s)
        P.drain("vector")
        P.do("vector", lambda e: e.tensor_scalar_mul(mcol[:, 0:1], mcol[:, 0:1], 1.0 / C))
        P.wait("vector", s_s)
        P.drain("vector")
        P.do("vector", lambda e: e.tensor_mul(mcol[:, 5:6], mcol[:, 0:1], mcol[:, 0:1]))
        P.do("vector", lambda e: e.tensor_add(mcol[:, 4:5], mcol[:, 2:3], mcol[:, 3:4]))
        P.drain("vector")
        P.do("vector", lambda e: e.tensor_scalar_mul(mcol[:, 4:5], mcol[:, 4:5], 1.0 / C))
        P.drain("vector")
        P.do("vector", lambda e: e.tensor_sub(mcol[:, 4:5], mcol[:, 4:5], mcol[:, 5:6]))
        P.drain("vector")
        P.do("vector", lambda e: e.tensor_scalar(mcol[:, 4:5], mcol[:, 4:5], 1.0, 1e-5, OP.mult, OP.add))
        P.drain("vector")
        P.do("vector", lambda e: e.reciprocal(mcol[:, 5:6], mcol[:, 4:5]), inc=s_v)
        P.wait("scalar", s_v)
        P.do("scalar", lambda e: e.activation(mcol[:, 5:6], mcol[:, 5:6], AF.Sqrt), inc=s_s)
        P.wait("vector", s_s)
        P.do("vector", lambda e, xb=xb: e.tensor_scalar(xntok[:], xb[:], mcol[:, 0:1], mcol[:, 5:6], OP.subtract, OP.mult), inc=s_v)
        v_xn = s_v.n
        xn_done.append(v_xn)
        if debug:
            P.wait("gpsimd", s_v, v_xn)
            P.do("gpsimd", lambda e, tile=tile: e.dma_start(out=dr["dbg_xn"][tile, :], in_=xntok[:]), inc=s_g, k=16)

        P.wait("scalar", s_v, v_xn)
        for k in range(CS):
            P.do("scalar", lambda e, k=k: e.dma_start(
                out=xtr[k][:], in_=xntok[:, 128 * k:128 * (k + 1)], transpose=True), inc=s_tr, k=16)
        tr_xn = s_tr.n
        P.wait("scalar", s_tr, tr_xn)
        P.wait("scalar", s_g)
        for k in range(CS):
            P.do("scalar", lambda e, k=k: e.copy(xnfm[:, k, 1:129], xtr[k][:]),
                 inc=(s_s if k == CS - 1 else None))
        s_fm = s_s.n
        P.wait("gpsimd", s_s, s_fm)
        for hp in range(2):
            P.do("gpsimd", lambda e, ch=ch, hp=hp: e.dma_start(
                out=dr["xnfm_sc"][64 * hp:64 * (hp + 1), :, 128 * ch:128 * (ch + 1)],
                in_=xnfm[64 * hp:64 * (hp + 1), :, 1:129]), inc=s_g, k=16)

        P.wait("vector", s_s, s_fm)
        xnv = xnfm[:, :, 1:129]
        xsv = xnfm[:, :, 0:128]
        P.do("vector", lambda e, xnv=xnv, xsv=xsv: e.tensor_sub(mixtmp[:], xnv, xsv))
        P.drain("vector")
        for pi in range(4):
            for k in range(CS):
                P.do("vector", lambda e, pi=pi, k=k, xsv=xsv: e.scalar_tensor_tensor(
                    xproj[:, pi, k, :], mixtmp[:, k, :], tmc[:, pi, k:k + 1], xsv[:, k, :],
                    OP.mult, OP.add))
        P.do("vector", lambda e: e.engine_nop(), inc=s_v)
        v_mix = s_v.n
        P.drain("vector")
        P.do("vector", lambda e, xnv=xnv: e.tensor_copy(xnfm[:, :, 0:1], xnfm[:, :, 128:129]))
        P.do("vector", lambda e: e.engine_nop(), inc=s_v)

        wlist = [wkT, wvT, wrT, wgT]
        def proj(pi, pst, bias_i, evac_fn):
            P.wait("tensor", s_v, v_mix)
            wait_free("tensor", id(pst))
            for nn in range(2):
                for k in range(CS):
                    P.do("tensor", lambda e, pi=pi, k=k, pst=pst, nn=nn: e.matmul(
                        pst[:, 512 * nn:512 * (nn + 1)], xproj[:, pi, k, :],
                        wlist[pi][:, k, 512 * nn:512 * (nn + 1)], start=(k == 0), stop=False))
                P.do("tensor", lambda e, pst=pst, bias_i=bias_i, nn=nn: e.matmul(
                    pst[:, 512 * nn:512 * (nn + 1)], onesb_r[:],
                    brow4[0:1, C * bias_i + 512 * nn:C * bias_i + 512 * (nn + 1)],
                    start=False, stop=True), inc=(s_t if nn == 1 else None))
            evac_fn()

        def ev_k():
            P.wait("vector", s_t)
            P.do("vector", lambda e: e.tensor_copy(ktokp[par][:], pjA[:]), inc=s_v)
            mark_free(id(pjA), s_v)
        def ev_v():
            P.wait("vector", s_t)
            P.do("vector", lambda e: e.tensor_copy(vtokp[par][:], pjB[:]), inc=s_v)
            mark_free(id(pjB), s_v)
        def ev_r():
            P.wait("scalar", s_t)
            P.do("scalar", lambda e: e.activation(rtok[:], pjA[:], AF.Sigmoid), inc=s_s)
            mark_free(id(pjA), s_s)
        def ev_g():
            P.wait("scalar", s_t)
            P.do("scalar", lambda e: e.activation(gsigb[:], pjB[:], AF.Silu), inc=s_s)
            mark_free(id(pjB), s_s)

        proj(0, pjA, 0, ev_k)
        proj(1, pjB, 1, ev_v)
        proj(2, pjA, 2, ev_r)
        proj(3, pjB, 3, ev_g)
        s_rg = s_s.n
        P.wait("gpsimd", s_s, s_rg)
        P.do("gpsimd", lambda e, tile=tile: e.dma_start(out=dr["gsig_sc"][tile, :], in_=gsigb[:]), inc=s_g, k=16)

        P.drain("vector")
        wait_free("vector", id(sqs))
        P.do("vector", lambda e: e.tensor_mul(sqs[:], ktokp[par][:], ktokp[par][:]))
        P.drain("vector")
        P.do("vector", lambda e: e.tensor_reduce(nuss[:, 0:H], sqs[:].rearrange("p (h d) -> p h d", d=D), AX.X, OP.add))
        P.drain("vector")
        P.do("vector", lambda e: e.tensor_mul(sqs[:], vtokp[par][:], vtokp[par][:]))
        P.drain("vector")
        P.do("vector", lambda e: e.tensor_reduce(nuss[:, H:2 * H], sqs[:].rearrange("p (h d) -> p h d", d=D), AX.X, OP.add))
        P.drain("vector")
        mark_free(id(sqs), s_v)
        P.do("vector", lambda e: e.tensor_scalar_max(nuss[:], nuss[:], 1e-24))
        P.drain("vector")
        P.do("vector", lambda e: e.reciprocal(nuss[:], nuss[:]), inc=s_v)
        P.wait("scalar", s_v)
        P.do("scalar", lambda e: e.activation(nuss[:], nuss[:], AF.Sqrt), inc=s_s)
        P.wait("vector", s_s)
        P.do("vector", lambda e: e.tensor_tensor(
            ktokp[par][:].rearrange("p (h d) -> p h d", d=D), ktokp[par][:].rearrange("p (h d) -> p h d", d=D),
            nuss[:, 0:H].unsqueeze(2).to_broadcast([128, H, D]), OP.mult))
        P.do("vector", lambda e: e.tensor_tensor(
            vtokp[par][:].rearrange("p (h d) -> p h d", d=D), vtokp[par][:].rearrange("p (h d) -> p h d", d=D),
            nuss[:, H:2 * H].unsqueeze(2).to_broadcast([128, H, D]), OP.mult), inc=s_v)
        P.drain("vector")
        P.do("vector", lambda e: e.engine_nop(), inc=s_v)
        v_kv = s_v.n
        if debug:
            P.wait("gpsimd", s_v, v_kv)
            for nm, src in [("dbg_k", ktokp[par]), ("dbg_v", vtokp[par]), ("dbg_r", rtok)]:
                P.do("gpsimd", lambda e, tile=tile, nm=nm, src=src: e.dma_start(
                    out=dr[nm][tile, :], in_=src[:]), inc=s_g, k=16)

        P.wait("tensor", s_v, v_kv)
        for hh in range(0, H, 2):
            wait_free("tensor", id(pt))
            for j in range(4):
                h = hh + (j // 2)
                src = ktokp[par] if (j % 2 == 0) else rtok
                P.do("tensor", lambda e, src=src, h=h, j=j: e.transpose(
                    pt[:, 128 * j:128 * (j + 1)], src[:, D * h:D * (h + 1)], identbf[:]),
                    inc=(s_t if j == 3 else None))
            P.wait("scalar", s_t)
            P.do("scalar", lambda e, hh=hh: e.copy(
                krfm[:, par, hh:hh + 2, :, :].rearrange("p a b t -> p (a b t)"), pt[:]), inc=s_s)
            mark_free(id(pt), s_s)
        s_krfm = s_s.n

        P.wait("tensor", s_s, s_krfm)
        for h in range(H):
            wait_free("tensor", id(pg))
            P.do("tensor", lambda e, h=h: e.matmul(
                pg[:], krfm[:, par, h, 0, :], krfm[:, par, h, :, :].rearrange("p a t -> p (a t)"),
                start=True, stop=True), inc=s_t)
            P.wait("vector", s_t)
            P.do("vector", lambda e, h=h: e.scalar_tensor_tensor(
                mup[:, par * 2048 + 128 * h:par * 2048 + 128 * (h + 1)], pg[:, 0:128], t_eta_b[:, h:h + 1],
                masksneg[:], OP.mult, OP.mult))
            P.do("vector", lambda e, h=h: e.scalar_tensor_tensor(
                gkrm[:, par * 2048 + 128 * h:par * 2048 + 128 * (h + 1)], pg[:, 128:256], -1.0,
                maskincl[:], OP.mult, OP.mult), inc=s_v)
            mark_free(id(pg), s_v)
        v_masks = s_v.n

        for half in range(2):
            wait_free("tensor", id(pd))
            for h8 in range(8):
                h = 8 * half + h8
                for j in range(4):
                    last = (h8 == 7 and j == 3)
                    P.do("tensor", lambda e, h=h, h8=h8, j=j: e.matmul(
                        pd[:, 256 * j + 32 * h8:256 * j + 32 * (h8 + 1)],
                        krfm[:, par, h, 0, 32 * j:32 * (j + 1)], krfm[:, par, h, 0, 32 * j:32 * (j + 1)],
                        start=True, stop=True), inc=(s_t if last else None))
            P.wait("vector", s_t)
            P.do("vector", lambda e, half=half: e.tensor_tensor(
                nev[:].rearrange("p (j h c) -> p j h c", h=H, c=32)[:, :, 8 * half:8 * (half + 1), :],
                pd[:].rearrange("p (j h c) -> p j h c", h=8, c=32),
                etamask[:].rearrange("p (j h c) -> p j h c", h=H, c=32)[:, :, 8 * half:8 * (half + 1), :],
                OP.mult), inc=s_v)
            mark_free(id(pd), s_v)
            mark_free(id(pjB), s_v)
        v_nev = s_v.n

        P.wait("gpsimd", s_v, v_nev)
        for rr in range(4):
            P.do("gpsimd", lambda e, par=par, rr=rr: e.dma_start(
                out=dr["bn1"][par].rearrange("(i r c) -> r i c", r=32, c=32)[8 * rr:8 * (rr + 1), :, :],
                in_=nev[8 * rr:8 * (rr + 1), :].rearrange("r (i c) -> r i c", c=32)), inc=s_g, k=16)
        P.wait("gpsimd", s_g)
        P.do("gpsimd", lambda e, par=par: e.dma_start(
            out=soln[64 * par:64 * (par + 1), :],
            in_=dr["bn1"][par].rearrange("(i f) -> i f", i=64)), inc=s_g, k=16)
        g_soln = s_g.n


    def dosolve(ch):
        par = ch % 2
        if par == 1:
            P.wait("vector", s_g)
            P.do("vector", lambda e: e.memset(solw[:], 0.0))
            P.drain("vector")
            if "nodiag" not in solvemode:
                P.do("vector", lambda e: e.memset(solw_diag, 1.0))
            P.drain("vector")
            for r in range(30, -1, -1):
                if "noiter" in solvemode:
                    break
                ns = 31 - r
                P.do("vector", lambda e, r=r, ns=ns: e.tensor_tensor(
                    soltmp[:, 0:ns * ns].rearrange("p (c s) -> p c s", s=ns),
                    solw[:].rearrange("p (s c) -> p c s", c=32)[:, r + 1:32, r + 1:32],
                    soln[:, 32 * r + r + 1:32 * r + 32].unsqueeze(1).to_broadcast([128, ns, ns]),
                    OP.mult))
                P.drain("vector")
                def _red(e, r=r, ns=ns):
                    with nc.allow_low_precision(reason="bf16 triangular-solve accumulate (matmul-grade)"):
                        return e.tensor_reduce(
                            solw[:, 32 * r + r + 1:32 * r + 32],
                            soltmp[:, 0:ns * ns].rearrange("p (c s) -> p c s", s=ns),
                            AX.X, OP.add, negate=True)
                P.do("vector", _red)
                P.drain("vector")
            P.do("vector", lambda e: e.engine_nop(), inc=s_v)
            v_solve = s_v.n
            P.wait("gpsimd", s_t)
            for pr in range(2):
                if "nodma" in solvemode:
                    break
                P.wait("gpsimd", s_v, v_solve)
                for j in range(4):
                    P.do("gpsimd", lambda e, pr=pr, j=j: e.dma_start(
                        out=dr["bn2"][pr].rearrange("(r h c) -> h r c", h=16, c=32)
                            [:, 32 * j:32 * (j + 1), :] if False else
                            dr["bn2"][pr].rearrange("(rr c) -> rr c", c=32)
                            .rearrange("(r h) c -> h r c", h=16)[:, :, :][:, 32 * j:32 * (j + 1), :]
                            if False else
                            AP(dr["bn2"].tensor, pr * 65536 + 512 * 32 * j, [[32, 16], [512, 32], [1, 32]]),
                        in_=solw[64 * pr + 16 * j:64 * pr + 16 * (j + 1), :]
                            .rearrange("h (r c) -> h r c", c=32)), inc=s_g, k=16)
                P.wait("gpsimd", s_g)
                P.do("gpsimd", lambda e, pr=pr: e.dma_start(
                    out=thtiles[:, pr, :],
                    in_=dr["bn2"][pr].rearrange("(r f) -> r f", r=128)), inc=s_g, k=16)
            


    def phaseB(ch):
        par = ch % 2
        tile = slice(128 * ch, 128 * (ch + 1))
        P.wait("vector", s_g)
        P.wait("scalar", s_g)
        skip_qa = "noqa" in phb
        if skip_qa:
            P.do("vector", lambda e: e.memset(omb[:], 0.1))
            P.do("vector", lambda e: e.memset(qcol[:], 0.1))
            P.do("vector", lambda e: e.memset(acol[:], 0.1), inc=s_v)
        v_dv = s_v.n
        if not skip_qa:
            P.do("vector", lambda e: e.tensor_tensor(
                dbv[:].rearrange("p (h d) -> p h d", d=D), vtokp[par][:].rearrange("p (h d) -> p h d", d=D),
                t_negdelta[:].unsqueeze(2).to_broadcast([128, H, D]), OP.mult))
            P.do("vector", lambda e: e.tensor_tensor(
                dev[:].rearrange("p (h d) -> p h d", d=D), vtokp[par][:].rearrange("p (h d) -> p h d", d=D),
                t_etadelta[:].unsqueeze(2).to_broadcast([128, H, D]), OP.mult), inc=s_v)
            v_dv = s_v.n

        P.wait("tensor", s_g)
        P.wait("tensor", s_v, v_dv)
        thp = thtiles[:, par, :]
        for h in range(H if "noptl" not in phb else 0):
            wait_free("tensor", id(pp))
            wait_free("tensor", id(pp2))
            P.do("tensor", lambda e, h=h: e.matmul(
                pp[:], krfm[:, par, h, 0, :], sigma[:, D * h:D * (h + 1)], start=True, stop=False,
                skip_group_check=True))
            for jj in range(4):
                P.do("tensor", lambda e, h=h, jj=jj: e.matmul(
                    pp[32 * jj:32 * (jj + 1), :],
                    mup[:, par * 2048 + 128 * h + 32 * jj:par * 2048 + 128 * h + 32 * (jj + 1)],
                    dbv[:, D * h:D * (h + 1)],
                    start=False, stop=(jj == 0), skip_group_check=True,
                    tile_position=((0, 96) if jj == 3 else None)), inc=(s_t if jj == 3 else None))
            for j in range(4):
                if j > 0:
                    P.wait("tensor", s_s)
                    P.do("tensor", lambda e, h=h, j=j: e.matmul(
                        pp[32 * j:32 * (j + 1), :],
                        mup[0:32 * j, par * 2048 + 128 * h + 32 * j:par * 2048 + 128 * h + 32 * (j + 1)],
                        ptl[0:32 * j, D * h:D * (h + 1)], start=False, stop=True,
                        skip_group_check=True,
                        tile_position=((0, 96) if j == 3 else None)), inc=s_t)
                P.wait("scalar", s_t)
                P.do("scalar", lambda e, j=j: e.copy(ysb[32 * j:32 * (j + 1), :], pp[32 * j:32 * (j + 1), :]), inc=s_s)
                P.wait("tensor", s_s)
                P.do("tensor", lambda e, h=h, j=j, thp=thp: e.matmul(
                    pp2[32 * j:32 * (j + 1), :],
                    thp[32 * j:32 * (j + 1), 32 * h:32 * (h + 1)],
                    ysb[32 * j:32 * (j + 1), :], start=True, stop=True,
                    skip_group_check=True,
                    tile_position=((96, 96) if j == 3 else None)), inc=s_t)
                P.wait("scalar", s_t)
                P.do("scalar", lambda e, h=h, j=j: e.copy(
                    ptl[32 * j:32 * (j + 1), D * h:D * (h + 1)], pp2[32 * j:32 * (j + 1), :]), inc=s_s)
            mark_free(id(pp), s_s)
            mark_free(id(pp2), s_s)
        s_ptl = s_s.n

        P.wait("vector", s_s, s_ptl)
        for h in range(H if not skip_qa else 0):
            P.do("scalar", lambda e, h=h: e.activation(
                ttrjunk[:, D * h:D * (h + 1)], ptl[:, D * h:D * (h + 1)], AF.Square,
                accum_out=qcol[:, h:h + 1]), inc=s_s)
            P.do("vector", lambda e, h=h: e.scalar_tensor_tensor(
                scanbig[:, 1024 + D * h:1024 + D * (h + 1)], ptl[:, D * h:D * (h + 1)], 1.0,
                dev[:, D * h:D * (h + 1)],
                OP.mult, OP.mult, accum_out=acol[:, h:h + 1]))
            P.drain("vector")
        P.drain("vector")
        if not skip_qa:
            P.do("vector", lambda e: e.tensor_tensor(
                omb[:].rearrange("p (h d) -> p h d", d=D), ptl[:].rearrange("p (h d) -> p h d", d=D),
                t_eta_b[:].unsqueeze(2).to_broadcast([128, H, D]), OP.mult))
            P.drain("vector")
            P.do("vector", lambda e: e.tensor_sub(omb[:], omb[:], dev[:]), inc=s_v)
        v_omb = s_v.n
        P.wait("vector", s_s)
        P.do("vector", lambda e: e.tensor_mul(sc1[:], qcol[:], t_q[:]))
        P.drain("vector")
        P.do("vector", lambda e: e.tensor_mul(scol[:], acol[:], t_a[:]))
        P.drain("vector")
        P.do("vector", lambda e: e.tensor_add(scol[:], scol[:], sc1[:]))
        P.drain("vector")
        P.do("vector", lambda e: e.tensor_add(scol[:], scol[:], t_et2[:]), inc=s_v)
        v_scol = s_v.n
        if "noscal" in phb:
            P.do("vector", lambda e: e.memset(betcol[:], 0.5))
            P.do("vector", lambda e: e.memset(nbLb[:], 0.5), inc=s_v)
            skip_scal = True
        else:
            skip_scal = False
        if not skip_scal:
            P.wait("tensor", s_v, v_scol)
            wait_free("tensor", id(psm2))
            P.do("tensor", lambda e: e.transpose(psm2[:], scol[:], identf[:]), inc=s_t)
            P.wait("vector", s_t)
            P.do("vector", lambda e: e.tensor_copy(sigT[:], psm2[:]), inc=s_v)
            mark_free(id(psm2), s_v)
            P.drain("vector")
            P.do("vector", lambda e, par=par: e.tensor_tensor_scan(
                lamT[:], t_dec2c[:].to_broadcast([16, 128]), sigT[:], lamc[:, par:par + 1],
                OP.mult, OP.add))
            P.drain("vector")
            P.do("vector", lambda e: e.tensor_scalar_max(lamT[:], lamT[:], 1e-30))
            P.drain("vector")
            P.do("vector", lambda e, par=par: e.tensor_scalar_min(
                lamc[:, 1 - par:2 - par], lamT[:, 127:128], 256.0))
            P.drain("vector")
            P.do("vector", lambda e: e.reciprocal(recT[:], lamT[:]))
            P.drain("vector")
            P.do("vector", lambda e: e.tensor_scalar(cT[:], recT[:], 256.0, 1.0, OP.mult, OP.min), inc=s_v)
            P.wait("scalar", s_v)
            P.do("scalar", lambda e: e.activation(cT[:], cT[:], AF.Sqrt), inc=s_s)
            P.wait("vector", s_s)
            P.do("vector", lambda e: e.tensor_tensor_scan(
                gamT[:], cT[:], cT[:], 1.0, OP.mult, OP.bypass))
            P.drain("vector")
            P.do("vector", lambda e: e.tensor_mul(betT[:], gamT[:], t_dect1[:]), inc=s_v)
            v_bet = s_v.n
            P.wait("tensor", s_v, v_bet)
            wait_free("tensor", id(psm2))
            P.do("tensor", lambda e: e.transpose(psmall[:], betT[:], identf[0:16, 0:16]), inc=s_t)
            P.wait("vector", s_t)
            P.do("vector", lambda e: e.tensor_copy(betcol[:], psmall[:]), inc=s_v)
            mark_free(id(psmall), s_v)
            v_betc = s_v.n
            if debug:
                P.wait("gpsimd", s_v, v_betc)
                P.do("gpsimd", lambda e, tile=tile: e.dma_start(out=dr["dbg_beta"][tile, :], in_=betcol[:]), inc=s_g, k=16)
            P.wait("gpsimd", s_v, v_bet)
            P.do("gpsimd", lambda e: e.dma_start(out=dr["rowb"][0, :], in_=betT[:, 127:128]), inc=s_g, k=16)
            P.wait("gpsimd", s_g)
            P.do("gpsimd", lambda e: e.dma_start(out=glrow[:], in_=dr["rowb"][0:1, :]), inc=s_g, k=16)
            P.wait("tensor", s_g)
            wait_free("tensor", id(pbl))
            P.do("tensor", lambda e: e.matmul(pbl[:], negones_r[:], glrow[:], start=True, stop=True), inc=s_t)
            P.wait("vector", s_t)
            P.do("vector", lambda e: e.tensor_copy(nbLb[:], pbl[:]), inc=s_v)
            mark_free(id(pbl), s_v)
            v_nbl = s_v.n

        if skip_scal:
            v_betc = s_v.n
            v_nbl = s_v.n
        P.wait("tensor", s_v, v_omb)
        P.wait("tensor", s_v, v_betc)
        P.wait("tensor", s_v, v_nbl)
        acc1 = None if "noacc" in phb else 1
        for hg in range(2 if "nowkv" not in phb else 0):
            wait_free("tensor", id(pk2))
            for hh in range(8):
                h = 8 * hg + hh
                P.do("tensor", lambda e, h=h, hh=hh: e.matmul(
                    pk2[:, 64 * hh:64 * (hh + 1)], krfm[:, par, h, 1, :],
                    sigma[:, D * h:D * (h + 1)], start=True, stop=False))
                P.do("tensor", lambda e, h=h, hh=hh: e.matmul(
                    pk2[:, 64 * hh:64 * (hh + 1)],
                    gkrm[:, par * 2048 + 128 * h:par * 2048 + 128 * (h + 1)],
                    omb[:, D * h:D * (h + 1)],
                    start=False, stop=True), inc=(s_t if hh == 7 else None))
            P.wait("vector", s_t)
            for hh in range(8):
                h = 8 * hg + hh
                P.do("vector", lambda e, h=h, hh=hh, ch=ch, acc1=acc1: e.tensor_scalar(
                    wkvt[:, D * h:D * (h + 1)], pk2[:, 64 * hh:64 * (hh + 1)], betcol[:, h:h + 1],
                    0.0, OP.mult, OP.add,
                    accum_out=(statS[:, h, (ch % 8):(ch % 8) + 1] if acc1 else None)),
                    inc=(s_v if hh == 7 else None))
            mark_free(id(pk2), s_v)
            P.drain("vector")
            for hh in range(8):
                h = 8 * hg + hh
                P.do("vector", lambda e, h=h, ch=ch, acc1=acc1: e.scalar_tensor_tensor(
                    scanbig[:, 1024 + D * h:1024 + D * (h + 1)], wkvt[:, D * h:D * (h + 1)], 1.0,
                    wkvt[:, D * h:D * (h + 1)],
                    OP.mult, OP.mult, accum_out=(statQ[:, h, (ch % 8):(ch % 8) + 1] if acc1 else None)),
                    inc=(s_v if hh == 7 else None))
            if "nosig" in phb:
                continue
            wait_free("tensor", id(pk2))
            for hh in range(8):
                h = 8 * hg + hh
                P.do("tensor", lambda e, h=h, hh=hh: e.matmul(
                    pk2[0:64, 64 * hh:64 * (hh + 1)], negibf[:],
                    sigma[:, D * h:D * (h + 1)], start=True, stop=False))
                P.do("tensor", lambda e, h=h, hh=hh: e.matmul(
                    pk2[0:64, 64 * hh:64 * (hh + 1)], ktokp[par][:, D * h:D * (h + 1)],
                    omb[:, D * h:D * (h + 1)],
                    start=False, stop=True), inc=(s_t if hh == 7 else None))
            P.wait("vector", s_t)
            for hh in range(8):
                h = 8 * hg + hh
                P.do("vector", lambda e, h=h, hh=hh: e.tensor_scalar(
                    sigma[:, D * h:D * (h + 1)], pk2[0:64, 64 * hh:64 * (hh + 1)], nbLb[:, h:h + 1],
                    0.0, OP.mult, OP.add), inc=(s_v if hh == 7 else None))
            mark_free(id(pk2), s_v)
        v_wkv = s_v.n
        if (ch % 8 == 7) or (ch == nch - 1):
            nsl = (ch % 8) + 1
            P.drain("vector")
            P.do("vector", lambda e, nsl=nsl: e.tensor_reduce(stmp[:], statS[:, :, 0:nsl], AX.X, OP.add))
            P.drain("vector")
            P.do("vector", lambda e: e.tensor_add(statacc[:, 0:H], statacc[:, 0:H], stmp[:]))
            P.drain("vector")
            P.do("vector", lambda e, nsl=nsl: e.tensor_reduce(stmp[:], statQ[:, :, 0:nsl], AX.X, OP.add))
            P.drain("vector")
            P.do("vector", lambda e: e.tensor_add(statacc[:, H:2 * H], statacc[:, H:2 * H], stmp[:]), inc=s_v)
        if debug:
            P.wait("gpsimd", s_v, v_wkv)
            P.do("gpsimd", lambda e, tile=tile: e.dma_start(out=dr["dbg_wkv"][tile, :], in_=wkvt[:]), inc=s_g, k=16)

        P.wait("tensor", s_v, v_wkv)
        for bb in range(0, 8 if ("nowkv" not in phb and "nofmt" not in phb) else 0, 4):
            wait_free("tensor", id(ptw))
            for p4 in range(4):
                hh = 2 * (bb + p4)
                P.do("tensor", lambda e, hh=hh, p4=p4: e.transpose(
                    ptw[0:64, 128 * p4:128 * (p4 + 1)], wkvt[:, D * hh:D * (hh + 1)], identbf[:]))
                P.do("tensor", lambda e, hh=hh, p4=p4: e.transpose(
                    ptw[64:128, 128 * p4:128 * (p4 + 1)], wkvt[:, D * (hh + 1):D * (hh + 2)],
                    identbf[:]), inc=(s_t if p4 == 3 else None))
            P.wait("vector", s_t)
            for p4 in range(4):
                P.do("vector", lambda e, bb=bb, p4=p4: e.tensor_copy(
                    wkvfmb[:, bb + p4, :], ptw[:, 128 * p4:128 * (p4 + 1)]),
                    inc=(s_v if p4 == 3 else None))
            mark_free(id(ptw), s_v)
        v_wfm = s_v.n
        P.wait("gpsimd", s_v, v_wfm)
        for hp in range(2 if ("nowkv" not in phb and "nofmt" not in phb) else 0):
            P.do("gpsimd", lambda e, ch=ch, hp=hp: e.dma_start(
                out=dr["wkvfm_sc"][64 * hp:64 * (hp + 1), :, 128 * ch:128 * (ch + 1)],
                in_=wkvfmb[64 * hp:64 * (hp + 1), :, :]), inc=s_g, k=16)


    for pp_ in range(nch // 2):
        phaseA(2 * pp_)
        P.mark(f"A{2 * pp_}", [s_ld, s_x0, s_x1, s_tr, s_g, s_t, s_v, s_s, s_o])
        phaseA(2 * pp_ + 1)
        P.mark(f"A{2 * pp_ + 1}", [s_ld, s_x0, s_x1, s_tr, s_g, s_t, s_v, s_s, s_o])
        dosolve(2 * pp_ + 1)
        P.mark(f"S{pp_}", [s_ld, s_x0, s_x1, s_tr, s_g, s_t, s_v, s_s, s_o])
        phaseB(2 * pp_)
        P.mark(f"B{2 * pp_}", [s_ld, s_x0, s_x1, s_tr, s_g, s_t, s_v, s_s, s_o])
        phaseB(2 * pp_ + 1)
        P.mark(f"pair{pp_}", [s_ld, s_x0, s_x1, s_tr, s_g, s_t, s_v, s_s, s_o])
    P.mark("pass1", [s_ld, s_x0, s_x1, s_tr, s_g, s_t, s_v, s_s, s_o])
    P.drain("vector")
    P.do("vector", lambda e: e.tensor_copy(statr[:], statacc[:]), inc=s_v)
    v_statr = s_v.n
    P.wait("tensor", s_v, v_statr)
    wait_free("tensor", id(psm2))
    P.do("tensor", lambda e: e.matmul(psm2[0:1, 0:32], onescol_f[:], statr[:], start=True, stop=True), inc=s_t)
    P.wait("vector", s_t)
    P.do("vector", lambda e: e.tensor_scalar_mul(murow[0:1, 0:32], psm2[0:1, 0:32], 1.0 / (T * D)), inc=s_v)
    mark_free(id(psm2), s_v)
    P.drain("vector")
    P.do("vector", lambda e: e.tensor_mul(murow[0:1, 32:48], murow[0:1, 0:16], murow[0:1, 0:16]))
    P.drain("vector")
    P.do("vector", lambda e: e.tensor_sub(murow[0:1, 16:32], murow[0:1, 16:32], murow[0:1, 32:48]))
    P.drain("vector")
    P.do("vector", lambda e: e.tensor_scalar(murow[0:1, 16:32], murow[0:1, 16:32], 1.0, 1e-5, OP.mult, OP.add))
    P.drain("vector")
    P.do("vector", lambda e: e.reciprocal(murow[0:1, 16:32], murow[0:1, 16:32]), inc=s_v)
    P.wait("scalar", s_v)
    P.do("scalar", lambda e: e.activation(murow[0:1, 16:32], murow[0:1, 16:32], AF.Sqrt), inc=s_s)
    P.wait("tensor", s_s)
    wait_free("tensor", id(psmall))
    P.do("tensor", lambda e: e.matmul(psmall[:], ones_r[:], murow[0:1, 0:16], start=True, stop=True), inc=s_t)
    P.wait("vector", s_t)
    P.do("vector", lambda e: e.tensor_copy(mubc[:], psmall[:]), inc=s_v)
    P.wait("tensor", s_v)
    P.do("tensor", lambda e: e.matmul(psmall[:], ones_r[:], murow[0:1, 16:32], start=True, stop=True), inc=s_t)
    P.wait("vector", s_t)
    P.do("vector", lambda e: e.tensor_copy(rstdb[:], psmall[:]), inc=s_v)
    mark_free(id(psmall), s_v)
    P.drain("vector")
    for k in range(CS):
        for hf in range(2):
            rows = slice(64 * hf, 64 * (hf + 1))
            hidx = 2 * k + hf
            P.do("vector", lambda e, k=k, rows=rows, hidx=hidx: e.tensor_tensor(
                svcol[rows, k:k + 1], gnc[rows, 0, k:k + 1], rstdb[rows, hidx:hidx + 1], OP.mult))
    P.drain("vector")
    for k in range(CS):
        for hf in range(2):
            rows = slice(64 * hf, 64 * (hf + 1))
            hidx = 2 * k + hf
            P.do("vector", lambda e, k=k, rows=rows, hidx=hidx: e.tensor_tensor(
                shf32[rows, k:k + 1], mubc[rows, hidx:hidx + 1], svcol[rows, k:k + 1], OP.mult))
    P.drain("vector")
    P.do("vector", lambda e: e.tensor_sub(shf32[:], gnc[:, 1, :], shf32[:]))
    P.drain("vector")
    P.do("vector", lambda e: e.tensor_copy(shcol[:], shf32[:]), inc=s_v)
    v_fold = s_v.n

    P.wait("sync", s_v)
    P.do("sync", lambda e: e.dma_start(out=wstage[0:1, 0:C], in_=dr["biases"][0:1, 4 * C:5 * C]), inc=s_ld, k=16)
    P.wait("vector", s_ld)
    P.do("vector", lambda e: e.tensor_copy(brow_o[:], wstage[0:1, 0:C]), inc=s_v)
    for k in range(CS):
        P.wait("sync", s_v)
        P.do("sync", lambda e, k=k: e.dma_start(
            out=wstage[:, 0:C],
            in_=dr["WoT"][:].rearrange("(k p) c -> k p c", p=128)[k]), inc=s_ld, k=16)
        P.wait("vector", s_ld)
        P.do("vector", lambda e, k=k: e.tensor_copy(woT[:, k, :], wstage[:, 0:C]), inc=s_v)
    v_wo = s_v.n
    P.wait("tensor", s_v, v_wo)
    wait_free("tensor", id(prow))
    for nn in range(2):
        for k in range(CS):
            P.do("tensor", lambda e, k=k, nn=nn: e.matmul(
                prow[:, 512 * nn:512 * (nn + 1)], shcol[:, k:k + 1],
                woT[:, k, 512 * nn:512 * (nn + 1)], start=(k == 0), stop=False))
        P.do("tensor", lambda e, nn=nn: e.matmul(
            prow[:, 512 * nn:512 * (nn + 1)], onesb_r[:, 0:1],
            brow_o[0:1, 512 * nn:512 * (nn + 1)],
            start=False, stop=True), inc=(s_t if nn == 1 else None))
    P.wait("scalar", s_t)
    P.do("scalar", lambda e: e.copy(borow[:], prow[:]), inc=s_s)
    mark_free(id(prow), s_s)
    P.wait("vector", s_s)
    for k in range(CS):
        for hf in range(2):
            rows = slice(64 * hf, 64 * (hf + 1))
            P.do("vector", lambda e, k=k, rows=rows: e.tensor_scalar(
                woT[:, k, :][rows, :], woT[:, k, :][rows, :], svcol[rows, k:k + 1], None, OP.mult))
    P.do("vector", lambda e: e.engine_nop(), inc=s_v)
    v_woscaled = s_v.n
    if debug:
        P.wait("gpsimd", s_v, v_woscaled)
        P.do("gpsimd", lambda e: e.dma_start(out=dr["dbg_stats"][:, 0:48], in_=murow[0:1, 0:48]), inc=s_g, k=16)

    P.mark("fold", [s_ld, s_x0, s_x1, s_tr, s_g, s_t, s_v, s_s, s_o])
    nmac = T // 256
    for phase in range(2):
        hof = phase * HIDH
        P.wait("scalar", s_t)
        P.wait("scalar", s_v)
        for wname, wdst, wid, ncols in (("wgT", fgh, 0, HIDH), ("wuT", fuh, 1, HIDH)):
            for k in range(CS):
                P.wait("sync", s_s)
                if phase == 0 and wname == "wgT" and k == 0:
                    P.wait("sync", s_v, v_woscaled if False else v_fold)
                P.do("sync", lambda e, wname=wname, k=k, hof=hof: e.dma_start(
                    out=wstage[:, 0:HIDH].bitcast(I32),
                    in_=dr[wname][:].rearrange("(k p) c -> k p c", p=128)[k][:, hof:hof + HIDH]), inc=s_ld, k=16)
                P.wait("scalar", s_ld)
                P.do("scalar", lambda e, wdst=wdst, k=k: e.copy(wdst[:, k, :], wstage[:, 0:HIDH].bitcast(I32)), inc=s_s)
        for ks in range(HSH):
            P.wait("sync", s_s)
            P.do("sync", lambda e, ks=ks, hof=hof: e.dma_start(
                out=wstage[:, 0:C].bitcast(I32),
                in_=dr["wdT"][:].rearrange("(k p) c -> k p c", p=128)[hof // 128 + ks]), inc=s_ld, k=16)
            P.wait("scalar", s_ld)
            P.do("scalar", lambda e, ks=ks: e.copy(fdh[:, ks, :], wstage[:, 0:C].bitcast(I32)), inc=s_s)
        s_wffn = s_s.n
        P.wait("tensor", s_s, s_wffn)
        if phase == 1:
            P.wait("tensor", s_v, v_woscaled)

        mc_done = 0
        for mc in range(nmac):
            trange = slice(256 * mc, 256 * (mc + 1))
            P.wait("sync", s_s)
            P.wait("sync", s_v, mc_done)
            P.wait("vector", s_g)
            P.wait("vector", s_o)
            P.wait("scalar", s_g)
            for hp in range(2):
                P.do("sync", lambda e, trange=trange, hp=hp: e.dma_start(
                    out=p2xn[64 * hp:64 * (hp + 1), :, :], in_=dr["xnfm_sc"][64 * hp:64 * (hp + 1), :, trange]), inc=s_ld, k=16)
            if phase == 1:
                for hp in range(2):
                    P.do("sync", lambda e, trange=trange, hp=hp: e.dma_start(
                        out=p2wk[64 * hp:64 * (hp + 1), :, :], in_=dr["wkvfm_sc"][64 * hp:64 * (hp + 1), :, trange]), inc=s_ld, k=16)
                for tb in range(2):
                    tsl = slice(256 * mc + 128 * tb, 256 * mc + 128 * (tb + 1))
                    P.do("sync", lambda e, tb=tb, tsl=tsl: e.dma_start(
                        out=p2gs[:, tb, :], in_=dr["gsig_sc"][tsl, :]), inc=s_ld, k=16)
                    P.do("sync", lambda e, tb=tb, tsl=tsl: e.dma_start(
                        out=p2x[:, tb, :], in_=dr["x"][tsl, :]), inc=s_ld, k=16)
                    P.do("sync", lambda e, tb=tb, tsl=tsl: e.dma_start(
                        out=p2dp[:, tb, :], in_=dr["dpart_sc"][tsl, :]), inc=s_ld, k=16)
            ld_mc = s_ld.n
            P.wait("tensor", s_ld, ld_mc)
            for hs in range(HSH):
                wait_free("tensor", id(pg))
                for k in range(CS):
                    P.do("tensor", lambda e, hs=hs, k=k: e.matmul(
                        pg[:], fgh[:, k, 128 * hs:128 * (hs + 1)], p2xn[:, k, :],
                        start=(k == 0), stop=(k == CS - 1)), inc=(s_t if k == CS - 1 else None))
                P.wait("scalar", s_t)
                P.wait("scalar", s_v)
                P.do("scalar", lambda e, hs=hs: e.activation(gatsl[:], pg[:], AF.Silu, scale=S127), inc=s_s)
                mark_free(id(pg), s_s)
                wait_free("tensor", id(pjA))
                for k in range(CS):
                    P.do("tensor", lambda e, hs=hs, k=k: e.matmul(
                        pjA[:, 0:256], fuh[:, k, 128 * hs:128 * (hs + 1)], p2xn[:, k, :],
                        start=(k == 0), stop=(k == CS - 1)), inc=(s_t if k == CS - 1 else None))
                P.wait("vector", s_t)
                P.wait("vector", s_s)
                P.do("vector", lambda e, hs=hs: e.scalar_tensor_tensor(
                    hfm[:, hs, :], pjA[:, 0:256], S127, gatsl[:], OP.mult, OP.mult), inc=s_v)
                mark_free(id(pjA), s_v)
            v_hfm = s_v.n
            P.wait("tensor", s_v, v_hfm)
            for tb in range(2):
                pdn = pjB
                wait_free("tensor", id(pjB))
                for nn in range(2):
                    for ks in range(HSH):
                        P.do("tensor", lambda e, ks=ks, tb=tb, pdn=pdn, nn=nn: e.matmul(
                            pdn[:, 512 * nn:512 * (nn + 1)], hfm[:, ks, 128 * tb:128 * (tb + 1)],
                            fdh[:, ks, 512 * nn:512 * (nn + 1)],
                            start=(ks == 0), stop=(ks == HSH - 1)),
                            inc=(s_t if (ks == HSH - 1 and nn == 1) else None))
                if phase == 0:
                    P.wait("vector", s_t)
                    P.do("vector", lambda e, pdn=pdn, tb=tb: e.tensor_copy(outb[:, tb, :], pdn[:]), inc=s_v)
                    mark_free(id(pjB), s_v)
                    P.wait("gpsimd", s_v)
                    P.do("gpsimd", lambda e, mc=mc, tb=tb: e.dma_start(
                        out=dr["dpart_sc"][256 * mc + 128 * tb:256 * mc + 128 * (tb + 1), :],
                        in_=outb[:, tb, :]), inc=s_g, k=16)
                    mc_done = s_v.n
                else:
                    P.wait("vector", s_t)
                    P.do("vector", lambda e, pdn=pdn, tb=tb: e.tensor_add(outb[:, tb, :], pdn[:], p2dp[:, tb, :]), inc=s_v)
                    mark_free(id(pjB), s_v)
                    P.wait("scalar", s_v)
                    P.do("scalar", lambda e, tb=tb: e.activation(gffn[:], outb[:, tb, :], AF.Silu, scale=S127), inc=s_s)
                    wait_free("tensor", id(pjA))
                    P.wait("tensor", s_v)
                    for nn in range(2):
                        for k in range(CS):
                            P.do("tensor", lambda e, k=k, tb=tb, nn=nn: e.matmul(
                                pjA[:, 512 * nn:512 * (nn + 1)], p2wk[:, k, 128 * tb:128 * (tb + 1)],
                                woT[:, k, 512 * nn:512 * (nn + 1)], start=(k == 0), stop=False))
                        P.do("tensor", lambda e, nn=nn: e.matmul(
                            pjA[:, 512 * nn:512 * (nn + 1)], onesb_r[:],
                            borow[0:1, 512 * nn:512 * (nn + 1)], start=False, stop=True),
                            inc=(s_t if nn == 1 else None))
                    P.wait("vector", s_t)
                    P.do("vector", lambda e, tb=tb: e.tensor_tensor(p2sgb[:], pjA[:], p2gs[:, tb, :], OP.mult), inc=s_v)
                    mark_free(id(pjA), s_v)
                    P.wait("vector", s_s)
                    if debug:
                        P.wait("gpsimd", s_v)
                        P.wait("gpsimd", s_s)
                        tslx = slice(256 * mc + 128 * tb, 256 * mc + 128 * (tb + 1))
                        P.do("gpsimd", lambda e, tslx=tslx: e.dma_start(out=dr["dbg_g"][tslx, :], in_=gffn[:]), inc=s_g, k=16)
                        P.do("gpsimd", lambda e, tslx=tslx, tb=tb: e.dma_start(out=dr["dbg_sg"][tslx, :], in_=p2sgb[:]), inc=s_g, k=16)
                        P.do("gpsimd", lambda e, tslx=tslx, tb=tb: e.dma_start(out=dr["dbg_gsig"][tslx, :], in_=p2gs[:, tb, :]), inc=s_g, k=16)
                        P.wait("vector", s_g)
                    P.drain("vector")
                    P.do("vector", lambda e: e.tensor_mul(t5b[:], p2sgb[:], gffn[:]))
                    P.drain("vector")
                    P.do("vector", lambda e, tb=tb: e.scalar_tensor_tensor(
                        outb[:, tb, :], t5b[:], 1.0, p2x[:, tb, :], OP.mult, OP.add), inc=s_v)
                    P.wait("sync", s_v)
                    P.do("sync", lambda e, mc=mc, tb=tb: e.dma_start(
                        out=dr["out"][256 * mc + 128 * tb:256 * mc + 128 * (tb + 1), :],
                        in_=outb[:, tb, :]), inc=s_o, k=16)
                    mc_done = s_v.n

    P.mark("p2", [s_ld, s_x0, s_x1, s_tr, s_g, s_t, s_v, s_s, s_o])
    if upto is not None:
        if isinstance(upto, int):
            cut = upto
            snap = dict(zip(P.sems, P.snaps[cut - 1])) if cut > 0 else {}
        else:
            cut, snap = P.marks[upto]
        P.recs = P.recs[:cut]
        for s, v in snap.items():
            P.recs.append(("sync", lambda e, s=s, v=v: e.wait_ge(s.h, v) if v > 0 else e.nop()))
    else:
        P.wait("sync", s_o)
        P.wait("sync", s_g)
        P.wait("sync", s_tr)
        P.wait("sync", s_v)
        P.wait("sync", s_s)
        P.wait("sync", s_t)

    es.close()
    with nc.Block() as block:
        @block.sync
        def _(e):
            P.replay(e, "sync")
        @block.vector
        def _(e):
            P.replay(e, "vector")
        @block.scalar
        def _(e):
            P.replay(e, "scalar")
        @block.tensor
        def _(e):
            P.replay(e, "tensor")
        @block.gpsimd
        def _(e):
            P.replay(e, "gpsimd")
    global _last_marks, _last_nrec
    _last_marks = {k: v[0] for k, v in P.marks.items()}
    _last_nrec = len(P.snaps)
    return nc, dr


def host_consts():
    eye128 = np.eye(128, dtype=np.float32)
    s, t = np.meshgrid(np.arange(128), np.arange(128), indexing="ij")
    r32, c32 = np.meshgrid(np.arange(32), np.arange(32), indexing="ij")
    m32 = (r32 < c32).astype(np.float32)
    return {
        "identbf": eye128.astype(NBF),
        "identf": eye128,
        "negibf": (-np.eye(64, dtype=np.float32)).astype(NBF),
        "masksneg": (-(s < t).astype(np.float32)),
        "maskincl": (s <= t).astype(np.float32),
        "mask32r4": np.tile(m32, (1, 4)),
        "iota_r": np.arange(128, dtype=np.float32)[None, :],
        "iota1_r": np.arange(1, 129, dtype=np.float32)[None, :],
        "onesb_r": np.ones((1, 128), np.float32).astype(NBF),
        "onescol_f": np.ones((128, 1), np.float32),
    }


def make_in_map(b, x, W, consts):
    m = dict(consts)
    m["x"] = np.ascontiguousarray(x[b])
    for k, v in W.items():
        m[k] = v
    return m


def prep_weights(inp):
    cT = lambda a: np.ascontiguousarray(np.asarray(a).T)
    W = {
        "WkT": cT(inp["L0_attn_W_k"]), "WvT": cT(inp["L0_attn_W_v"]),
        "WrT": cT(inp["L0_attn_W_r"]), "WgT": cT(inp["L0_attn_W_g"]),
        "WoT": cT(inp["L0_attn_W_o"]),
        "wgT": cT(inp["L0_ffn_w_gate"]), "wuT": cT(inp["L0_ffn_w_up"]),
        "wdT": cT(inp["L0_ffn_w_down"]),
        "biases": np.concatenate([np.asarray(inp[f"L0_attn_b_{p}"]).reshape(-1)
                                  for p in ("k", "v", "r", "g", "o")])[None, :].copy(),
        "decay_eta": np.concatenate([np.asarray(inp["L0_attn_decay"]).reshape(-1),
                                     np.asarray(inp["L0_attn_eta"]).reshape(-1)])[None, :].copy(),
    }
    def cols(vecs):
        a = np.stack([np.asarray(v).reshape(C) for v in vecs])
        n = a.shape[0]
        return np.ascontiguousarray(a.reshape(n, CS, 128).transpose(2, 0, 1).reshape(128, n * CS))
    W["tmcols"] = cols([inp["L0_attn_time_mix_k"], inp["L0_attn_time_mix_v"],
                        inp["L0_attn_time_mix_r"], inp["L0_attn_time_mix_g"]])
    W["gncols"] = cols([inp["L0_attn_gn_weight"], inp["L0_attn_gn_bias"]])
    return W


def kernel(**inputs):
    x = np.asarray(inputs["L0_attn_x"])
    B, T, _ = x.shape
    nc, dr = build_nc(T, debug=False)
    W = prep_weights(inputs)
    consts = host_consts()
    in_maps = [make_in_map(b, x, W, consts) for b in range(B)]
    res = run_bass_kernel_spmd(nc, in_maps, core_ids=list(range(B)))
    return np.stack([np.asarray(r["out"]) for r in res.results], axis=0)
